# version 11
# baseline (speedup 1.0000x reference)
import math
import contextlib
import numpy as np
import concourse.bass as bass
import concourse.mybir as mybir
from concourse.bass_utils import run_bass_kernel_spmd

F32 = mybir.dt.float32
BF16 = mybir.dt.bfloat16
AF = mybir.ActivationFunctionType
ALU = mybir.AluOpType
AX = mybir.AxisListType

B, S, D = 16, 2048, 1024
NH, DH = 16, 64
FFD = 4096
NE = 8
NCORES = 8
EPC = B // NCORES
KC = D // 128
FFC = FFD // 128
NTOK = EPC * S
TB = 512
NTB = NTOK // TB
QT = S // 128
CH = 256
NCH = S // CH
WIN = 768
EPS = 1e-5
NEG = -1.0e30

BD_W = 2304
BD_SLOTS = 4

ARENA_KB = 202
CONST_KB = 4


class Sem:

    def __init__(self, h):
        self.h = h
        self.n = 0

    def inc(self, instr, k=1):
        instr.then_inc(self.h, k)
        self.n += k
        return self.n


class Arena:

    def __init__(self, t, nbytes):
        self.t = t
        self.top = 0
        self.lim = nbytes
        self.marks = []

    def alloc(self, nbytes, align=64):
        off = (self.top + align - 1) // align * align
        self.top = off + nbytes
        assert self.top <= self.lim, f"arena overflow: {self.top} > {self.lim}"
        return off

    def mark(self):
        self.marks.append(self.top)

    def release(self):
        self.top = self.marks.pop()

    def f32(self, off, cols, parts=128, p0=0):
        return self.t[p0:p0 + parts, off:off + 4 * cols].bitcast(F32)

    def bf16(self, off, cols, parts=128, p0=0):
        return self.t[p0:p0 + parts, off:off + 2 * cols].bitcast(BF16)


class KB:

    def __init__(self, nc, es, block):
        self.nc = nc
        self.es = es
        self.block = block
        self._sems = {}

    def sem(self, name):
        if name not in self._sems:
            self._sems[name] = Sem(self.es.enter_context(self.nc.semaphore(name)))
        return self._sems[name]


def _host_pe_T():
    pos = np.arange(S - 1, -1, -1, dtype=np.float32)
    inv_freq = (1.0 / 10000.0 ** (np.arange(0, D, 2, dtype=np.float32) / D)).astype(np.float32)
    sinusoid = pos[:, None].astype(np.float64) * inv_freq[None, :].astype(np.float64)
    pe = np.concatenate([np.sin(sinusoid), np.cos(sinusoid)], axis=-1).astype(np.float32)
    return np.ascontiguousarray(pe.T)


def _host_sparse_masks():
    out = np.zeros((128, 2 * (512 + WIN + 512)), dtype=np.float32)
    i = np.arange(128)[:, None]
    for qt in range(2):
        base = qt * (512 + WIN + 512)
        j = np.arange(512)[None, :]
        qpos = qt * 128 + i
        valid = np.abs(qpos - j) <= 256
        out[:, base:base + 512] = np.where(valid, 0.0, NEG)
        j = np.arange(WIN)[None, :]
        qpos = 256 + qt * 128 + i
        valid = np.abs(qpos - j) <= 256
        out[:, base + 512:base + 512 + WIN] = np.where(valid, 0.0, NEG)
        j = np.arange(512)[None, :]
        qpos = 256 + qt * 128 + i
        valid = np.abs(qpos - j) <= 256
        out[:, base + 512 + WIN:base + 1024 + WIN] = np.where(valid, 0.0, NEG)
    return out


def _to_chunks(v):
    return np.ascontiguousarray(v.reshape(-1, 128).T)


def build_program(dbg=None):
    dbg = dbg or []
    nc = bass.Bass("TRN2", target_bir_lowering=False)

    x_in = nc.dram_tensor("x", [EPC, D, S], F32, kind="ExternalInput").ap()
    w = {}
    w_shapes = {
        "attn_wq": [D, D], "attn_wk": [D, D], "attn_wv": [D, D], "attn_wo": [D, D],
        "attn_bq": [D], "attn_bk": [D], "attn_bv": [D], "attn_bo": [D],
        "ffn_Wqkv": [D, 3 * D], "ffn_Wo": [D, D], "ffn_Wr": [D, D],
        "ffn_u_bias": [NH, DH], "ffn_v_bias": [NH, DH],
        "ffn_ln1_g": [D], "ffn_ln1_b": [D], "ffn_ln2_g": [D], "ffn_ln2_b": [D],
        "ffn_ln3_g": [D], "ffn_ln3_b": [D],
        "ffn_Wff1": [D, FFD], "ffn_bff1": [FFD], "ffn_Wff2": [FFD, D], "ffn_bff2": [D],
        "ffn_Wg": [D, NE], "ffn_We1": [NE, D, FFD], "ffn_be1": [NE, FFD],
        "ffn_We2": [NE, FFD, D], "ffn_be2": [NE, D], "ffn_Wout": [D, D],
    }
    for k, shp in w_shapes.items():
        w[k] = nc.dram_tensor(k, shp, F32, kind="ExternalInput").ap()
    pe_T = nc.dram_tensor("pe_T", [D, S], F32, kind="ExternalInput").ap()
    cpack = nc.dram_tensor("cpack", [128, 160], F32, kind="ExternalInput").ap()
    spmask = nc.dram_tensor("spmask", [128, 2 * (512 + WIN + 512)], F32, kind="ExternalInput").ap()
    ident_in = nc.dram_tensor("ident", [128, 128], F32, kind="ExternalInput").ap()
    x_out = nc.dram_tensor("y", [EPC, D, S], F32, kind="ExternalOutput").ap()

    ds = {}
    def dscr(name, shape, dt=F32):
        ds[name] = nc.dram_tensor(name, shape, dt, kind="Internal")
        return ds[name]

    dscr("xres_a", [EPC, D, S])
    dscr("xres_b", [EPC, D, S])
    dscr("ybuf", [EPC, D, S])
    dscr("h1f", [EPC, D, S])
    dscr("h2f", [EPC, D, S])
    dscr("h1b", [EPC, D, S], BF16)
    dscr("h2b", [EPC, D, S], BF16)
    dscr("rbuf", [D, S], BF16)
    dscr("gall", [NE, NTOK], BF16)
    dscr("moeacc", [D, NTOK])
    dscr("bdscr", [BD_SLOTS, 128, BD_W], BF16)

    dbg_outs = {}
    for name in dbg:
        if name in ds:
            dbg_outs[name] = nc.dram_tensor("dbg_" + name, list(ds[name].shape),
                                            ds[name].dtype, kind="ExternalOutput").ap()

    es = contextlib.ExitStack()
    with es:
        arena_t = es.enter_context(nc.sbuf_tensor("arena", [128, ARENA_KB * 1024], mybir.dt.uint8))
        const_t = es.enter_context(nc.sbuf_tensor("consts", [128, CONST_KB * 1024], mybir.dt.uint8))
        A = Arena(arena_t, ARENA_KB * 1024)
        C = Arena(const_t, CONST_KB * 1024)
        PS = [es.enter_context(nc.psum_tensor(f"ps{i}", [128, 512], F32)) for i in range(6)]
        PT = [es.enter_context(nc.psum_tensor(f"pt{i}", [128, 1024], BF16)) for i in range(2)]
        block = es.enter_context(nc.Block())
        kb = KB(nc, es, block)
        kb.A, kb.C, kb.PS, kb.PT = A, C, PS, PT
        kb.w, kb.ds, kb.dbg_outs = w, ds, dbg_outs
        kb.x_in, kb.x_out = x_in, x_out
        kb.pe_T, kb.cpack, kb.spmask = pe_T, cpack, spmask
        kb.ident_in = ident_in

        for name in ["cw", "ca", "cs", "cb",
                     "pe", "dv", "ac", "po",
                     "gp",
                     "bar"]:
            kb.sem(name)

        _emit_all(kb, dbg)

    _group_engines(nc)
    return nc


def _group_engines(nc):
    order = {}

    def key(ins):
        e = str(ins.engine)
        if e not in order:
            order[e] = len(order) + 1
        return 0 if e == "EngineType.Unassigned" else order[e]

    for f in nc.m.functions:
        for b in f.blocks:
            ins = list(b.instructions)
            ins.sort(key=key)
            b.instructions = ins


_CP_LAYOUT = [("ln1_g", 8), ("ln1_b", 8), ("ln2_g", 8), ("ln2_b", 8), ("ln3_g", 8),
              ("ln3_b", 8), ("u_b", 8), ("v_b", 8), ("bq", 8), ("bk", 8), ("bv", 8),
              ("bo", 8), ("bff1", 32), ("bff2", 8), ("cv", 8)]
_CONST_VALS = [EPS, 0.125, 0.5, 1.0, -1.0, 0.0, NEG, -0.5]
CP = {}
_o = 0
for _n, _w in _CP_LAYOUT:
    CP[_n] = _o
    _o += _w
assert _o <= 160


def _cpack_host(inp):
    out = np.zeros((128, 160), np.float32)
    src = {
        "ln1_g": inp["L0_ffn_ln1_g"], "ln1_b": inp["L0_ffn_ln1_b"],
        "ln2_g": inp["L0_ffn_ln2_g"], "ln2_b": inp["L0_ffn_ln2_b"],
        "ln3_g": inp["L0_ffn_ln3_g"], "ln3_b": inp["L0_ffn_ln3_b"],
        "u_b": np.asarray(inp["L0_ffn_u_bias"]).reshape(-1),
        "v_b": np.asarray(inp["L0_ffn_v_bias"]).reshape(-1),
        "bq": inp["L0_attn_bq"], "bk": inp["L0_attn_bk"], "bv": inp["L0_attn_bv"],
        "bo": inp["L0_attn_bo"], "bff1": inp["L0_ffn_bff1"], "bff2": inp["L0_ffn_bff2"],
    }
    for name, width in _CP_LAYOUT:
        if name == "cv":
            out[:, CP[name]:CP[name] + width] = np.asarray(_CONST_VALS, np.float32)[None, :]
        else:
            a = _to_chunks(np.asarray(src[name]))
            out[:, CP[name]:CP[name] + width] = a
    return out


_XL_DBG = None


def _emit_all(kb, dbg):
    kb.xl_first = True
    kb.xl_dbg = _XL_DBG
    kb.ln_ps_free = []
    kb.r_bank_free = [[] for _ in range(4)]
    kb.dbg_list = dbg
    A = kb.A
    _p_const(kb)
    _p_r(kb)

    def alltok():
        return [(kb.sem(n), kb.sem(n).n)
                for n in ["cw", "ca", "cs", "cb", "pe", "dv", "ac", "gp"]
                if kb.sem(n).n > 0]

    def ffn_call(x_src, x_dst, tag):
        nsc = kb.nc.named_scope
        for e in range(EPC):
            A.mark()
            ctx_off = A.alloc(KC * S * 2)
            A.mark()
            q_off = A.alloc(KC * S * 2)
            k_off = A.alloc(KC * S * 2)
            vt_off = A.alloc(16 * 1024 * 2)
            A.mark()
            y_off = A.alloc(KC * S * 2)
            A.mark()
            scr = _mln_scr(kb)
            with nsc(f"MLN_{tag}{e}", notify=True):
                _p_mln(kb, x_src[e], y_off, f32_dram=kb.ds["ybuf"].ap()[e], scr=scr,
                       ready0=alltok())
            mln_done = alltok()
            A.release()
            wbuf_off = A.alloc(2 * KC * 512 * 2)
            with nsc(f"QKV_{tag}{e}", notify=True):
                _p_qkv(kb, y_off, q_off, k_off, vt_off, wbuf_off, ready=mln_done)
            A.release()
            xscr = _xl_scr(kb)
            with nsc(f"XL_{tag}{e}", notify=True):
                _p_xl(kb, q_off, k_off, vt_off, ctx_off, xscr, ready=alltok())
                kb.xl_first = False
            if tag == "B" and e == 0:
                _dump(kb, "ctx", kb.A.bf16(ctx_off, KC * S), kb.xl_done)
            A.release()
            with nsc(f"WOH_{tag}{e}", notify=True):
                _p_woh(kb, e, ctx_off, ready=alltok())
            A.release()
        with nsc(f"FFG_{tag}", notify=True):
            _p_ffg(kb, ready=alltok())
        with nsc(f"GATE_{tag}", notify=True):
            _p_gate(kb, ready=alltok())
        with nsc(f"MOE_{tag}", notify=True):
            _p_moe(kb, ready=alltok())
        with nsc(f"OUT_{tag}", notify=True):
            _p_out(kb, x_src, x_dst, 0.5, ready=alltok())

    def attn_call(x_src, x_dst):
        A.mark()
        MW = 512 + WIN + 512
        mask_off = A.alloc(2 * MW * 2)
        ca = kb.sem("ca")

        @kb.block.gpsimd
        def _(q):
            for s, n in alltok():
                q.wait_ge(s.h, n)
            q.dma_start(out=A.bf16(mask_off, 2 * MW), in_=kb.spmask).then_inc(ca.h, 16)
            ca.n += 16
        mask_tok = [(ca, ca.n)]
        nsc = kb.nc.named_scope
        for e in range(EPC):
            A.mark()
            ctx_off = A.alloc(KC * S * 2)
            A.mark()
            q_off = A.alloc(KC * S * 2)
            k_off = A.alloc(KC * S * 2)
            vt_off = A.alloc(16 * 1024 * 2)
            A.mark()
            z_off = A.alloc(KC * S * 2)
            A.mark()
            scr = _mln_scr(kb)
            with nsc(f"SLN_{e}", notify=True):
                _p_mln(kb, x_src[e], z_off, f32_dram=None, scr=scr, ready0=alltok())
            mln_done = alltok()
            A.release()
            wbuf_off = A.alloc(2 * KC * 512 * 2)
            bvr_off = A.alloc(D * 2)
            with nsc(f"SQKV_{e}", notify=True):
                _p_sqkv(kb, z_off, q_off, k_off, vt_off, wbuf_off, bvr_off, ready=mln_done)
            A.release()
            spscr = _sp_scr(kb)
            with nsc(f"SATT_{e}", notify=True):
                _p_satt(kb, q_off, k_off, vt_off, ctx_off, mask_off, spscr,
                        ready=alltok() + mask_tok)
            A.release()
            with nsc(f"SWO_{e}", notify=True):
                _p_swo(kb, e, ctx_off, x_src, x_dst, ready=alltok())
            A.release()
        A.release()

    xa = kb.ds["xres_a"].ap()
    xb = kb.ds["xres_b"].ap()
    ffn_call(kb.x_in, xa, "B")
    attn_call(xa, xb)
    ffn_call(xb, kb.x_out, "F")
    _p_finish(kb, dbg)


def _p_const(kb):
    C, block = kb.C, kb.block
    cw, dv = kb.sem("cw"), kb.sem("dv")

    kb.cp_off = C.alloc(160 * 4)
    kb.ident_off = C.alloc(128 * 2)
    kb.ones_col_off = C.alloc(2 * 2)
    kb.ones_row_bf_off = C.alloc(128 * 2)
    kb.ones_row_f_off = C.alloc(128 * 4)
    kb.identf_off = C.alloc(128 * 4)
    kb.neg_off = C.alloc(256 * 4)
    kb.neg_bf_off = C.alloc(256 * 2)

    @block.sync
    def _(q):
        q.dma_start(out=C.f32(kb.cp_off, 160), in_=kb.cpack).then_inc(cw.h, 16); cw.n += 16
        q.dma_start(out=C.f32(kb.identf_off, 128), in_=kb.ident_in).then_inc(cw.h, 16); cw.n += 16

    @block.gpsimd
    def _(q):
        q.dma_start(out=C.bf16(kb.ident_off, 128), in_=kb.ident_in).then_inc(cw.h, 16); cw.n += 16

    @block.vector
    def _(v):
        v.memset(C.bf16(kb.ones_col_off, 2), 1.0)
        v.memset(C.bf16(kb.ones_row_bf_off, 128, parts=1), 1.0)
        v.memset(C.f32(kb.ones_row_f_off, 128, parts=1), 1.0)
        v.memset(C.f32(kb.neg_off, 256), NEG)
        v.memset(C.bf16(kb.neg_bf_off, 256), NEG).then_inc(dv.h, 1); dv.n += 1

    cs = kb.sem("cs")

    @block.sync
    def _(q):
        q.wait_ge(dv.h, dv.n)
        for sl in range(BD_SLOTS):
            q.dma_start(out=kb.ds["bdscr"].ap()[sl, :, S:BD_W],
                        in_=C.bf16(kb.neg_bf_off, BD_W - S)).then_inc(cs.h, 16)
            cs.n += 16
    kb.bdtail_tok = (cs, cs.n)

    for i, val in enumerate(_CONST_VALS):
        ap = C.f32(kb.cp_off + 4 * (CP["cv"] + i), 1)
        kb.nc.const_aps.register(F32, float(val), ap)
    kb.const_tok = [(cw, cw.n), (dv, dv.n)]


LNW = 512


def ln_scr_alloc(A):
    return {
        "xbf": A.alloc(KC * LNW * 2),
        "rows": A.alloc(3 * LNW * 4),
        "tmp": A.alloc(2 * LNW * 4),
    }


def _ln_block(kb, x_blk, scr, y_put, gamma=None, beta=None, ready=(), state=None):
    A, C, block, PS = kb.A, kb.C, kb.block, kb.PS
    nc = kb.nc
    W = LNW
    pe, dv, ac = kb.sem("pe"), kb.sem("dv"), kb.sem("ac")
    ps_s, ps_q = PS[4], PS[5]
    if state is None:
        state = {}
    ps_free = state.get("ps_free", [])
    tmp_busy = state.get("tmp_busy", {0: [], 1: []})

    xbf = A.bf16(scr["xbf"], KC * W)
    xsq = xbf
    rows = scr["rows"]
    r_s = A.f32(rows, W, parts=1)
    r_q = A.f32(rows + 4 * W, W, parts=1)
    r_t = A.f32(rows + 8 * W, W, parts=1)
    tmp_ap = lambda par: A.f32(scr["tmp"] + par * 4 * W, W)
    ones_col = C.bf16(kb.ones_col_off, 2)
    ones_row_f = C.f32(kb.ones_row_f_off, 128, parts=1)

    def wt(eng, toks):
        for s, n in toks:
            eng.wait_ge(s.h, n)

    xbf_free = state.get("xbf_free", [])

    @block.vector
    def _(v):
        wt(v, ready)
        wt(v, xbf_free)
        for c in range(KC):
            ins = v.tensor_copy(out=xbf[:, c * W:(c + 1) * W], in_=x_blk[:, c * W:(c + 1) * W])
        dv.inc(ins)
    dv_cast = dv.n

    @block.tensor
    def _(t):
        t.wait_ge(dv.h, dv_cast)
        wt(t, ps_free)
        for c in range(KC):
            ins = t.matmul(ps_s[0:1, 0:W], ones_col[:, 0:1], xbf[:, c * W:(c + 1) * W],
                           start=(c == 0), stop=(c == KC - 1))
        pe.inc(ins)
    pe_sums = pe.n

    @block.vector
    def _(v):
        wt(v, ready)
        v.wait_ge(pe.h, pe_sums)
        for c in range(KC):
            ins = v.tensor_mul(xsq[:, c * W:(c + 1) * W], x_blk[:, c * W:(c + 1) * W],
                               x_blk[:, c * W:(c + 1) * W])
        dv.inc(ins)
    dv_sq = dv.n

    @block.tensor
    def _(t):
        t.wait_ge(dv.h, dv_sq)
        for c in range(KC):
            ins = t.matmul(ps_q[0:1, 0:W], ones_col[:, 0:1], xsq[:, c * W:(c + 1) * W],
                           start=(c == 0), stop=(c == KC - 1))
        pe.inc(ins)
    pe_stats = pe.n
    state["xbf_free"] = [(pe, pe.n)]

    @block.vector
    def _(v):
        v.wait_ge(pe.h, pe_stats)
        v.tensor_copy(out=r_s, in_=ps_s[0:1, 0:W])
        v.tensor_copy(out=r_q, in_=ps_q[0:1, 0:W])
        v.tensor_scalar_mul(r_s, r_s, -1.0 / D)
        v.tensor_scalar_mul(r_q, r_q, 1.0 / D)
        v.tensor_mul(r_t, r_s, r_s)
        dv.inc(v.tensor_sub(r_q, r_q, r_t))
    dv_var = dv.n

    @block.scalar
    def _(s):
        s.wait_ge(dv.h, dv_var)
        s.activation(out=r_q, in_=r_q, func=AF.Ln, bias=EPS, scale=1.0)
        ac.inc(s.activation(out=r_q, in_=r_q, func=AF.Exp, bias=0.0, scale=-0.5))
    ac_rstd = ac.n

    @block.tensor
    def _(t):
        t.wait_ge(ac.h, ac_rstd)
        t.matmul(ps_s[:, 0:W], ones_row_f, r_s, start=True, stop=True)
        pe.inc(t.matmul(ps_q[:, 0:W], ones_row_f, r_q, start=True, stop=True))
    pe_bc = pe.n

    for c in range(KC):
        par = c % 2
        @block.vector
        def _(v, c=c, par=par):
            if c == 0:
                v.wait_ge(pe.h, pe_bc)
            wt(v, tmp_busy.get(par, []))
            tmp = tmp_ap(par)
            v.tensor_add(tmp, x_blk[:, c * W:(c + 1) * W], ps_s[:, 0:W])
            ins = v.tensor_mul(tmp, tmp, ps_q[:, 0:W])
            if gamma is not None:
                g_ap = C.f32(kb.cp_off + 4 * (gamma + c), 1)
                b_ap = C.f32(kb.cp_off + 4 * (beta + c), 1)
                ins = v.tensor_scalar(tmp, tmp, g_ap, b_ap, ALU.mult, ALU.add)
            dv.inc(ins)
        toks = y_put(c, tmp_ap(par), (dv, dv.n))
        tmp_busy[par] = list(toks)
    state["ps_free"] = [(dv, dv.n)]
    state["tmp_busy"] = tmp_busy
    return state


def _dump(kb, name, ap, waits):
    if name not in kb.dbg_list:
        return None
    cs = kb.sem("cs")
    shp = list(ap.shape)
    out = kb.nc.dram_tensor("dbg_" + name, shp, ap.dtype, kind="ExternalOutput").ap()

    @kb.block.sync
    def _(q):
        for s, n in waits:
            q.wait_ge(s.h, n)
        q.dma_start(out=out, in_=ap).then_inc(cs.h, 16)
        cs.n += 16
    return (cs, cs.n)


def _p_qkv(kb, y_off, q_off, k_off, vt_off, wbuf_off, ready=()):
    A, C, block, nc, PS = kb.A, kb.C, kb.block, kb.nc, kb.PS
    cw, pe, ac, dv = kb.sem("cw"), kb.sem("pe"), kb.sem("ac"), kb.sem("dv")
    Wqkv = kb.w["ffn_Wqkv"]

    @block.tensor
    def _(t):
        for s, n in ready:
            t.wait_ge(s.h, n)

    @block.gpsimd
    def _(q):
        for s, n in ready:
            q.wait_ge(s.h, n)

    wb = lambda par: A.bf16(wbuf_off + par * KC * 512 * 2, KC * 512)
    y_ap = lambda ksl: A.bf16(y_off + 2 * ksl * S, S)
    wload = {}
    wfree = [[], []]
    bank_free = [[] for _ in range(4)]

    def emit_load(cb):
        par = cb % 2

        @block.gpsimd
        def _(q):
            for s, n in wfree[par]:
                q.wait_ge(s.h, n)
            src = Wqkv[:, cb * 512:(cb + 1) * 512].rearrange("(k p) c -> p k c", p=128)
            dst = wb(par).rearrange("p (k c) -> p k c", k=KC)
            q.dma_start(out=dst, in_=src).then_inc(cw.h, 16)
            cw.n += 16
        wload[cb] = (cw, cw.n)

    emit_load(0)
    emit_load(1)
    nbank = 0
    for cb in range(6):
        par = cb % 2
        if cb < 4:
            dst_off = q_off if cb < 2 else k_off
            fbase = (cb % 2) * 4
            for m in range(4):
                f = fbase + m
                for n in range(4):
                    bank = nbank % 4
                    nbank += 1

                    @block.tensor
                    def _(t, cb=cb, par=par, m=m, n=n, bank=bank):
                        t.wait_ge(cw.h, wload[cb][1])
                        for s, nn in bank_free[bank]:
                            t.wait_ge(s.h, nn)
                        for k in range(KC):
                            ins = t.matmul(PS[bank][:, 0:512],
                                           wb(par)[:, k * 512 + m * 128:k * 512 + (m + 1) * 128],
                                           y_ap(k)[:, n * 512:(n + 1) * 512],
                                           start=(k == 0), stop=(k == KC - 1))
                        pe.inc(ins)
                    pe_t = pe.n

                    @block.vector
                    def _(v, f=f, n=n, bank=bank, pe_t=pe_t, dst_off=dst_off):
                        v.wait_ge(pe.h, pe_t)
                        dst = kb.A.bf16(dst_off + 2 * (f * S + n * 512), 512)
                        dv.inc(v.tensor_copy(out=dst, in_=PS[bank][:, 0:512]))
                    bank_free[bank] = [(dv, dv.n)]
            wfree[par] = [(pe, pe.n)]
            if cb + 2 < 6:
                emit_load(cb + 2)
        else:
            vb = cb - 4
            for tt in range(16):
                bank = nbank % 4
                nbank += 1

                @block.tensor
                def _(t, cb=cb, par=par, tt=tt, bank=bank):
                    t.wait_ge(cw.h, wload[cb][1])
                    for s, nn in bank_free[bank]:
                        t.wait_ge(s.h, nn)
                    for k in range(KC):
                        ins = t.matmul(PS[bank][:, 0:512],
                                       y_ap(k)[:, tt * 128:(tt + 1) * 128],
                                       wb(par)[:, k * 512:(k + 1) * 512],
                                       start=(k == 0), stop=(k == KC - 1))
                    pe.inc(ins)
                pe_t = pe.n

                @block.vector
                def _(v, tt=tt, vb=vb, bank=bank, pe_t=pe_t):
                    v.wait_ge(pe.h, pe_t)
                    dst = kb.A.bf16(vt_off + 2 * (tt * 1024 + vb * 512), 512)
                    dv.inc(v.tensor_copy(out=dst, in_=PS[bank][:, 0:512]))
                bank_free[bank] = [(dv, dv.n)]
            wfree[par] = [(pe, pe.n)]
            if cb + 2 < 6:
                emit_load(cb + 2)

    kb.qkv_done = [(ac, ac.n)]


def _p_r(kb):
    A, C, block, nc, PS = kb.A, kb.C, kb.block, kb.nc, kb.PS
    cw, ca, cs, pe, ac, dv = (kb.sem(n) for n in ["cw", "ca", "cs", "pe", "ac", "dv"])
    A.mark()
    peT_off = A.alloc(KC * S * 2)
    wbuf_off = A.alloc(2 * KC * 512 * 2)
    rst_off = A.alloc(2 * 512 * 2)
    wb = lambda par: A.bf16(wbuf_off + par * KC * 512 * 2, KC * 512)
    peT = lambda k: A.bf16(peT_off + 2 * k * S, S)

    @block.gpsimd
    def _(q):
        for k in range(KC):
            q.dma_start(out=peT(k), in_=kb.pe_T[k * 128:(k + 1) * 128, :]).then_inc(ca.h, 16)
            ca.n += 16
        for cb in range(2):
            src = kb.w["ffn_Wr"][:, cb * 512:(cb + 1) * 512].rearrange("(k p) c -> p k c", p=128)
            q.dma_start(out=wb(cb).rearrange("p (k c) -> p k c", k=KC), in_=src).then_inc(cw.h, 16)
            cw.n += 16
    loads = (ca.n, cw.n)
    st_free = [[], []]

    for cb in range(2):
        for m in range(4):
            f = cb * 4 + m
            for n in range(4):
                bank = (f * 4 + n) % 4

                @block.tensor
                def _(t, cb=cb, m=m, n=n, bank=bank):
                    t.wait_ge(ca.h, loads[0])
                    t.wait_ge(cw.h, loads[1])
                    for s, nn in kb.r_bank_free[bank]:
                        t.wait_ge(s.h, nn)
                    for k in range(KC):
                        ins = t.matmul(PS[bank][:, 0:512],
                                       wb(cb)[:, k * 512 + m * 128:k * 512 + (m + 1) * 128],
                                       peT(k)[:, n * 512:(n + 1) * 512],
                                       start=(k == 0), stop=(k == KC - 1))
                    pe.inc(ins)
                pe_t = pe.n
                par = n % 2

                @block.vector
                def _(v, bank=bank, par=par, pe_t=pe_t):
                    v.wait_ge(pe.h, pe_t)
                    for s, nn in st_free[par]:
                        v.wait_ge(s.h, nn)
                    dv.inc(v.tensor_copy(out=A.bf16(rst_off + par * 512 * 2, 512),
                                         in_=PS[bank][:, 0:512]))
                kb.r_bank_free[bank] = [(dv, dv.n)]
                ac_t = dv.n

                @block.sync
                def _(q, f=f, n=n, par=par, ac_t=ac_t):
                    q.wait_ge(dv.h, ac_t)
                    q.dma_start(out=kb.ds["rbuf"].ap()[f * 128:(f + 1) * 128, n * 512:(n + 1) * 512],
                                in_=A.bf16(rst_off + par * 512 * 2, 512)).then_inc(cs.h, 16)
                    cs.n += 16
                st_free[par] = [(cs, cs.n)]
    A.release()
    kb.r_done = [(cs, cs.n)]


def _xl_scr(kb):
    A = kb.A
    return {
        "u2h": A.alloc(S * 2), "v2h": A.alloc(S * 2), "rh": A.alloc(S * 2),
        "sc": A.alloc(3 * S * 4), "bdst": A.alloc(2 * S * 2),
        "p": A.alloc(3 * S * 2), "pT": A.alloc(4 * 1024 * 2),
        "cst": A.alloc(2 * S * 2), "nm": A.alloc(32 * 4),
    }


def _p_xl(kb, q_off, k_off, vt_off, ctx_off, scr, ready):
    A, C, block, nc, PS, PT = kb.A, kb.C, kb.block, kb.nc, kb.PS, kb.PT
    pe, dv, ac = kb.sem("pe"), kb.sem("dv"), kb.sem("ac")
    ca, cs, cb, gp = kb.sem("ca"), kb.sem("cs"), kb.sem("cb"), kb.sem("gp")
    bdscr = kb.ds["bdscr"]
    rbuf = kb.ds["rbuf"]
    ident = C.bf16(kb.ident_off, 128)

    u2h = lambda bb: A.bf16(scr["u2h"], S, parts=64, p0=bb)
    v2h = lambda bb: A.bf16(scr["v2h"], S, parts=64, p0=bb)
    rh = lambda bb: A.bf16(scr["rh"], S, parts=64, p0=bb)
    sc_ap = lambda sl, o, w: A.f32(scr["sc"] + sl * S * 4 + o * 4, w)
    bdst_ap = lambda par, o, w: A.bf16(scr["bdst"] + par * S * 2 + o * 2, w)
    p_ap = lambda par, o, w: A.bf16(scr["p"] + par * S * 2 + o * 2, w)
    pT_ap = lambda sl, w: A.bf16(scr["pT"] + sl * 1024 * 2, w)
    cst = lambda hp: A.bf16(scr["cst"] + hp * S * 2, S, parts=64)
    nm_ap = lambda i, w=1: A.f32(scr["nm"] + i * 4, w)

    NU = NH * QT
    first = {"t": True, "v": True, "s": True, "g": True, "sq": True}

    def wt(eng, toks, key=None):
        if key is not None and first[key]:
            first[key] = False
            toks = list(toks) + list(ready)
        for s, n in toks:
            eng.wait_ge(s.h, n)

    bd_bank = [[], []]
    ac_bank = [[], []]
    bdst_st = [[], []]
    slot_st = [[] for _ in range(BD_SLOTS)]
    p_st = [[], [], []]
    sc_st = [[], [], []]
    PTb_st = [[], []]
    pT_st = [[], [], [], []]
    ctx_col = [[], []]
    head_done_pe = {}
    cst_tok = {}
    prep_tok = {}
    U = [dict() for _ in range(NU)]
    gidx = 0

    def jblocks(kt):
        out, o = [], S - kt
        while o < S:
            w = min(512, S - o)
            out.append((o, w))
            o += w
        return out

    def front(u):
        h, t = divmod(u, QT)
        bb = (h % 2) * 64
        kt = 128 * (t + 1)
        st = U[u]
        st.update(h=h, t=t, bb=bb, kt=kt, par=u % 3, scs=u % 3, par2=u % 2,
                  slot=u % BD_SLOTS, hc=h // 2)
        if t == 0:
            gate = list(head_done_pe.get(h - 2, []))

            @block.vector
            def _(v, h=h, bb=bb, gate=gate):
                wt(v, gate, "v")
                hc = h // 2
                qsl = A.bf16(q_off + 2 * (hc * S), S, parts=64, p0=bb)
                ub = C.f32(kb.cp_off + 4 * (CP["u_b"] + hc), 1, parts=64, p0=bb)
                vb = C.f32(kb.cp_off + 4 * (CP["v_b"] + hc), 1, parts=64, p0=bb)
                v.tensor_scalar(u2h(bb), qsl, ub, None, ALU.add)
                dv.inc(v.tensor_scalar(v2h(bb), qsl, vb, None, ALU.add))
            uv_tok = (dv, dv.n)

            @block.scalar
            def _(s, h=h, bb=bb, gate=gate):
                wt(s, gate, "s")
                s.dma_start(out=rh(bb), in_=rbuf.ap()[h * DH:(h + 1) * DH, :]).then_inc(ca.h, 16)
                ca.n += 16
            prep_tok[h] = [uv_tok, (ca, ca.n)]

        jb = jblocks(kt)
        evac_toks = []
        for bi, (jo, wb_) in enumerate(jb):
            bank = bi % 2

            @block.tensor
            def _(tq, u=u, t=t, bb=bb, jo=jo, wb_=wb_, bank=bank, bi=bi, h=h):
                toks = list(bd_bank[bank])
                if bi == 0:
                    toks += prep_tok[h]
                wt(tq, toks, "t")
                pe.inc(tq.matmul(PS[bank][:, 0:wb_],
                                 v2h(bb)[:, t * 128:(t + 1) * 128],
                                 rh(bb)[:, jo:jo + wb_], start=True, stop=True))
            pe_t = pe.n

            @block.vector
            def _(v, jo=jo, wb_=wb_, bank=bank, pe_t=pe_t, kt=kt, par2=u % 2, bi=bi):
                toks = [(pe, pe_t)]
                if bi == 0:
                    toks += bdst_st[par2]
                wt(v, toks, "v")
                dv.inc(v.tensor_copy(out=bdst_ap(par2, jo - (S - kt), wb_),
                                     in_=PS[bank][:, 0:wb_]))
            bd_bank[bank] = [(dv, dv.n)]
        st["bdev"] = (dv, dv.n)

        @block.sync
        def _(q, u=u, kt=kt, par2=u % 2, slot=u % BD_SLOTS):
            wt(q, [U[u]["bdev"]] + slot_st[slot], "sq")
            q.dma_start(out=bdscr.ap()[slot, :, S - kt:S],
                        in_=bdst_ap(par2, 0, kt)).then_inc(cs.h, 16)
            cs.n += 16
        bdst_st[u % 2] = [(cs, cs.n)]
        st["bdwr"] = (cs, cs.n)

    def mid(u):
        st = U[u]
        kt, par, slot = st["kt"], st["par"], st["slot"]

        @block.gpsimd
        def _(g, u=u, kt=kt, par=par, slot=slot):
            toks = [U[u]["bdwr"]] + p_st[par]
            if kb.xl_first and u < BD_SLOTS:
                toks = toks + [kb.bdtail_tok]
            wt(g, toks, "g")
            src = bass.AP(bdscr, slot * 128 * BD_W + (S - 1 - 128 * st["t"]),
                          [[BD_W - 1, 128], [1, kt]])
            g.dma_start(out=p_ap(par, 0, kt), in_=src).then_inc(cb.h, 16)
            cb.n += 16
        st["bdrd"] = (cb, cb.n)

    def back1(u):
        st = U[u]
        h, t, bb, kt, par, scs = st["h"], st["t"], st["bb"], st["kt"], st["par"], st["scs"]
        hc = st["hc"]
        nblk = (kt + 511) // 512
        ttr_done = []
        for bi in range(nblk):
            ko = bi * 512
            wb_ = min(512, kt - ko)
            bank = bi % 2

            @block.tensor
            def _(tq, t=t, ko=ko, wb_=wb_, bank=bank, bb=bb, hc=hc):
                wt(tq, ac_bank[bank], "t")
                pe.inc(tq.matmul(PS[2 + bank][:, 0:wb_],
                                 u2h(bb)[:, t * 128:(t + 1) * 128],
                                 A.bf16(k_off, KC * S, parts=64, p0=bb)[:, hc * S + ko:hc * S + ko + wb_],
                                 start=True, stop=True))
            pe_t = pe.n

            @block.vector
            def _(v, u=u, bi=bi, ko=ko, wb_=wb_, bank=bank, pe_t=pe_t, par=par, scs=scs):
                toks = [(pe, pe_t)]
                if bi == 0:
                    toks = toks + [U[u]["bdrd"]] + sc_st[scs]
                wt(v, toks, "v")
                dv.inc(v.tensor_tensor(out=sc_ap(scs, ko, wb_), in0=PS[2 + bank][:, 0:wb_],
                                       in1=p_ap(par, ko, wb_), op=ALU.add))
            ac_bank[bank] = [(dv, dv.n)]
            ttr_done.append((dv, dv.n))
        st["ttr"] = (dv, dv.n)
        if t == QT - 1:
            head_done_pe[h] = [(pe, pe.n)]

        @block.vector
        def _(v, par=par, scs=scs, kt=kt):
            dv.inc(v.reduce_max(nm_ap(18 + par), sc_ap(scs, 0, kt), axis=AX.X))
            v.wait_ge(dv.h, dv.n)
            dv.inc(v.tensor_scalar_mul(nm_ap(21 + par), nm_ap(18 + par), -0.125))
        st["negm"] = (dv, dv.n)

        @block.scalar
        def _(sq, u=u, par=par, scs=scs, kt=kt):
            wt(sq, [U[u]["negm"]], "s")
            ac.inc(sq.activation(out=p_ap(par, 0, kt), in_=sc_ap(scs, 0, kt), func=AF.Exp,
                                 bias=nm_ap(21 + par), scale=0.125,
                                 accum_out=nm_ap(24 + par)))
        sc_st[scs] = [(ac, ac.n)]
        st["exp"] = (ac, ac.n)

    def back2(u):
        nonlocal gidx
        st = U[u]
        h, t, bb, kt, par = st["h"], st["t"], st["bb"], st["kt"], st["par"]
        hc, hp = st["hc"], h % 2

        @block.vector
        def _(v, u=u, par=par):
            v.wait_ge(ac.h, U[u]["exp"][1])
            dv.inc(v.reciprocal(nm_ap(27 + par), nm_ap(24 + par)))
        rec_tok = (dv, dv.n)

        @block.gpsimd
        def _(g, par=par, kt=kt, rec_tok=rec_tok):
            wt(g, [rec_tok])
            gp.inc(g.tensor_scalar(p_ap(par, 0, kt), p_ap(par, 0, kt), nm_ap(27 + par),
                                   None, ALU.mult))
        psc_tok = (gp, gp.n)

        ngrp = (t + 1 + 7) // 8
        grp_sl = []
        for g_ in range(ngrp):
            n_in_g = min(8, t + 1 - g_ * 8)
            bankb = gidx % 2
            sl = gidx % 4
            gidx += 1

            @block.tensor
            def _(tq, g_=g_, n_in_g=n_in_g, bankb=bankb, par=par, psc_tok=psc_tok):
                wt(tq, [psc_tok] + PTb_st[bankb], "t")
                for i in range(n_in_g):
                    kbk = g_ * 8 + i
                    ins = tq.transpose(PT[bankb][:, i * 128:(i + 1) * 128],
                                       p_ap(par, kbk * 128, 128), ident)
                pe.inc(ins)
            pe_tr = pe.n

            @block.vector
            def _(v, sl=sl, n_in_g=n_in_g, bankb=bankb, pe_tr=pe_tr):
                wt(v, [(pe, pe_tr)] + pT_st[sl])
                dv.inc(v.tensor_copy(out=pT_ap(sl, n_in_g * 128),
                                     in_=PT[bankb][:, 0:n_in_g * 128]))
            PTb_st[bankb] = [(dv, dv.n)]
            grp_sl.append((sl, (dv, dv.n)))
        p_st[par] = [(pe, pe.n)]

        chalf = u % 2
        c0 = chalf * 128

        @block.tensor
        def _(tq, t=t, h=h, grp_sl=grp_sl, c0=c0, chalf=chalf):
            for sl, tok in grp_sl:
                tq.wait_ge(tok[0].h, tok[1])
            wt(tq, ctx_col[chalf])
            for kbk in range(t + 1):
                sl = grp_sl[kbk // 8][0]
                ins = tq.matmul(PS[4][0:64, c0:c0 + 128],
                                A.bf16(vt_off + 2 * (kbk * 1024 + h * DH), DH),
                                pT_ap(sl, 1024)[:, (kbk % 8) * 128:(kbk % 8 + 1) * 128],
                                start=(kbk == 0), stop=(kbk == t))
            pe.inc(ins)
        pe_ctx = pe.n
        for kbk in range(t + 1):
            sl = grp_sl[kbk // 8][0]
            pT_st[sl] = [(pe, pe.n)]
        p_st[par] = [(pe, pe.n)]

        @block.vector
        def _(v, t=t, hp=hp, pe_ctx=pe_ctx, c0=c0, h=h):
            toks = [(pe, pe_ctx)]
            if t == 0:
                toks = toks + list(cst_tok.get(h - 2, []))
            wt(v, toks)
            dv.inc(v.tensor_copy(out=cst(hp)[:, t * 128:(t + 1) * 128],
                                 in_=PS[4][0:64, c0:c0 + 128]))
        ctx_col[chalf] = [(dv, dv.n)]

        if t == QT - 1:
            dv_last = dv.n

            @block.scalar
            def _(s, h=h, bb=bb, hc=hc, hp=hp, dv_last=dv_last):
                s.wait_ge(dv.h, dv_last)
                dst = A.bf16(ctx_off + 2 * (hc * S), S, parts=64, p0=bb)
                s.dma_start(out=dst, in_=cst(hp)[:, :]).then_inc(cs.h, 16)
                cs.n += 16
            cst_tok[h] = [(cs, cs.n)]

    for it in range(NU + 3):
        if 3 <= it < NU + 3:
            back2(it - 3)
        if 2 <= it < NU + 2:
            back1(it - 2)
        if 1 <= it < NU + 1:
            mid(it - 1)
        if it < NU:
            front(it)
        if 1 <= it < NU + 1:
            uu = it - 1
            slot_st[uu % BD_SLOTS] = [U[uu]["bdrd"]]

    kb.xl_done = [(cs, cs.n), (dv, dv.n)]


def _p_woh(kb, elem, ctx_off, ready):
    A, C, block, nc, PS = kb.A, kb.C, kb.block, kb.nc, kb.PS
    cw, ca, cs, pe, dv, ac = (kb.sem(n) for n in ["cw", "ca", "cs", "pe", "dv", "ac"])
    A.mark()
    wo_off = A.alloc(KC * D * 2)
    y32_off = A.alloc(KC * 512 * 4)
    s2_off = A.alloc(KC * 512 * 4)
    h1c_off = A.alloc(2 * 512 * 2)
    scr = ln_scr_alloc(A)
    wo = A.bf16(wo_off, KC * D)
    y32 = A.f32(y32_off, KC * 512)
    s2 = A.f32(s2_off, KC * 512)

    @block.gpsimd
    def _(q):
        for s, n in ready:
            q.wait_ge(s.h, n)
        src = kb.w["ffn_Wo"].rearrange("(k p) c -> p k c", p=128)
        q.dma_start(out=wo.rearrange("p (k c) -> p k c", k=KC), in_=src).then_inc(cw.h, 16)
        cw.n += 16
    wo_tok = (cw, cw.n)

    state = {"ps_free": list(ready)}
    bank_free = [[] for _ in range(4)]
    y32_free = []
    s2_free = []
    h1c_busy = [[], []]
    for tb in range(S // 512):
        @block.sync
        def _(q, tb=tb, elem=elem):
            for s, n in y32_free:
                q.wait_ge(s.h, n)
            src = kb.ds["ybuf"].ap()[elem][:, tb * 512:(tb + 1) * 512].rearrange(
                "(c p) w -> p c w", p=128)
            q.dma_start(out=y32.rearrange("p (c w) -> p c w", c=KC), in_=src).then_inc(ca.h, 16)
            ca.n += 16
        y_tok = (ca, ca.n)

        dc_toks = []
        for dc in range(KC):
            bank = dc % 4

            @block.tensor
            def _(t, tb=tb, dc=dc, bank=bank):
                if tb == 0 and dc == 0:
                    t.wait_ge(cw.h, wo_tok[1])
                    for s, n in ready:
                        t.wait_ge(s.h, n)
                for s, n in bank_free[bank]:
                    t.wait_ge(s.h, n)
                for k in range(KC):
                    ins = t.matmul(PS[bank][:, 0:512],
                                   wo[:, k * D + dc * 128:k * D + (dc + 1) * 128],
                                   A.bf16(ctx_off + 2 * (k * S + tb * 512), 512),
                                   start=(k == 0), stop=(k == KC - 1))
                pe.inc(ins)
            pe_t = pe.n

            @block.vector
            def _(v, dc=dc, bank=bank, pe_t=pe_t, y_tok=y_tok):
                v.wait_ge(pe.h, pe_t)
                v.wait_ge(ca.h, y_tok[1])
                for s, n in s2_free:
                    v.wait_ge(s.h, n)
                dv.inc(v.tensor_add(s2[:, dc * 512:(dc + 1) * 512],
                                    PS[bank][:, 0:512], y32[:, dc * 512:(dc + 1) * 512]))
            bank_free[bank] = [(dv, dv.n)]
            dc_toks.append((dv, dv.n))
        y32_free = [(dv, dv.n)]
        s2_free = []

        def y_put(c, tmp, tok, tb=tb):
            h1c = A.bf16(h1c_off + (c % 2) * 512 * 2, 512)
            toks = []

            @block.vector
            def _(v, c=c):
                for s, n in h1c_busy[c % 2]:
                    v.wait_ge(s.h, n)
                dv.inc(v.tensor_copy(out=h1c, in_=tmp))
            bf_tok = (dv, dv.n)

            @block.sync
            def _(q, c=c, bf_tok=bf_tok, tok=tok):
                q.wait_ge(tok[0].h, tok[1])
                q.dma_start(out=kb.ds["h1f"].ap()[elem, c * 128:(c + 1) * 128,
                                                  tb * 512:(tb + 1) * 512],
                            in_=tmp).then_inc(cs.h, 16)
                cs.n += 16
                toks.append((cs, cs.n))
                q.wait_ge(bf_tok[0].h, bf_tok[1])
                q.dma_start(out=kb.ds["h1b"].ap()[elem, c * 128:(c + 1) * 128,
                                                  tb * 512:(tb + 1) * 512],
                            in_=h1c).then_inc(cs.h, 16)
                cs.n += 16
            h1c_busy[c % 2] = [(cs, cs.n)]
            return toks

        state = _ln_block(kb, s2, scr, y_put, gamma=CP["ln1_g"], beta=CP["ln1_b"],
                          ready=dc_toks, state=state)
        s2_free = [(dv, kb.sem("dv").n), (ac, kb.sem("ac").n), (pe, kb.sem("pe").n)]
    A.release()
    kb.ln_ps_free = state["ps_free"]
    kb.woh_done = [(cs, cs.n)]


def _p_ffg(kb, ready):
    A, C, block, nc, PS = kb.A, kb.C, kb.block, kb.nc, kb.PS
    cw, ca, cs, pe, dv, ac = (kb.sem(n) for n in ["cw", "ca", "cs", "pe", "dv", "ac"])
    A.mark()
    w1_off = A.alloc(KC * FFD * 2)
    w2_off = A.alloc(FFC * D * 2)
    hid_off = A.alloc(FFC * 512 * 2)
    h1b_off = A.alloc(KC * 512 * 2)
    h132_off = A.alloc(2 * 512 * 4)
    s2_off = A.alloc(KC * 512 * 4)
    h2c_off = A.alloc(2 * 512 * 2)
    scr = {"xbf": hid_off, "rows": A.alloc(3 * LNW * 4), "tmp": A.alloc(2 * LNW * 4)}
    w1 = A.bf16(w1_off, KC * FFD)
    w2 = A.bf16(w2_off, FFC * D)
    hid = A.bf16(hid_off, FFC * 512)
    h1btb = A.bf16(h1b_off, KC * 512)
    s2 = A.f32(s2_off, KC * 512)

    @block.gpsimd
    def _(q):
        for s, n in ready:
            q.wait_ge(s.h, n)
        q.dma_start(out=w1.rearrange("p (k c) -> p k c", k=KC),
                    in_=kb.w["ffn_Wff1"].rearrange("(k p) c -> p k c", p=128)).then_inc(cw.h, 16)
        cw.n += 16
        q.dma_start(out=w2.rearrange("p (f c) -> p f c", f=FFC),
                    in_=kb.w["ffn_Wff2"].rearrange("(f p) c -> p f c", p=128)).then_inc(cw.h, 16)
        cw.n += 16
    w_tok = (cw, cw.n)

    state = {"ps_free": list(ready)}
    bank_free = [[] for _ in range(4)]
    h1b_free = []
    h132_free = [[], []]
    s2_free = []
    h2c_busy = [[], []]
    hid_writable = []

    @block.tensor
    def _(t):
        for s, n in ready:
            t.wait_ge(s.h, n)
        t.wait_ge(cw.h, w_tok[1])

    for tb in range(NTB):
        elem, tbe = tb // 4, tb % 4

        @block.sync
        def _(q, elem=elem, tbe=tbe):
            for s, n in h1b_free:
                q.wait_ge(s.h, n)
            src = kb.ds["h1b"].ap()[elem, :, tbe * 512:(tbe + 1) * 512].rearrange(
                "(c p) w -> p c w", p=128)
            q.dma_start(out=h1btb.rearrange("p (c w) -> p c w", c=KC), in_=src).then_inc(ca.h, 16)
            ca.n += 16
        h1b_tok = (ca, ca.n)

        hid_toks = []
        for ffc in range(FFC):
            bank = ffc % 4

            @block.tensor
            def _(t, ffc=ffc, bank=bank, h1b_tok=h1b_tok):
                if ffc == 0:
                    t.wait_ge(ca.h, h1b_tok[1])
                for s, n in bank_free[bank]:
                    t.wait_ge(s.h, n)
                for k in range(KC):
                    ins = t.matmul(PS[bank][:, 0:512],
                                   w1[:, k * FFD + ffc * 128:k * FFD + (ffc + 1) * 128],
                                   h1btb[:, k * 512:(k + 1) * 512],
                                   start=(k == 0), stop=(k == KC - 1))
                pe.inc(ins)
            pe_t = pe.n

            @block.vector
            def _(v, ffc=ffc, bank=bank, pe_t=pe_t, hw=list(hid_writable)):
                v.wait_ge(pe.h, pe_t)
                for s, n in hw:
                    v.wait_ge(s.h, n)
                b_ap = C.f32(kb.cp_off + 4 * (CP["bff1"] + ffc), 1)
                dv.inc(v.tensor_scalar(hid[:, ffc * 512:(ffc + 1) * 512],
                                       PS[bank][:, 0:512], b_ap, 0.0,
                                       ALU.add, ALU.max))
            bank_free[bank] = [(dv, dv.n)]
            hid_toks.append((dv, dv.n))
        h1b_free = [(pe, pe.n)]

        dc_toks = []
        for dc in range(KC):
            bank = dc % 4
            par = dc % 2

            @block.sync
            def _(q, elem=elem, tbe=tbe, dc=dc, par=par):
                for s, n in h132_free[par]:
                    q.wait_ge(s.h, n)
                q.dma_start(out=A.f32(h132_off + par * 512 * 4, 512),
                            in_=kb.ds["h1f"].ap()[elem, dc * 128:(dc + 1) * 128,
                                                  tbe * 512:(tbe + 1) * 512]).then_inc(ca.h, 16)
                ca.n += 16
            h132_tok = (ca, ca.n)

            @block.tensor
            def _(t, dc=dc, bank=bank, hid_toks=list(hid_toks)):
                if dc == 0:
                    for s, n in hid_toks:
                        t.wait_ge(s.h, n)
                for s, n in bank_free[bank]:
                    t.wait_ge(s.h, n)
                for ffc in range(FFC):
                    ins = t.matmul(PS[bank][:, 0:512],
                                   w2[:, ffc * D + dc * 128:ffc * D + (dc + 1) * 128],
                                   hid[:, ffc * 512:(ffc + 1) * 512],
                                   start=(ffc == 0), stop=(ffc == FFC - 1))
                pe.inc(ins)
            pe_t = pe.n

            @block.vector
            def _(v, dc=dc, bank=bank, par=par, pe_t=pe_t, h132_tok=h132_tok):
                v.wait_ge(pe.h, pe_t)
                v.wait_ge(ca.h, h132_tok[1])
                for s, n in s2_free:
                    v.wait_ge(s.h, n)
                b2 = C.f32(kb.cp_off + 4 * (CP["bff2"] + dc), 1)
                dv.inc(v.scalar_tensor_tensor(s2[:, dc * 512:(dc + 1) * 512],
                                              PS[bank][:, 0:512], b2,
                                              A.f32(h132_off + par * 512 * 4, 512),
                                              ALU.add, ALU.add))
            bank_free[bank] = [(dv, dv.n)]
            h132_free[par] = [(dv, dv.n)]
            dc_toks.append((dv, dv.n))
        s2_free = []

        def y_put(c, tmp, tok, tb=tb, elem=elem, tbe=tbe):
            h2c = A.bf16(h2c_off + (c % 2) * 512 * 2, 512)
            toks = []

            @block.vector
            def _(v, c=c):
                for s, n in h2c_busy[c % 2]:
                    v.wait_ge(s.h, n)
                dv.inc(v.tensor_copy(out=h2c, in_=tmp))
            bf_tok = (dv, dv.n)

            @block.sync
            def _(q, c=c, bf_tok=bf_tok, tok=tok):
                q.wait_ge(tok[0].h, tok[1])
                q.dma_start(out=kb.ds["h2f"].ap()[elem, c * 128:(c + 1) * 128,
                                                  tbe * 512:(tbe + 1) * 512],
                            in_=tmp).then_inc(cs.h, 16)
                cs.n += 16
                toks.append((cs, cs.n))
                q.wait_ge(bf_tok[0].h, bf_tok[1])
                q.dma_start(out=kb.ds["h2b"].ap()[elem, c * 128:(c + 1) * 128,
                                                  tbe * 512:(tbe + 1) * 512],
                            in_=h2c).then_inc(cs.h, 16)
                cs.n += 16
            h2c_busy[c % 2] = [(cs, cs.n)]
            return toks

        state = _ln_block(kb, s2, scr, y_put, gamma=CP["ln2_g"], beta=CP["ln2_b"],
                          ready=dc_toks, state=state)
        s2_free = [(dv, kb.sem("dv").n), (ac, kb.sem("ac").n)]
        hid_writable = state["xbf_free"]
    A.release()
    kb.ln_ps_free = state["ps_free"]
    kb.ffg_done = [(cs, cs.n)]


def _p_gate(kb, ready):
    A, C, block, nc, PS = kb.A, kb.C, kb.block, kb.nc, kb.PS
    cw, ca, cs, pe, dv, ac = (kb.sem(n) for n in ["cw", "ca", "cs", "pe", "dv", "ac"])
    A.mark()
    wg_off = A.alloc(KC * NE * 2)
    h2b_off = A.alloc(KC * 512 * 2)
    lg_off = A.alloc(512 * 4)
    gt_off = A.alloc(32 * 4)
    gg_off = A.alloc(32 * 4)
    sm_off = A.alloc(16 * 4)
    gr_off = A.alloc(512 * 2)
    wg = A.bf16(wg_off, KC * NE)
    h2btb = A.bf16(h2b_off, KC * 512)
    lg = A.f32(lg_off, 512, parts=NE)
    gt = A.f32(gt_off, 32)
    gg = A.f32(gg_off, 32)
    gr = A.bf16(gr_off, 512, parts=NE)
    identf = C.f32(kb.identf_off, 128)
    gbf = [[] for _ in range(4)]

    @block.gpsimd
    def _(q):
        for s, n in ready:
            q.wait_ge(s.h, n)
        q.dma_start(out=wg.rearrange("p (k c) -> p k c", k=KC),
                    in_=kb.w["ffn_Wg"].rearrange("(k p) c -> p k c", p=128)).then_inc(cw.h, 16)
        cw.n += 16
    wg_tok = (cw, cw.n)

    h2b_free = []
    lg_free = []
    gr_free = []
    for tb in range(NTB):
        elem, tbe = tb // 4, tb % 4
        bankA = tb % 2
        bankB = 2 + tb % 2

        @block.sync
        def _(q, elem=elem, tbe=tbe):
            for s, n in h2b_free:
                q.wait_ge(s.h, n)
            src = kb.ds["h2b"].ap()[elem, :, tbe * 512:(tbe + 1) * 512].rearrange(
                "(c p) w -> p c w", p=128)
            q.dma_start(out=h2btb.rearrange("p (c w) -> p c w", c=KC), in_=src).then_inc(ca.h, 16)
            ca.n += 16
        h_tok = (ca, ca.n)

        @block.tensor
        def _(t, bankA=bankA, h_tok=h_tok, tb=tb):
            t.wait_ge(ca.h, h_tok[1])
            if tb == 0:
                t.wait_ge(cw.h, wg_tok[1])
                for s, n in ready:
                    t.wait_ge(s.h, n)
            for s, n in gbf[bankA]:
                t.wait_ge(s.h, n)
            for k in range(KC):
                ins = t.matmul(PS[bankA][0:NE, 0:512], wg[:, k * NE:(k + 1) * NE],
                               h2btb[:, k * 512:(k + 1) * 512],
                               start=(k == 0), stop=(k == KC - 1))
            pe.inc(ins)
        pe_lg = pe.n
        h2b_free = [(pe, pe.n)]

        @block.vector
        def _(v, pe_lg=pe_lg, lgf=list(lg_free), bankA=bankA):
            v.wait_ge(pe.h, pe_lg)
            for s, n in lgf:
                v.wait_ge(s.h, n)
            dv.inc(v.tensor_copy(out=lg, in_=PS[bankA][0:NE, 0:512]))
        ac_lg = dv.n
        gbf[bankA] = [(dv, dv.n)]

        @block.tensor
        def _(t, bankB=bankB, ac_lg=ac_lg):
            t.wait_ge(dv.h, ac_lg)
            for s, n in gbf[bankB]:
                t.wait_ge(s.h, n)
            for g in range(4):
                ins = t.transpose(PS[bankB][:, g * 8:(g + 1) * 8],
                                  lg[:, g * 128:(g + 1) * 128], identf[0:NE, 0:NE])
            pe.inc(ins)
        pe_tr = pe.n
        lg_free = [(pe, pe.n)]

        @block.vector
        def _(v, bankB=bankB, pe_tr=pe_tr):
            v.wait_ge(pe.h, pe_tr)
            dv.inc(v.tensor_copy(out=gt, in_=PS[bankB][:, 0:32]))
        ac_gt = dv.n
        gbf[bankB] = [(dv, dv.n)]

        for g in range(4):
            @block.vector
            def _(v, g=g, ac_gt=ac_gt):
                if g == 0:
                    v.wait_ge(dv.h, ac_gt)
                gsl = gt[:, g * 8:(g + 1) * 8]
                dv.inc(v.reduce_max(A.f32(sm_off, 1), gsl, axis=AX.X))
                v.wait_ge(dv.h, dv.n)
                dv.inc(v.tensor_scalar_mul(A.f32(sm_off + 4, 1), A.f32(sm_off, 1), -1.0))
            dv_mx = dv.n

            @block.scalar
            def _(sq, g=g, dv_mx=dv_mx):
                sq.wait_ge(dv.h, dv_mx)
                ac.inc(sq.activation(out=gg[:, g * 8:(g + 1) * 8],
                                     in_=gt[:, g * 8:(g + 1) * 8], func=AF.Exp,
                                     bias=A.f32(sm_off + 4, 1), scale=1.0,
                                     accum_out=A.f32(sm_off + 8, 1)))
            ac_ex = ac.n

            @block.vector
            def _(v, g=g, ac_ex=ac_ex):
                v.wait_ge(ac.h, ac_ex)
                dv.inc(v.reciprocal(A.f32(sm_off + 12, 1), A.f32(sm_off + 8, 1)))
                v.wait_ge(dv.h, dv.n)
                gsl = gt[:, g * 8:(g + 1) * 8]
                v.tensor_scalar(gg[:, g * 8:(g + 1) * 8], gsl, A.f32(sm_off, 1), None,
                                ALU.is_equal)
                dv.inc(v.tensor_scalar(gg[:, g * 8:(g + 1) * 8], gg[:, g * 8:(g + 1) * 8],
                                       A.f32(sm_off + 12, 1), None, ALU.mult))
            dv_gg = dv.n

        @block.tensor
        def _(t, bankB=bankB, dv_gg=dv_gg):
            t.wait_ge(dv.h, dv_gg)
            for s, n in gbf[bankB]:
                t.wait_ge(s.h, n)
            for g in range(4):
                ins = t.transpose(PS[bankB][0:NE, g * 128:(g + 1) * 128],
                                  gg[:, g * 8:(g + 1) * 8], identf)
            pe.inc(ins)
        pe_gb = pe.n

        @block.vector
        def _(v, bankB=bankB, pe_gb=pe_gb, grf=list(gr_free)):
            v.wait_ge(pe.h, pe_gb)
            for s, n in grf:
                v.wait_ge(s.h, n)
            dv.inc(v.tensor_copy(out=gr, in_=PS[bankB][0:NE, 0:512]))
        ac_gr = dv.n
        gbf[bankB] = [(dv, dv.n)]

        @block.sync
        def _(q, tb=tb, ac_gr=ac_gr):
            q.wait_ge(dv.h, ac_gr)
            q.dma_start(out=kb.ds["gall"].ap()[:, tb * 512:(tb + 1) * 512],
                        in_=gr).then_inc(cs.h, 16)
            cs.n += 16
        gr_free = [(cs, cs.n)]
    A.release()
    kb.gate_done = [(cs, cs.n)]


def _p_moe(kb, ready):
    A, C, block, nc, PS = kb.A, kb.C, kb.block, kb.nc, kb.PS
    cw, ca, cs, cb, pe, dv, ac = (kb.sem(n) for n in ["cw", "ca", "cs", "cb", "pe", "dv", "ac"])
    A.mark()
    w1_off = A.alloc(KC * FFD * 2)
    w2_off = A.alloc(FFC * D * 2)
    hid_off = A.alloc(FFC * 512 * 2)
    h2tb_off = A.alloc(KC * 512 * 2)
    gb_off = A.alloc(2 * 512 * 2)
    stage_off = A.alloc(KC * 512 * 4)
    be1_off = A.alloc(FFD * 2)
    ge_off = A.alloc(2 * 512 * 2)
    ga_off = A.alloc(2 * 512 * 2)
    be2_off = A.alloc(D * 2)
    w1 = A.bf16(w1_off, KC * FFD)
    w2 = A.bf16(w2_off, FFC * D)
    hid = A.bf16(hid_off, FFC * 512)
    h2tb = A.bf16(h2tb_off, KC * 512)
    gb_ap = lambda p_: A.bf16(gb_off + p_ * 512 * 2, 512)
    stage = A.f32(stage_off, KC * 512)
    be1r = A.bf16(be1_off, FFD, parts=1)
    ge_ap = lambda p_: A.bf16(ge_off + p_ * 512 * 2, 512, parts=1)
    ga_ap = lambda p_: A.bf16(ga_off + p_ * 512 * 2, 512, parts=NE)
    be2 = A.bf16(be2_off, D, parts=NE)
    ones_row = C.bf16(kb.ones_row_bf_off, 128, parts=1)

    @block.gpsimd
    def _(q):
        for s, n in ready:
            q.wait_ge(s.h, n)
        q.dma_start(out=be2, in_=kb.w["ffn_be2"][:, :]).then_inc(cw.h, 16)
        cw.n += 16
    be2_tok = (cw, cw.n)

    bank_free = [[] for _ in range(4)]
    w_free = []
    h2_free = []
    gb_free = [[], []]
    ge_free = [[], []]
    ga_free = [[], []]
    stage_free = []
    acc_state = [[] for _ in range(NTB)]
    hid_writable = []

    for e in range(NE):
        @block.gpsimd
        def _(q, e=e, wf=list(w_free)):
            if e == 0:
                for s, n in ready:
                    q.wait_ge(s.h, n)
            for s, n in wf:
                q.wait_ge(s.h, n)
            q.dma_start(out=w1.rearrange("p (k c) -> p k c", k=KC),
                        in_=kb.w["ffn_We1"][e].rearrange("(k p) c -> p k c", p=128)).then_inc(cw.h, 16)
            cw.n += 16
            q.dma_start(out=w2.rearrange("p (f c) -> p f c", f=FFC),
                        in_=kb.w["ffn_We2"][e].rearrange("(f p) c -> p f c", p=128)).then_inc(cw.h, 16)
            cw.n += 16
            q.dma_start(out=be1r, in_=kb.w["ffn_be1"][e][None, :]).then_inc(cw.h, 16)
            cw.n += 16
        w_tok = (cw, cw.n)

        for tb in range(NTB):
            elem, tbe = tb // 4, tb % 4
            par = tb % 2

            @block.sync
            def _(q, e=e, tb=tb, elem=elem, tbe=tbe, par=par, h2f=list(h2_free)):
                for s, n in h2f + ge_free[par] + ga_free[par]:
                    q.wait_ge(s.h, n)
                src = kb.ds["h2b"].ap()[elem, :, tbe * 512:(tbe + 1) * 512].rearrange(
                    "(c p) w -> p c w", p=128)
                q.dma_start(out=h2tb.rearrange("p (c w) -> p c w", c=KC),
                            in_=src).then_inc(ca.h, 16)
                ca.n += 16
                q.dma_start(out=ge_ap(par), in_=kb.ds["gall"].ap()[e:e + 1, tb * 512:(tb + 1) * 512]
                            ).then_inc(ca.h, 16)
                ca.n += 16
                if e == NE - 1:
                    q.dma_start(out=ga_ap(par), in_=kb.ds["gall"].ap()[:, tb * 512:(tb + 1) * 512]
                                ).then_inc(ca.h, 16)
                    ca.n += 16
            ld_tok = (ca, ca.n)

            bankg = 3

            @block.tensor
            def _(t, e=e, tb=tb, par=par, bankg=bankg, ld_tok=ld_tok, w_tok=w_tok,
                  gbf=list(gb_free[par])):
                t.wait_ge(ca.h, ld_tok[1])
                if tb == 0:
                    t.wait_ge(cw.h, w_tok[1])
                    if e == 0:
                        t.wait_ge(cw.h, be2_tok[1])
                for s, n in bank_free[bankg]:
                    t.wait_ge(s.h, n)
                for s, n in gbf:
                    t.wait_ge(s.h, n)
                pe.inc(t.matmul(PS[bankg][:, 0:512], ones_row, ge_ap(par),
                                start=True, stop=True))
            pe_gb = pe.n

            @block.vector
            def _(v, par=par, bankg=bankg, pe_gb=pe_gb):
                v.wait_ge(pe.h, pe_gb)
                dv.inc(v.tensor_copy(out=gb_ap(par), in_=PS[bankg][:, 0:512]))
            bank_free[bankg] = [(dv, dv.n)]
            gb_tok = (dv, dv.n)

            @block.vector
            def _(v, par=par, gb_tok=gb_tok):
                v.wait_ge(dv.h, gb_tok[1])
                for k in range(KC):
                    ins = v.tensor_mul(h2tb[:, k * 512:(k + 1) * 512],
                                       h2tb[:, k * 512:(k + 1) * 512], gb_ap(par))
                dv.inc(ins)
            gh_tok = (dv, dv.n)
            gb_free[par] = [(dv, dv.n)]

            hid_toks = []
            for ffc in range(FFC):
                bank = ffc % 3

                @block.tensor
                def _(t, e=e, ffc=ffc, par=par, bank=bank, gh_tok=gh_tok):
                    if ffc == 0:
                        t.wait_ge(dv.h, gh_tok[1])
                    for s, n in bank_free[bank]:
                        t.wait_ge(s.h, n)
                    for k in range(KC):
                        t.matmul(PS[bank][:, 0:512],
                                 w1[:, k * FFD + ffc * 128:k * FFD + (ffc + 1) * 128],
                                 h2tb[:, k * 512:(k + 1) * 512],
                                 start=(k == 0), stop=False)
                    pe.inc(t.matmul(PS[bank][:, 0:512],
                                    be1r[0:1, ffc * 128:(ffc + 1) * 128], ge_ap(par),
                                    start=False, stop=True))
                pe_t = pe.n

                @block.vector
                def _(v, ffc=ffc, bank=bank, pe_t=pe_t, hw=list(hid_writable)):
                    v.wait_ge(pe.h, pe_t)
                    for s, n in hw:
                        v.wait_ge(s.h, n)
                    dv.inc(v.tensor_scalar(hid[:, ffc * 512:(ffc + 1) * 512],
                                           PS[bank][:, 0:512], 0.0, None, ALU.max))
                bank_free[bank] = [(dv, dv.n)]
                hid_toks.append((dv, dv.n))
            hid_writable = []
            h2_free = [(pe, pe.n)]
            ge_free[par] = [(pe, pe.n)]

            dc_toks = []
            for dc in range(KC):
                bank = dc % 3

                @block.tensor
                def _(t, e=e, dc=dc, par=par, bank=bank, hid_toks=list(hid_toks)):
                    if dc == 0:
                        for s, n in hid_toks:
                            t.wait_ge(s.h, n)
                    for s, n in bank_free[bank]:
                        t.wait_ge(s.h, n)
                    last = (e != NE - 1)
                    for ffc in range(FFC):
                        ins = t.matmul(PS[bank][:, 0:512],
                                       w2[:, ffc * D + dc * 128:ffc * D + (dc + 1) * 128],
                                       hid[:, ffc * 512:(ffc + 1) * 512],
                                       start=(ffc == 0),
                                       stop=(last and ffc == FFC - 1))
                    if e == NE - 1:
                        ins = t.matmul(PS[bank][:, 0:512],
                                       be2[:, dc * 128:(dc + 1) * 128], ga_ap(par),
                                       start=False, stop=True)
                    pe.inc(ins)
                pe_t = pe.n

                @block.vector
                def _(v, dc=dc, bank=bank, pe_t=pe_t, sf=list(stage_free)):
                    v.wait_ge(pe.h, pe_t)
                    for s, n in sf:
                        v.wait_ge(s.h, n)
                    dv.inc(v.tensor_copy(out=stage[:, dc * 512:(dc + 1) * 512],
                                         in_=PS[bank][:, 0:512]))
                bank_free[bank] = [(dv, dv.n)]
                dc_toks.append((dv, dv.n))
            if e == NE - 1:
                ga_free[par] = [(pe, pe.n)]
            stage_free = []

            accv = kb.ds["moeacc"].ap()[:, tb * 512:(tb + 1) * 512].rearrange(
                "(c p) w -> p c w", p=128)

            @block.gpsimd
            def _(q, e=e, tb=tb, dc_toks=list(dc_toks), accst=list(acc_state[tb])):
                if e > 0:
                    for s, n in dc_toks + accst:
                        q.wait_ge(s.h, n)
                    q.dma_start(out=stage.rearrange("p (c w) -> p c w", c=KC), in_=accv,
                                accum_op=ALU.add).then_inc(cb.h, 16)
                    cb.n += 16
            acc_tok = (cb, cb.n)

            @block.sync
            def _(q, e=e, tb=tb, dc_toks=list(dc_toks), acc_tok=acc_tok,
                  accst=list(acc_state[tb])):
                if e > 0:
                    q.wait_ge(cb.h, acc_tok[1])
                else:
                    for s, n in dc_toks + accst:
                        q.wait_ge(s.h, n)
                q.dma_start(out=accv, in_=stage.rearrange("p (c w) -> p c w", c=KC)
                            ).then_inc(cs.h, 16)
                cs.n += 16
            acc_state[tb] = [(cs, cs.n)]
            stage_free = [(cs, cs.n)]
        w_free = [(pe, pe.n)]
    A.release()
    kb.moe_done = [(cs, cs.n)]


def _p_out(kb, x_src, out_dst, scale, ready):
    A, C, block, nc, PS = kb.A, kb.C, kb.block, kb.nc, kb.PS
    cw, ca, cs, pe, dv, ac = (kb.sem(n) for n in ["cw", "ca", "cs", "pe", "dv", "ac"])
    A.mark()
    wo_off = A.alloc(KC * D * 2)
    acc_off = A.alloc(KC * 512 * 4)
    h2f_off = A.alloc(KC * 512 * 4)
    s3_off = A.alloc(KC * 512 * 4)
    h3_off = A.alloc(KC * 512 * 2)
    xr_off = A.alloc(KC * 512 * 4)
    oc_off = A.alloc(2 * 512 * 4)
    scr = ln_scr_alloc(A)
    wo = A.bf16(wo_off, KC * D)
    acc = A.f32(acc_off, KC * 512)
    h2f = A.f32(h2f_off, KC * 512)
    s3 = A.f32(s3_off, KC * 512)
    h3 = A.bf16(h3_off, KC * 512)
    xr = A.f32(xr_off, KC * 512)

    @block.gpsimd
    def _(q):
        for s, n in ready:
            q.wait_ge(s.h, n)
        q.dma_start(out=wo.rearrange("p (k c) -> p k c", k=KC),
                    in_=kb.w["ffn_Wout"].rearrange("(k p) c -> p k c", p=128)).then_inc(cw.h, 16)
        cw.n += 16
    wo_tok = (cw, cw.n)

    state = {"ps_free": list(ready)}
    bank_free = [[] for _ in range(4)]
    in_free = []
    s3_free = []
    h3_free = []
    xr_free = []
    oc_busy = [[], []]

    @block.tensor
    def _(t):
        for s, n in ready:
            t.wait_ge(s.h, n)
        t.wait_ge(cw.h, wo_tok[1])

    for tb in range(NTB):
        elem, tbe = tb // 4, tb % 4

        @block.sync
        def _(q, elem=elem, tbe=tbe, tb=tb, inf=list(in_free)):
            for s, n in inf:
                q.wait_ge(s.h, n)
            q.dma_start(out=acc.rearrange("p (c w) -> p c w", c=KC),
                        in_=kb.ds["moeacc"].ap()[:, tb * 512:(tb + 1) * 512].rearrange(
                            "(c p) w -> p c w", p=128)).then_inc(ca.h, 16)
            ca.n += 16
            q.dma_start(out=h2f.rearrange("p (c w) -> p c w", c=KC),
                        in_=kb.ds["h2f"].ap()[elem, :, tbe * 512:(tbe + 1) * 512].rearrange(
                            "(c p) w -> p c w", p=128)).then_inc(ca.h, 16)
            ca.n += 16
        ld_tok = (ca, ca.n)

        @block.vector
        def _(v, ld_tok=ld_tok, s3f=list(s3_free)):
            v.wait_ge(ca.h, ld_tok[1])
            for s, n in s3f:
                v.wait_ge(s.h, n)
            for c in range(KC):
                ins = v.tensor_add(s3[:, c * 512:(c + 1) * 512],
                                   h2f[:, c * 512:(c + 1) * 512],
                                   acc[:, c * 512:(c + 1) * 512])
            dv.inc(ins)
        s3_tok = (dv, dv.n)
        in_free = [(dv, dv.n)]

        def y_put(c, tmp, tok):
            @block.vector
            def _(v, c=c):
                for s, n in h3_free:
                    v.wait_ge(s.h, n)
                dv.inc(v.tensor_copy(out=h3[:, c * 512:(c + 1) * 512], in_=tmp))
            return [(dv, dv.n)]

        state = _ln_block(kb, s3, scr, y_put, gamma=CP["ln3_g"], beta=CP["ln3_b"],
                          ready=[s3_tok], state=state)
        h3_tok = (dv, kb.sem("dv").n)
        s3_free = [(dv, kb.sem("dv").n), (ac, kb.sem("ac").n)]

        @block.sync
        def _(q, elem=elem, tbe=tbe, xrf=list(xr_free)):
            for s, n in xrf:
                q.wait_ge(s.h, n)
            q.dma_start(out=xr.rearrange("p (c w) -> p c w", c=KC),
                        in_=x_src[elem][:, tbe * 512:(tbe + 1) * 512].rearrange(
                            "(c p) w -> p c w", p=128)).then_inc(ca.h, 16)
            ca.n += 16
        xr_tok = (ca, ca.n)

        h3_free = []
        for dc in range(KC):
            bank = dc % 4
            par = dc % 2

            @block.tensor
            def _(t, dc=dc, bank=bank, h3_tok=h3_tok):
                if dc == 0:
                    t.wait_ge(dv.h, h3_tok[1])
                for s, n in bank_free[bank]:
                    t.wait_ge(s.h, n)
                for k in range(KC):
                    ins = t.matmul(PS[bank][:, 0:512],
                                   wo[:, k * D + dc * 128:k * D + (dc + 1) * 128],
                                   h3[:, k * 512:(k + 1) * 512],
                                   start=(k == 0), stop=(k == KC - 1))
                pe.inc(ins)
            pe_t = pe.n
            h3_free.append((pe, pe.n))

            @block.vector
            def _(v, dc=dc, bank=bank, par=par, pe_t=pe_t, xr_tok=xr_tok,
                  ocb=list(oc_busy[par])):
                v.wait_ge(pe.h, pe_t)
                v.wait_ge(ca.h, xr_tok[1])
                for s, n in ocb:
                    v.wait_ge(s.h, n)
                dv.inc(v.scalar_tensor_tensor(A.f32(oc_off + par * 512 * 4, 512),
                                              PS[bank][:, 0:512], scale,
                                              xr[:, dc * 512:(dc + 1) * 512],
                                              ALU.mult, ALU.add))
            bank_free[bank] = [(dv, dv.n)]
            oc_tok = (dv, dv.n)

            @block.sync
            def _(q, elem=elem, tbe=tbe, dc=dc, par=par, oc_tok=oc_tok):
                q.wait_ge(dv.h, oc_tok[1])
                q.dma_start(out=out_dst[elem][dc * 128:(dc + 1) * 128,
                                              tbe * 512:(tbe + 1) * 512],
                            in_=A.f32(oc_off + par * 512 * 4, 512)).then_inc(cs.h, 16)
                cs.n += 16
            oc_busy[par] = [(cs, cs.n)]
        xr_free = [(dv, dv.n)]
    A.release()
    kb.ln_ps_free = state["ps_free"]
    kb.out_done = [(cs, cs.n)]


def _p_sqkv(kb, z_off, q_off, k_off, vt_off, wbuf_off, bvr_off, ready):
    A, C, block, nc, PS = kb.A, kb.C, kb.block, kb.nc, kb.PS
    cw, pe, ac, dv = kb.sem("cw"), kb.sem("pe"), kb.sem("ac"), kb.sem("dv")
    names = ["attn_wq", "attn_wk", "attn_wv"]
    wb = lambda par: A.bf16(wbuf_off + par * KC * 512 * 2, KC * 512)
    z_ap = lambda k: A.bf16(z_off + 2 * k * S, S)
    bvr = A.bf16(bvr_off, D, parts=1)
    ones_row = C.bf16(kb.ones_row_bf_off, 128, parts=1)

    @block.tensor
    def _(t):
        for s, n in ready:
            t.wait_ge(s.h, n)

    @block.gpsimd
    def _(q):
        for s, n in ready:
            q.wait_ge(s.h, n)
        q.dma_start(out=bvr, in_=kb.w["attn_bv"][None, :]).then_inc(cw.h, 16)
        cw.n += 16
    bv_tok = (cw, cw.n)

    wload = {}
    wfree = [[], []]
    bank_free = [[] for _ in range(4)]

    def emit_load(cb):
        par = cb % 2
        mat, half = names[cb // 2], cb % 2

        @block.gpsimd
        def _(q):
            for s, n in wfree[par]:
                q.wait_ge(s.h, n)
            src = kb.w[mat][:, half * 512:(half + 1) * 512].rearrange("(k p) c -> p k c", p=128)
            q.dma_start(out=wb(par).rearrange("p (k c) -> p k c", k=KC), in_=src).then_inc(cw.h, 16)
            cw.n += 16
        wload[cb] = (cw, cw.n)

    emit_load(0)
    emit_load(1)
    nbank = 0
    for cb in range(6):
        par = cb % 2
        if cb < 4:
            dst_off = q_off if cb < 2 else k_off
            bias = "bq" if cb < 2 else "bk"
            fbase = (cb % 2) * 4
            for m in range(4):
                f = fbase + m
                for n in range(4):
                    bank = nbank % 4
                    nbank += 1

                    @block.tensor
                    def _(t, cb=cb, par=par, m=m, n=n, bank=bank):
                        t.wait_ge(cw.h, wload[cb][1])
                        for s, nn in bank_free[bank]:
                            t.wait_ge(s.h, nn)
                        for k in range(KC):
                            ins = t.matmul(PS[bank][:, 0:512],
                                           wb(par)[:, k * 512 + m * 128:k * 512 + (m + 1) * 128],
                                           z_ap(k)[:, n * 512:(n + 1) * 512],
                                           start=(k == 0), stop=(k == KC - 1))
                        pe.inc(ins)
                    pe_t = pe.n

                    @block.vector
                    def _(v, f=f, n=n, bank=bank, pe_t=pe_t, dst_off=dst_off, bias=bias):
                        v.wait_ge(pe.h, pe_t)
                        dst = kb.A.bf16(dst_off + 2 * (f * S + n * 512), 512)
                        b_ap = C.f32(kb.cp_off + 4 * (CP[bias] + f), 1)
                        dv.inc(v.tensor_scalar(dst, PS[bank][:, 0:512], b_ap, None, ALU.add))
                    bank_free[bank] = [(dv, dv.n)]
            wfree[par] = [(pe, pe.n)]
            if cb + 2 < 6:
                emit_load(cb + 2)
        else:
            vb = cb - 4
            for tt in range(16):
                bank = nbank % 4
                nbank += 1

                @block.tensor
                def _(t, cb=cb, par=par, tt=tt, vb=vb, bank=bank):
                    t.wait_ge(cw.h, wload[cb][1])
                    if tt == 0:
                        t.wait_ge(cw.h, bv_tok[1])
                    for s, nn in bank_free[bank]:
                        t.wait_ge(s.h, nn)
                    for k in range(KC):
                        t.matmul(PS[bank][:, 0:512],
                                 z_ap(k)[:, tt * 128:(tt + 1) * 128],
                                 wb(par)[:, k * 512:(k + 1) * 512],
                                 start=(k == 0), stop=False)
                    pe.inc(t.matmul(PS[bank][:, 0:512], ones_row,
                                    bvr[0:1, vb * 512:(vb + 1) * 512],
                                    start=False, stop=True))
                pe_t = pe.n

                @block.vector
                def _(v, tt=tt, vb=vb, bank=bank, pe_t=pe_t):
                    v.wait_ge(pe.h, pe_t)
                    dst = kb.A.bf16(vt_off + 2 * (tt * 1024 + vb * 512), 512)
                    dv.inc(v.tensor_copy(out=dst, in_=PS[bank][:, 0:512]))
                bank_free[bank] = [(dv, dv.n)]
            wfree[par] = [(pe, pe.n)]
            if cb + 2 < 6:
                emit_load(cb + 2)

    kb.sqkv_done = [(ac, ac.n), (dv, dv.n)]


def _sp_scr(kb):
    A = kb.A
    return {
        "sc": A.alloc(3 * WIN * 4), "p": A.alloc(2 * WIN * 2), "pT": A.alloc(2 * WIN * 2),
        "cst": A.alloc(S * 2), "nm": A.alloc(16 * 4),
    }


def _p_satt(kb, q_off, k_off, vt_off, ctx_off, mask_off, scr, ready):
    A, C, block, nc, PS, PT = kb.A, kb.C, kb.block, kb.nc, kb.PS, kb.PT
    pe, dv, ac, cs = kb.sem("pe"), kb.sem("dv"), kb.sem("ac"), kb.sem("cs")
    ident = C.bf16(kb.ident_off, 128)
    sc_ap = lambda sl, w: A.f32(scr["sc"] + sl * WIN * 4, w)
    p_ap = lambda par, w: A.bf16(scr["p"] + par * WIN * 2, w)
    pT_ap = lambda par, w: A.bf16(scr["pT"] + par * WIN * 2, w)
    cst = A.bf16(scr["cst"], S, parts=64)
    nm_ap = lambda i: A.f32(scr["nm"] + i * 4, 1)
    MW = 512 + WIN + 512

    head_tok = list(ready)
    sc_state = [[] for _ in range(3)]
    p_state = [[], []]
    pT_state = [[], []]
    PTb_state = [[], []]
    ab_state = [[] for _ in range(4)]
    ctx_state = []
    nm_state = [[], []]
    cst_free = []
    gidx = 0
    sidx = 0

    for h in range(NH):
        bb = (h % 2) * 64
        hc = h // 2
        for c in range(NCH):
            lo = max(0, 2 * c - 2)
            hi = min(16, 2 * c + 4)
            w128 = hi - lo
            wk = w128 * 128
            mvar = 0 if c == 0 else (2 if c == NCH - 1 else 1)
            mbase_col = [0, 512, 512 + WIN][mvar]
            for qt in range(2):
                scs = sidx % 3
                par = sidx % 2
                b01 = (sidx % 2) * 2
                sidx += 1
                qcol = c * 256 + qt * 128

                blocks = []
                o = 0
                while o < wk:
                    wb_ = min(512, wk - o)
                    blocks.append((o, wb_))
                    o += wb_
                for bi, (ko, wb_) in enumerate(blocks):
                    bank = b01 + bi % 2

                    @block.tensor
                    def _(t, h=h, bb=bb, hc=hc, qcol=qcol, lo=lo, ko=ko, wb_=wb_, bank=bank):
                        if gidx == 0 and ko == 0:
                            for s, n in head_tok:
                                t.wait_ge(s.h, n)
                        for s, n in ab_state[bank]:
                            t.wait_ge(s.h, n)
                        ksl = A.bf16(k_off, KC * S, parts=64, p0=bb)
                        qsl = A.bf16(q_off, KC * S, parts=64, p0=bb)
                        pe.inc(t.matmul(PS[bank][:, 0:wb_],
                                        qsl[:, hc * S + qcol:hc * S + qcol + 128],
                                        ksl[:, hc * S + lo * 128 + ko:hc * S + lo * 128 + ko + wb_],
                                        start=True, stop=True))
                    pe_t = pe.n

                    @block.vector
                    def _(v, scs=scs, ko=ko, wb_=wb_, bank=bank, pe_t=pe_t, qt=qt,
                          mbase_col=mbase_col, scst=list(sc_state[scs])):
                        v.wait_ge(pe.h, pe_t)
                        for s, n in scst:
                            v.wait_ge(s.h, n)
                        mask = A.bf16(mask_off + 2 * (qt * MW + mbase_col + ko), wb_)
                        dv.inc(v.tensor_add(sc_ap(scs, WIN)[:, ko:ko + wb_],
                                            PS[bank][:, 0:wb_], mask))
                    ab_state[bank] = [(dv, dv.n)]
                sc_state[scs] = []
                sc_tok = (dv, dv.n)

                @block.vector
                def _(v, scs=scs, wk=wk, par=par, sc_tok=sc_tok, nmst=list(nm_state[par])):
                    v.wait_ge(dv.h, sc_tok[1])
                    for s, n in nmst:
                        v.wait_ge(s.h, n)
                    dv.inc(v.reduce_max(nm_ap(par), sc_ap(scs, wk), axis=AX.X))
                    v.wait_ge(dv.h, dv.n)
                    dv.inc(v.tensor_scalar_mul(nm_ap(2 + par), nm_ap(par), -0.125))
                dv_nm = dv.n

                @block.scalar
                def _(sq, scs=scs, wk=wk, par=par, dv_nm=dv_nm, pst=list(p_state[par])):
                    sq.wait_ge(dv.h, dv_nm)
                    for s, n in pst:
                        sq.wait_ge(s.h, n)
                    ac.inc(sq.activation(out=p_ap(par, wk), in_=sc_ap(scs, wk), func=AF.Exp,
                                         bias=nm_ap(2 + par), scale=0.125,
                                         accum_out=nm_ap(4 + par)))
                ac_exp = ac.n
                sc_state[scs] = [(ac, ac.n)]
                p_state[par] = []

                @block.vector
                def _(v, par=par, wk=wk, ac_exp=ac_exp):
                    v.wait_ge(ac.h, ac_exp)
                    dv.inc(v.reciprocal(nm_ap(6 + par), nm_ap(4 + par)))
                    v.wait_ge(dv.h, dv.n)
                    dv.inc(v.tensor_scalar(p_ap(par, wk), p_ap(par, wk), nm_ap(6 + par),
                                           None, ALU.mult))
                dv_p = dv.n
                nm_state[par] = [(dv, dv.n)]

                bankb = gidx % 2
                gidx += 1

                @block.tensor
                def _(t, w128=w128, par=par, bankb=bankb, dv_p=dv_p):
                    t.wait_ge(dv.h, dv_p)
                    for s, n in PTb_state[bankb]:
                        t.wait_ge(s.h, n)
                    for i in range(w128):
                        ins = t.transpose(PT[bankb][:, i * 128:(i + 1) * 128],
                                          p_ap(par, WIN)[:, i * 128:(i + 1) * 128], ident)
                    pe.inc(ins)
                pe_tr = pe.n
                p_state[par] = p_state[par] + [(pe, pe.n)]

                @block.vector
                def _(v, w128=w128, par=par, bankb=bankb, pe_tr=pe_tr, ptst=list(pT_state[par])):
                    v.wait_ge(pe.h, pe_tr)
                    for s, n in ptst:
                        v.wait_ge(s.h, n)
                    dv.inc(v.tensor_copy(out=pT_ap(par, w128 * 128),
                                         in_=PT[bankb][:, 0:w128 * 128]))
                PTb_state[bankb] = [(dv, dv.n)]
                pT_tok = (dv, dv.n)

                @block.tensor
                def _(t, h=h, w128=w128, lo=lo, par=par, pT_tok=pT_tok, ctxst=list(ctx_state)):
                    t.wait_ge(dv.h, pT_tok[1])
                    for s, n in ctxst:
                        t.wait_ge(s.h, n)
                    for i in range(w128):
                        kbk = lo + i
                        ins = t.matmul(PS[4][0:64, 0:128],
                                       A.bf16(vt_off + 2 * (kbk * 1024 + h * DH), DH),
                                       pT_ap(par, WIN)[:, i * 128:(i + 1) * 128],
                                       start=(i == 0), stop=(i == w128 - 1))
                    pe.inc(ins)
                pe_ctx = pe.n
                pT_state[par] = [(pe, pe.n)]

                @block.vector
                def _(v, qcol=qcol, pe_ctx=pe_ctx, cstf=list(cst_free), first=(c == 0 and qt == 0)):
                    v.wait_ge(pe.h, pe_ctx)
                    if first:
                        for s, n in cstf:
                            v.wait_ge(s.h, n)
                    dv.inc(v.tensor_copy(out=cst[:, qcol:qcol + 128],
                                         in_=PS[4][0:64, 0:128]))
                ctx_state = [(dv, dv.n)]

        @block.sync
        def _(q, h=h, bb=bb, hc=hc):
            q.wait_ge(dv.h, dv.n)
            dst = A.bf16(ctx_off + 2 * (hc * S), S, parts=64, p0=bb)
            q.dma_start(out=dst, in_=cst[:, :]).then_inc(cs.h, 16)
            cs.n += 16
        cst_free = [(cs, cs.n)]
        head_tok = [(pe, pe.n), (cs, cs.n)]

    kb.satt_done = [(cs, cs.n)]


def _p_swo(kb, elem, ctx_off, x_src, x_dst, ready):
    A, C, block, nc, PS = kb.A, kb.C, kb.block, kb.nc, kb.PS
    cw, ca, cs, pe, dv = (kb.sem(n) for n in ["cw", "ca", "cs", "pe", "dv"])
    A.mark()
    wo_off = A.alloc(KC * D * 2)
    x1_off = A.alloc(KC * 512 * 4)
    oc_off = A.alloc(2 * 512 * 4)
    wo = A.bf16(wo_off, KC * D)
    x1 = A.f32(x1_off, KC * 512)

    @block.gpsimd
    def _(q):
        for s, n in ready:
            q.wait_ge(s.h, n)
        q.dma_start(out=wo.rearrange("p (k c) -> p k c", k=KC),
                    in_=kb.w["attn_wo"].rearrange("(k p) c -> p k c", p=128)).then_inc(cw.h, 16)
        cw.n += 16
    wo_tok = (cw, cw.n)

    bank_free = [[] for _ in range(4)]
    x1_free = []
    oc_busy = [[], []]
    for tb in range(S // 512):
        @block.sync
        def _(q, tb=tb, x1f=list(x1_free)):
            for s, n in x1f:
                q.wait_ge(s.h, n)
            q.dma_start(out=x1.rearrange("p (c w) -> p c w", c=KC),
                        in_=x_src[elem][:, tb * 512:(tb + 1) * 512].rearrange(
                            "(c p) w -> p c w", p=128)).then_inc(ca.h, 16)
            ca.n += 16
        x1_tok = (ca, ca.n)

        for dc in range(KC):
            bank = dc % 4
            par = dc % 2

            @block.tensor
            def _(t, tb=tb, dc=dc, bank=bank):
                if tb == 0 and dc == 0:
                    t.wait_ge(cw.h, wo_tok[1])
                    for s, n in ready:
                        t.wait_ge(s.h, n)
                for s, n in bank_free[bank]:
                    t.wait_ge(s.h, n)
                for k in range(KC):
                    ins = t.matmul(PS[bank][:, 0:512],
                                   wo[:, k * D + dc * 128:k * D + (dc + 1) * 128],
                                   A.bf16(ctx_off + 2 * (k * S + tb * 512), 512),
                                   start=(k == 0), stop=(k == KC - 1))
                pe.inc(ins)
            pe_t = pe.n

            @block.vector
            def _(v, dc=dc, bank=bank, par=par, pe_t=pe_t, x1_tok=x1_tok,
                  ocb=list(oc_busy[par])):
                v.wait_ge(pe.h, pe_t)
                v.wait_ge(ca.h, x1_tok[1])
                for s, n in ocb:
                    v.wait_ge(s.h, n)
                bo = C.f32(kb.cp_off + 4 * (CP["bo"] + dc), 1)
                dv.inc(v.scalar_tensor_tensor(A.f32(oc_off + par * 512 * 4, 512),
                                              PS[bank][:, 0:512], bo,
                                              x1[:, dc * 512:(dc + 1) * 512],
                                              ALU.add, ALU.add))
            bank_free[bank] = [(dv, dv.n)]
            oc_tok = (dv, dv.n)

            @block.sync
            def _(q, tb=tb, dc=dc, par=par, oc_tok=oc_tok):
                q.wait_ge(dv.h, oc_tok[1])
                q.dma_start(out=x_dst[elem][dc * 128:(dc + 1) * 128, tb * 512:(tb + 1) * 512],
                            in_=A.f32(oc_off + par * 512 * 4, 512)).then_inc(cs.h, 16)
                cs.n += 16
            oc_busy[par] = [(cs, cs.n)]
        x1_free = [(dv, dv.n)]
    A.release()
    kb.swo_done = [(cs, cs.n)]


def _p_mln(kb, src, y_sb_off, f32_dram=None, scr=None, ready0=()):
    A, block, nc = kb.A, kb.block, kb.nc
    ca, cs, dv, ac = kb.sem("ca"), kb.sem("cs"), kb.sem("dv"), kb.sem("ac")
    nblk = S // LNW
    state = {"ps_free": list(kb.ln_ps_free), "tmp_busy": None}
    xloaded = []
    xfree = scr.get("xfree", [[], []])

    for b in range(nblk):
        par = b % 2
        x_blk = A.f32(scr["xblk"], KC * LNW)

        @block.sync
        def _(q, b=b, par=par, x_blk=x_blk):
            if b == 0:
                for s, n in ready0:
                    q.wait_ge(s.h, n)
            for s, n in xfree[0] + xfree[1]:
                q.wait_ge(s.h, n)
            src_blk = src[:, b * LNW:(b + 1) * LNW].rearrange("(c p) w -> p c w", p=128)
            out3 = x_blk.rearrange("p (c w) -> p c w", c=KC)
            q.dma_start(out=out3, in_=src_blk).then_inc(ca.h, 16); ca.n += 16
        xloaded.append((ca, ca.n))

        def y_put(c, tmp, tok, b=b, par=par):
            toks = []
            @block.vector
            def _(v):
                y_sb = A.bf16(y_sb_off + 2 * (c * S + b * LNW), LNW)
                v.tensor_copy(out=y_sb, in_=tmp)
            if f32_dram is not None:
                @block.sync
                def _(q):
                    q.wait_ge(tok[0].h, tok[1])
                    q.dma_start(out=f32_dram[c * 128:(c + 1) * 128, b * LNW:(b + 1) * LNW],
                                in_=tmp).then_inc(cs.h, 16)
                    cs.n += 16
                toks.append((cs, cs.n))
            else:
                toks.append(tok)
            return toks

        state["tmp_busy"] = scr.setdefault("tmp_busy", {0: [], 1: []})
        st = _ln_block(kb, x_blk, scr, y_put, ready=[xloaded[-1]] + list(kb.const_tok) + list(ready0), state=state)
        state = st
        scr["tmp_busy"] = st["tmp_busy"]
        xfree[par] = [(dv, kb.sem("dv").n), (ac, kb.sem("ac").n)]
    scr["xfree"] = xfree
    kb.ln_ps_free = state["ps_free"]


def _mln_scr(kb):
    A = kb.A
    scr = ln_scr_alloc(A)
    scr["xblk"] = A.alloc(KC * LNW * 4)
    return scr


def _p_finish(kb, dbg):
    nc, block = kb.nc, kb.block
    cs = kb.sem("cs")

    @block.sync
    def _(q):
        for s in kb._sems.values():
            if s.n > 0 and s.h is not cs.h:
                q.wait_ge(s.h, s.n)
        for name, out_ap in kb.dbg_outs.items():
            t = kb.ds[name]
            q.dma_start(out=out_ap, in_=t.ap()).then_inc(cs.h, 16); cs.n += 16
        q.wait_ge(cs.h, cs.n)


def kernel(**inputs):
    xs = np.asarray(inputs["L0_attn_hidden_states"])
    shards = []
    for c in range(NCORES):
        xc = xs[c * EPC:(c + 1) * EPC]
        shards.append(np.ascontiguousarray(xc.transpose(0, 2, 1)))

    cpack_np = _cpack_host({k: np.asarray(v) for k, v in inputs.items()})
    pe_np = _host_pe_T()
    mask_np = _host_sparse_masks()

    nc = build_program()
    base = {"pe_T": pe_np, "cpack": cpack_np, "spmask": mask_np,
            "ident": np.eye(128, dtype=np.float32)}
    for k in list(inputs.keys()):
        if k == "L0_attn_hidden_states":
            continue
        base[k.replace("L0_", "")] = np.ascontiguousarray(np.asarray(inputs[k], dtype=np.float32))
    in_maps = [dict(base, x=shards[c]) for c in range(NCORES)]
    res = run_bass_kernel_spmd(nc, in_maps, core_ids=list(range(NCORES)))
    outs = []
    for c in range(NCORES):
        yc = res.results[c]["y"]
        outs.append(yc.transpose(0, 2, 1))
    return np.ascontiguousarray(np.concatenate(outs, axis=0))


# version 12
# speedup vs baseline: 1.3372x; 1.3372x over previous
import math
import contextlib
import numpy as np
import concourse.bass as bass
import concourse.mybir as mybir
from concourse.bass_utils import run_bass_kernel_spmd

F32 = mybir.dt.float32
BF16 = mybir.dt.bfloat16
AF = mybir.ActivationFunctionType
ALU = mybir.AluOpType
AX = mybir.AxisListType

B, S, D = 16, 2048, 1024
NH, DH = 16, 64
FFD = 4096
NE = 8
NCORES = 8
EPC = B // NCORES
KC = D // 128
FFC = FFD // 128
NTOK = EPC * S
TB = 512
NTB = NTOK // TB
QT = S // 128
CH = 256
NCH = S // CH
WIN = 768
EPS = 1e-5
NEG = -1.0e30

BD_W = 2304
BD_SLOTS = 4

ARENA_KB = 202
CONST_KB = 4


class Sem:

    def __init__(self, h):
        self.h = h
        self.n = 0

    def inc(self, instr, k=1):
        instr.then_inc(self.h, k)
        self.n += k
        return self.n


class Arena:

    def __init__(self, t, nbytes):
        self.t = t
        self.top = 0
        self.lim = nbytes
        self.marks = []

    def alloc(self, nbytes, align=64):
        off = (self.top + align - 1) // align * align
        self.top = off + nbytes
        assert self.top <= self.lim, f"arena overflow: {self.top} > {self.lim}"
        return off

    def mark(self):
        self.marks.append(self.top)

    def release(self):
        self.top = self.marks.pop()

    def f32(self, off, cols, parts=128, p0=0):
        return self.t[p0:p0 + parts, off:off + 4 * cols].bitcast(F32)

    def bf16(self, off, cols, parts=128, p0=0):
        return self.t[p0:p0 + parts, off:off + 2 * cols].bitcast(BF16)


class KB:

    def __init__(self, nc, es, block):
        self.nc = nc
        self.es = es
        self.block = block
        self._sems = {}

    def sem(self, name):
        if name not in self._sems:
            self._sems[name] = Sem(self.es.enter_context(self.nc.semaphore(name)))
        return self._sems[name]


def _host_pe_T():
    pos = np.arange(S - 1, -1, -1, dtype=np.float32)
    inv_freq = (1.0 / 10000.0 ** (np.arange(0, D, 2, dtype=np.float32) / D)).astype(np.float32)
    sinusoid = pos[:, None].astype(np.float64) * inv_freq[None, :].astype(np.float64)
    pe = np.concatenate([np.sin(sinusoid), np.cos(sinusoid)], axis=-1).astype(np.float32)
    return np.ascontiguousarray(pe.T)


def _host_sparse_masks():
    out = np.zeros((128, 2 * (512 + WIN + 512)), dtype=np.float32)
    i = np.arange(128)[:, None]
    for qt in range(2):
        base = qt * (512 + WIN + 512)
        j = np.arange(512)[None, :]
        qpos = qt * 128 + i
        valid = np.abs(qpos - j) <= 256
        out[:, base:base + 512] = np.where(valid, 0.0, NEG)
        j = np.arange(WIN)[None, :]
        qpos = 256 + qt * 128 + i
        valid = np.abs(qpos - j) <= 256
        out[:, base + 512:base + 512 + WIN] = np.where(valid, 0.0, NEG)
        j = np.arange(512)[None, :]
        qpos = 256 + qt * 128 + i
        valid = np.abs(qpos - j) <= 256
        out[:, base + 512 + WIN:base + 1024 + WIN] = np.where(valid, 0.0, NEG)
    return out


def _to_chunks(v):
    return np.ascontiguousarray(v.reshape(-1, 128).T)


def build_program(dbg=None):
    dbg = dbg or []
    nc = bass.Bass("TRN2", target_bir_lowering=False)

    x_in = nc.dram_tensor("x", [EPC, D, S], F32, kind="ExternalInput").ap()
    w = {}
    w_shapes = {
        "attn_wq": [D, D], "attn_wk": [D, D], "attn_wv": [D, D], "attn_wo": [D, D],
        "attn_bq": [D], "attn_bk": [D], "attn_bv": [D], "attn_bo": [D],
        "ffn_Wqkv": [D, 3 * D], "ffn_Wo": [D, D], "ffn_Wr": [D, D],
        "ffn_u_bias": [NH, DH], "ffn_v_bias": [NH, DH],
        "ffn_ln1_g": [D], "ffn_ln1_b": [D], "ffn_ln2_g": [D], "ffn_ln2_b": [D],
        "ffn_ln3_g": [D], "ffn_ln3_b": [D],
        "ffn_Wff1": [D, FFD], "ffn_bff1": [FFD], "ffn_Wff2": [FFD, D], "ffn_bff2": [D],
        "ffn_Wg": [D, NE], "ffn_We1": [NE, D, FFD], "ffn_be1": [NE, FFD],
        "ffn_We2": [NE, FFD, D], "ffn_be2": [NE, D], "ffn_Wout": [D, D],
    }
    for k, shp in w_shapes.items():
        w[k] = nc.dram_tensor(k, shp, F32, kind="ExternalInput").ap()
    pe_T = nc.dram_tensor("pe_T", [D, S], F32, kind="ExternalInput").ap()
    cpack = nc.dram_tensor("cpack", [128, 160], F32, kind="ExternalInput").ap()
    spmask = nc.dram_tensor("spmask", [128, 2 * (512 + WIN + 512)], F32, kind="ExternalInput").ap()
    ident_in = nc.dram_tensor("ident", [128, 128], F32, kind="ExternalInput").ap()
    x_out = nc.dram_tensor("y", [EPC, D, S], F32, kind="ExternalOutput").ap()

    ds = {}
    def dscr(name, shape, dt=F32):
        ds[name] = nc.dram_tensor(name, shape, dt, kind="Internal")
        return ds[name]

    dscr("xres_a", [EPC, D, S])
    dscr("xres_b", [EPC, D, S])
    dscr("ybuf", [EPC, D, S])
    dscr("h1f", [EPC, D, S])
    dscr("h2f", [EPC, D, S])
    dscr("h1b", [EPC, D, S], BF16)
    dscr("h2b", [EPC, D, S], BF16)
    dscr("rbuf", [D, S], BF16)
    dscr("gall", [NE, NTOK], BF16)
    dscr("moeacc", [D, NTOK])
    dscr("bdscr", [BD_SLOTS, 128, BD_W], BF16)

    dbg_outs = {}
    for name in dbg:
        if name in ds:
            dbg_outs[name] = nc.dram_tensor("dbg_" + name, list(ds[name].shape),
                                            ds[name].dtype, kind="ExternalOutput").ap()

    es = contextlib.ExitStack()
    with es:
        arena_t = es.enter_context(nc.sbuf_tensor("arena", [128, ARENA_KB * 1024], mybir.dt.uint8))
        const_t = es.enter_context(nc.sbuf_tensor("consts", [128, CONST_KB * 1024], mybir.dt.uint8))
        A = Arena(arena_t, ARENA_KB * 1024)
        C = Arena(const_t, CONST_KB * 1024)
        PS = [es.enter_context(nc.psum_tensor(f"ps{i}", [128, 512], F32)) for i in range(6)]
        PT = [es.enter_context(nc.psum_tensor(f"pt{i}", [128, 1024], BF16)) for i in range(2)]
        block = es.enter_context(nc.Block())
        kb = KB(nc, es, block)
        kb.A, kb.C, kb.PS, kb.PT = A, C, PS, PT
        kb.w, kb.ds, kb.dbg_outs = w, ds, dbg_outs
        kb.x_in, kb.x_out = x_in, x_out
        kb.pe_T, kb.cpack, kb.spmask = pe_T, cpack, spmask
        kb.ident_in = ident_in

        for name in ["cw", "ca", "cs", "cb",
                     "pe", "dv", "ac", "po",
                     "gp",
                     "bar"]:
            kb.sem(name)

        _emit_all(kb, dbg)

    _group_engines(nc)
    return nc


def _group_engines(nc):
    order = {}

    def key(ins):
        e = str(ins.engine)
        if e not in order:
            order[e] = len(order) + 1
        return 0 if e == "EngineType.Unassigned" else order[e]

    for f in nc.m.functions:
        for b in f.blocks:
            ins = list(b.instructions)
            ins.sort(key=key)
            b.instructions = ins


_CP_LAYOUT = [("ln1_g", 8), ("ln1_b", 8), ("ln2_g", 8), ("ln2_b", 8), ("ln3_g", 8),
              ("ln3_b", 8), ("u_b", 8), ("v_b", 8), ("bq", 8), ("bk", 8), ("bv", 8),
              ("bo", 8), ("bff1", 32), ("bff2", 8), ("cv", 8)]
_CONST_VALS = [EPS, 0.125, 0.5, 1.0, -1.0, 0.0, NEG, -0.5]
CP = {}
_o = 0
for _n, _w in _CP_LAYOUT:
    CP[_n] = _o
    _o += _w
assert _o <= 160


def _cpack_host(inp):
    out = np.zeros((128, 160), np.float32)
    src = {
        "ln1_g": inp["L0_ffn_ln1_g"], "ln1_b": inp["L0_ffn_ln1_b"],
        "ln2_g": inp["L0_ffn_ln2_g"], "ln2_b": inp["L0_ffn_ln2_b"],
        "ln3_g": inp["L0_ffn_ln3_g"], "ln3_b": inp["L0_ffn_ln3_b"],
        "u_b": np.asarray(inp["L0_ffn_u_bias"]).reshape(-1),
        "v_b": np.asarray(inp["L0_ffn_v_bias"]).reshape(-1),
        "bq": inp["L0_attn_bq"], "bk": inp["L0_attn_bk"], "bv": inp["L0_attn_bv"],
        "bo": inp["L0_attn_bo"], "bff1": inp["L0_ffn_bff1"], "bff2": inp["L0_ffn_bff2"],
    }
    for name, width in _CP_LAYOUT:
        if name == "cv":
            out[:, CP[name]:CP[name] + width] = np.asarray(_CONST_VALS, np.float32)[None, :]
        else:
            a = _to_chunks(np.asarray(src[name]))
            out[:, CP[name]:CP[name] + width] = a
    return out


_XL_DBG = None


def _emit_all(kb, dbg):
    kb.xl_first = True
    kb.xl_dbg = _XL_DBG
    kb.ln_ps_free = []
    kb.r_bank_free = [[] for _ in range(4)]
    kb.dbg_list = dbg
    A = kb.A
    _p_const(kb)
    _p_r(kb)

    def alltok():
        return [(kb.sem(n), kb.sem(n).n)
                for n in ["cw", "ca", "cs", "cb", "pe", "dv", "ac", "gp"]
                if kb.sem(n).n > 0]

    def ffn_call(x_src, x_dst, tag):
        nsc = kb.nc.named_scope
        for e in range(EPC):
            A.mark()
            ctx_off = A.alloc(KC * S * 2)
            A.mark()
            q_off = A.alloc(KC * S * 2)
            k_off = A.alloc(KC * S * 2)
            vt_off = A.alloc(16 * 1024 * 2)
            A.mark()
            y_off = A.alloc(KC * S * 2)
            A.mark()
            scr = _mln_scr(kb)
            with nsc(f"MLN_{tag}{e}", notify=True):
                _p_mln(kb, x_src[e], y_off, f32_dram=kb.ds["ybuf"].ap()[e], scr=scr,
                       ready0=alltok())
            mln_done = alltok()
            A.release()
            wbuf_off = A.alloc(2 * KC * 512 * 2)
            with nsc(f"QKV_{tag}{e}", notify=True):
                _p_qkv(kb, y_off, q_off, k_off, vt_off, wbuf_off, ready=mln_done)
            A.release()
            xscr = _xl_scr(kb)
            with nsc(f"XL_{tag}{e}", notify=True):
                _p_xl(kb, q_off, k_off, vt_off, ctx_off, xscr, ready=alltok())
                kb.xl_first = False
            if tag == "B" and e == 0:
                _dump(kb, "ctx", kb.A.bf16(ctx_off, KC * S), kb.xl_done)
            A.release()
            with nsc(f"WOH_{tag}{e}", notify=True):
                _p_woh(kb, e, ctx_off, ready=alltok())
            A.release()
        with nsc(f"FFG_{tag}", notify=True):
            _p_ffg(kb, ready=alltok())
        with nsc(f"GATE_{tag}", notify=True):
            _p_gate(kb, ready=alltok())
        with nsc(f"MOE_{tag}", notify=True):
            _p_moe(kb, ready=alltok())
        with nsc(f"OUT_{tag}", notify=True):
            _p_out(kb, x_src, x_dst, 0.5, ready=alltok())

    def attn_call(x_src, x_dst):
        A.mark()
        MW = 512 + WIN + 512
        mask_off = A.alloc(2 * MW * 2)
        ca = kb.sem("ca")

        @kb.block.gpsimd
        def _(q):
            for s, n in alltok():
                q.wait_ge(s.h, n)
            q.dma_start(out=A.bf16(mask_off, 2 * MW), in_=kb.spmask).then_inc(ca.h, 16)
            ca.n += 16
        mask_tok = [(ca, ca.n)]
        nsc = kb.nc.named_scope
        for e in range(EPC):
            A.mark()
            ctx_off = A.alloc(KC * S * 2)
            A.mark()
            q_off = A.alloc(KC * S * 2)
            k_off = A.alloc(KC * S * 2)
            vt_off = A.alloc(16 * 1024 * 2)
            A.mark()
            z_off = A.alloc(KC * S * 2)
            A.mark()
            scr = _mln_scr(kb)
            with nsc(f"SLN_{e}", notify=True):
                _p_mln(kb, x_src[e], z_off, f32_dram=None, scr=scr, ready0=alltok())
            mln_done = alltok()
            A.release()
            wbuf_off = A.alloc(2 * KC * 512 * 2)
            bvr_off = A.alloc(D * 2)
            with nsc(f"SQKV_{e}", notify=True):
                _p_sqkv(kb, z_off, q_off, k_off, vt_off, wbuf_off, bvr_off, ready=mln_done)
            A.release()
            spscr = _sp_scr(kb)
            with nsc(f"SATT_{e}", notify=True):
                _p_satt(kb, q_off, k_off, vt_off, ctx_off, mask_off, spscr,
                        ready=alltok() + mask_tok)
            A.release()
            with nsc(f"SWO_{e}", notify=True):
                _p_swo(kb, e, ctx_off, x_src, x_dst, ready=alltok())
            A.release()
        A.release()

    xa = kb.ds["xres_a"].ap()
    xb = kb.ds["xres_b"].ap()
    ffn_call(kb.x_in, xa, "B")
    attn_call(xa, xb)
    ffn_call(xb, kb.x_out, "F")
    _p_finish(kb, dbg)


def _p_const(kb):
    C, block = kb.C, kb.block
    cw, dv = kb.sem("cw"), kb.sem("dv")

    kb.cp_off = C.alloc(160 * 4)
    kb.ident_off = C.alloc(128 * 2)
    kb.ones_col_off = C.alloc(2 * 2)
    kb.ones_row_bf_off = C.alloc(128 * 2)
    kb.ones_row_f_off = C.alloc(128 * 4)
    kb.identf_off = C.alloc(128 * 4)
    kb.neg_off = C.alloc(256 * 4)
    kb.neg_bf_off = C.alloc(256 * 2)

    @block.sync
    def _(q):
        q.dma_start(out=C.f32(kb.cp_off, 160), in_=kb.cpack).then_inc(cw.h, 16); cw.n += 16
        q.dma_start(out=C.f32(kb.identf_off, 128), in_=kb.ident_in).then_inc(cw.h, 16); cw.n += 16

    @block.gpsimd
    def _(q):
        q.dma_start(out=C.bf16(kb.ident_off, 128), in_=kb.ident_in).then_inc(cw.h, 16); cw.n += 16

    @block.vector
    def _(v):
        v.memset(C.bf16(kb.ones_col_off, 2), 1.0)
        v.memset(C.bf16(kb.ones_row_bf_off, 128, parts=1), 1.0)
        v.memset(C.f32(kb.ones_row_f_off, 128, parts=1), 1.0)
        v.memset(C.f32(kb.neg_off, 256), NEG)
        v.memset(C.bf16(kb.neg_bf_off, 256), NEG).then_inc(dv.h, 1); dv.n += 1

    cs = kb.sem("cs")

    @block.sync
    def _(q):
        q.wait_ge(dv.h, dv.n)
        for sl in range(BD_SLOTS):
            q.dma_start(out=kb.ds["bdscr"].ap()[sl, :, S:BD_W],
                        in_=C.bf16(kb.neg_bf_off, BD_W - S)).then_inc(cs.h, 16)
            cs.n += 16
    kb.bdtail_tok = (cs, cs.n)

    for i, val in enumerate(_CONST_VALS):
        ap = C.f32(kb.cp_off + 4 * (CP["cv"] + i), 1)
        kb.nc.const_aps.register(F32, float(val), ap)
    kb.const_tok = [(cw, cw.n), (dv, dv.n)]


LNW = 512


def ln_scr_alloc(A):
    return {
        "xbf": A.alloc(KC * LNW * 2),
        "rows": A.alloc(3 * LNW * 4),
        "tmp": A.alloc(2 * LNW * 4),
    }


def _ln_block(kb, x_blk, scr, y_put, gamma=None, beta=None, ready=(), state=None):
    A, C, block, PS = kb.A, kb.C, kb.block, kb.PS
    nc = kb.nc
    W = LNW
    pe, dv, ac = kb.sem("pe"), kb.sem("dv"), kb.sem("ac")
    ps_s, ps_q = PS[4], PS[5]
    if state is None:
        state = {}
    ps_free = state.get("ps_free", [])
    tmp_busy = state.get("tmp_busy", {0: [], 1: []})

    xbf = A.bf16(scr["xbf"], KC * W)
    xsq = xbf
    rows = scr["rows"]
    r_s = A.f32(rows, W, parts=1)
    r_q = A.f32(rows + 4 * W, W, parts=1)
    r_t = A.f32(rows + 8 * W, W, parts=1)
    tmp_ap = lambda par: A.f32(scr["tmp"] + par * 4 * W, W)
    ones_col = C.bf16(kb.ones_col_off, 2)
    ones_row_f = C.f32(kb.ones_row_f_off, 128, parts=1)

    def wt(eng, toks):
        for s, n in toks:
            eng.wait_ge(s.h, n)

    xbf_free = state.get("xbf_free", [])

    @block.vector
    def _(v):
        wt(v, ready)
        wt(v, xbf_free)
        for c in range(KC):
            ins = v.tensor_copy(out=xbf[:, c * W:(c + 1) * W], in_=x_blk[:, c * W:(c + 1) * W])
        dv.inc(ins)
    dv_cast = dv.n

    @block.tensor
    def _(t):
        t.wait_ge(dv.h, dv_cast)
        wt(t, ps_free)
        for c in range(KC):
            ins = t.matmul(ps_s[0:1, 0:W], ones_col[:, 0:1], xbf[:, c * W:(c + 1) * W],
                           start=(c == 0), stop=(c == KC - 1))
        pe.inc(ins)
    pe_sums = pe.n

    @block.vector
    def _(v):
        wt(v, ready)
        v.wait_ge(pe.h, pe_sums)
        for c in range(KC):
            ins = v.tensor_mul(xsq[:, c * W:(c + 1) * W], x_blk[:, c * W:(c + 1) * W],
                               x_blk[:, c * W:(c + 1) * W])
        dv.inc(ins)
    dv_sq = dv.n

    @block.tensor
    def _(t):
        t.wait_ge(dv.h, dv_sq)
        for c in range(KC):
            ins = t.matmul(ps_q[0:1, 0:W], ones_col[:, 0:1], xsq[:, c * W:(c + 1) * W],
                           start=(c == 0), stop=(c == KC - 1))
        pe.inc(ins)
    pe_stats = pe.n
    state["xbf_free"] = [(pe, pe.n)]

    @block.vector
    def _(v):
        v.wait_ge(pe.h, pe_stats)
        v.tensor_copy(out=r_s, in_=ps_s[0:1, 0:W])
        v.tensor_copy(out=r_q, in_=ps_q[0:1, 0:W])
        v.tensor_scalar_mul(r_s, r_s, -1.0 / D)
        v.tensor_scalar_mul(r_q, r_q, 1.0 / D)
        v.tensor_mul(r_t, r_s, r_s)
        dv.inc(v.tensor_sub(r_q, r_q, r_t))
    dv_var = dv.n

    @block.scalar
    def _(s):
        s.wait_ge(dv.h, dv_var)
        s.activation(out=r_q, in_=r_q, func=AF.Ln, bias=EPS, scale=1.0)
        ac.inc(s.activation(out=r_q, in_=r_q, func=AF.Exp, bias=0.0, scale=-0.5))
    ac_rstd = ac.n

    @block.tensor
    def _(t):
        t.wait_ge(ac.h, ac_rstd)
        t.matmul(ps_s[:, 0:W], ones_row_f, r_s, start=True, stop=True)
        pe.inc(t.matmul(ps_q[:, 0:W], ones_row_f, r_q, start=True, stop=True))
    pe_bc = pe.n

    for c in range(KC):
        par = c % 2
        @block.vector
        def _(v, c=c, par=par):
            if c == 0:
                v.wait_ge(pe.h, pe_bc)
            wt(v, tmp_busy.get(par, []))
            tmp = tmp_ap(par)
            v.tensor_add(tmp, x_blk[:, c * W:(c + 1) * W], ps_s[:, 0:W])
            ins = v.tensor_mul(tmp, tmp, ps_q[:, 0:W])
            if gamma is not None:
                g_ap = C.f32(kb.cp_off + 4 * (gamma + c), 1)
                b_ap = C.f32(kb.cp_off + 4 * (beta + c), 1)
                ins = v.tensor_scalar(tmp, tmp, g_ap, b_ap, ALU.mult, ALU.add)
            dv.inc(ins)
        toks = y_put(c, tmp_ap(par), (dv, dv.n))
        tmp_busy[par] = list(toks)
    state["ps_free"] = [(dv, dv.n)]
    state["tmp_busy"] = tmp_busy
    return state


def _dump(kb, name, ap, waits):
    if name not in kb.dbg_list:
        return None
    cs = kb.sem("cs")
    shp = list(ap.shape)
    out = kb.nc.dram_tensor("dbg_" + name, shp, ap.dtype, kind="ExternalOutput").ap()

    @kb.block.sync
    def _(q):
        for s, n in waits:
            q.wait_ge(s.h, n)
        q.dma_start(out=out, in_=ap).then_inc(cs.h, 16)
        cs.n += 16
    return (cs, cs.n)


def _p_qkv(kb, y_off, q_off, k_off, vt_off, wbuf_off, ready=()):
    A, C, block, nc, PS = kb.A, kb.C, kb.block, kb.nc, kb.PS
    cw, pe, ac, dv = kb.sem("cw"), kb.sem("pe"), kb.sem("ac"), kb.sem("dv")
    Wqkv = kb.w["ffn_Wqkv"]

    @block.tensor
    def _(t):
        for s, n in ready:
            t.wait_ge(s.h, n)

    @block.gpsimd
    def _(q):
        for s, n in ready:
            q.wait_ge(s.h, n)

    wb = lambda par: A.bf16(wbuf_off + par * KC * 512 * 2, KC * 512)
    y_ap = lambda ksl: A.bf16(y_off + 2 * ksl * S, S)
    wload = {}
    wfree = [[], []]
    bank_free = [[] for _ in range(4)]

    def emit_load(cb):
        par = cb % 2

        @block.gpsimd
        def _(q):
            for s, n in wfree[par]:
                q.wait_ge(s.h, n)
            src = Wqkv[:, cb * 512:(cb + 1) * 512].rearrange("(k p) c -> p k c", p=128)
            dst = wb(par).rearrange("p (k c) -> p k c", k=KC)
            q.dma_start(out=dst, in_=src).then_inc(cw.h, 16)
            cw.n += 16
        wload[cb] = (cw, cw.n)

    emit_load(0)
    emit_load(1)
    nbank = 0
    for cb in range(6):
        par = cb % 2
        if cb < 4:
            dst_off = q_off if cb < 2 else k_off
            fbase = (cb % 2) * 4
            for m in range(4):
                f = fbase + m
                for n in range(4):
                    bank = nbank % 4
                    nbank += 1

                    @block.tensor
                    def _(t, cb=cb, par=par, m=m, n=n, bank=bank):
                        t.wait_ge(cw.h, wload[cb][1])
                        for s, nn in bank_free[bank]:
                            t.wait_ge(s.h, nn)
                        for k in range(KC):
                            ins = t.matmul(PS[bank][:, 0:512],
                                           wb(par)[:, k * 512 + m * 128:k * 512 + (m + 1) * 128],
                                           y_ap(k)[:, n * 512:(n + 1) * 512],
                                           start=(k == 0), stop=(k == KC - 1))
                        pe.inc(ins)
                    pe_t = pe.n

                    @block.vector
                    def _(v, f=f, n=n, bank=bank, pe_t=pe_t, dst_off=dst_off):
                        v.wait_ge(pe.h, pe_t)
                        dst = kb.A.bf16(dst_off + 2 * (f * S + n * 512), 512)
                        dv.inc(v.tensor_copy(out=dst, in_=PS[bank][:, 0:512]))
                    bank_free[bank] = [(dv, dv.n)]
            wfree[par] = [(pe, pe.n)]
            if cb + 2 < 6:
                emit_load(cb + 2)
        else:
            vb = cb - 4
            for tt in range(16):
                bank = nbank % 4
                nbank += 1

                @block.tensor
                def _(t, cb=cb, par=par, tt=tt, bank=bank):
                    t.wait_ge(cw.h, wload[cb][1])
                    for s, nn in bank_free[bank]:
                        t.wait_ge(s.h, nn)
                    for k in range(KC):
                        ins = t.matmul(PS[bank][:, 0:512],
                                       y_ap(k)[:, tt * 128:(tt + 1) * 128],
                                       wb(par)[:, k * 512:(k + 1) * 512],
                                       start=(k == 0), stop=(k == KC - 1))
                    pe.inc(ins)
                pe_t = pe.n

                @block.vector
                def _(v, tt=tt, vb=vb, bank=bank, pe_t=pe_t):
                    v.wait_ge(pe.h, pe_t)
                    dst = kb.A.bf16(vt_off + 2 * (tt * 1024 + vb * 512), 512)
                    dv.inc(v.tensor_copy(out=dst, in_=PS[bank][:, 0:512]))
                bank_free[bank] = [(dv, dv.n)]
            wfree[par] = [(pe, pe.n)]
            if cb + 2 < 6:
                emit_load(cb + 2)

    kb.qkv_done = [(ac, ac.n)]


def _p_r(kb):
    A, C, block, nc, PS = kb.A, kb.C, kb.block, kb.nc, kb.PS
    cw, ca, cs, pe, ac, dv = (kb.sem(n) for n in ["cw", "ca", "cs", "pe", "ac", "dv"])
    A.mark()
    peT_off = A.alloc(KC * S * 2)
    wbuf_off = A.alloc(2 * KC * 512 * 2)
    rst_off = A.alloc(2 * 512 * 2)
    wb = lambda par: A.bf16(wbuf_off + par * KC * 512 * 2, KC * 512)
    peT = lambda k: A.bf16(peT_off + 2 * k * S, S)

    @block.gpsimd
    def _(q):
        for k in range(KC):
            q.dma_start(out=peT(k), in_=kb.pe_T[k * 128:(k + 1) * 128, :]).then_inc(ca.h, 16)
            ca.n += 16
        for cb in range(2):
            src = kb.w["ffn_Wr"][:, cb * 512:(cb + 1) * 512].rearrange("(k p) c -> p k c", p=128)
            q.dma_start(out=wb(cb).rearrange("p (k c) -> p k c", k=KC), in_=src).then_inc(cw.h, 16)
            cw.n += 16
    loads = (ca.n, cw.n)
    st_free = [[], []]

    for cb in range(2):
        for m in range(4):
            f = cb * 4 + m
            for n in range(4):
                bank = (f * 4 + n) % 4

                @block.tensor
                def _(t, cb=cb, m=m, n=n, bank=bank):
                    t.wait_ge(ca.h, loads[0])
                    t.wait_ge(cw.h, loads[1])
                    for s, nn in kb.r_bank_free[bank]:
                        t.wait_ge(s.h, nn)
                    for k in range(KC):
                        ins = t.matmul(PS[bank][:, 0:512],
                                       wb(cb)[:, k * 512 + m * 128:k * 512 + (m + 1) * 128],
                                       peT(k)[:, n * 512:(n + 1) * 512],
                                       start=(k == 0), stop=(k == KC - 1))
                    pe.inc(ins)
                pe_t = pe.n
                par = n % 2

                @block.vector
                def _(v, bank=bank, par=par, pe_t=pe_t):
                    v.wait_ge(pe.h, pe_t)
                    for s, nn in st_free[par]:
                        v.wait_ge(s.h, nn)
                    dv.inc(v.tensor_copy(out=A.bf16(rst_off + par * 512 * 2, 512),
                                         in_=PS[bank][:, 0:512]))
                kb.r_bank_free[bank] = [(dv, dv.n)]
                ac_t = dv.n

                @block.sync
                def _(q, f=f, n=n, par=par, ac_t=ac_t):
                    q.wait_ge(dv.h, ac_t)
                    q.dma_start(out=kb.ds["rbuf"].ap()[f * 128:(f + 1) * 128, n * 512:(n + 1) * 512],
                                in_=A.bf16(rst_off + par * 512 * 2, 512)).then_inc(cs.h, 16)
                    cs.n += 16
                st_free[par] = [(cs, cs.n)]
    A.release()
    kb.r_done = [(cs, cs.n)]


def _xl_scr(kb):
    A = kb.A
    return {
        "u2h": A.alloc(S * 2), "v2h": A.alloc(S * 2), "rh": A.alloc(S * 2),
        "sc": A.alloc(3 * S * 4), "bdst": A.alloc(2 * S * 2),
        "p": A.alloc(3 * S * 2), "pT": A.alloc(4 * 1024 * 2),
        "cst": A.alloc(2 * S * 2), "nm": A.alloc(32 * 4),
    }


def _p_xl(kb, q_off, k_off, vt_off, ctx_off, scr, ready):
    A, C, block, nc, PS, PT = kb.A, kb.C, kb.block, kb.nc, kb.PS, kb.PT
    pe, dv, ac = kb.sem("pe"), kb.sem("dv"), kb.sem("ac")
    ca, cs, cb, gp = kb.sem("ca"), kb.sem("cs"), kb.sem("cb"), kb.sem("gp")
    bdscr = kb.ds["bdscr"]
    rbuf = kb.ds["rbuf"]
    ident = C.bf16(kb.ident_off, 128)

    u2h = lambda bb: A.bf16(scr["u2h"], S, parts=64, p0=bb)
    v2h = lambda bb: A.bf16(scr["v2h"], S, parts=64, p0=bb)
    rh = lambda bb: A.bf16(scr["rh"], S, parts=64, p0=bb)
    sc_ap = lambda sl, o, w: A.f32(scr["sc"] + sl * S * 4 + o * 4, w)
    bdst_ap = lambda par, o, w: A.bf16(scr["bdst"] + par * S * 2 + o * 2, w)
    p_ap = lambda par, o, w: A.bf16(scr["p"] + par * S * 2 + o * 2, w)
    pT_ap = lambda sl, w: A.bf16(scr["pT"] + sl * 1024 * 2, w)
    cst = lambda hp: A.bf16(scr["cst"] + hp * S * 2, S, parts=64)
    nm_ap = lambda i, w=1: A.f32(scr["nm"] + i * 4, w)

    NU = NH * QT
    first = {"t": True, "v": True, "s": True, "g": True, "sq": True}

    def wt(eng, toks, key=None):
        if key is not None and first[key]:
            first[key] = False
            toks = list(toks) + list(ready)
        for s, n in toks:
            eng.wait_ge(s.h, n)

    bd_bank = [[], []]
    ac_bank = [[], []]
    bdst_st = [[], []]
    slot_st = [[] for _ in range(BD_SLOTS)]
    p_st = [[], [], []]
    sc_st = [[], [], []]
    PTb_st = [[], []]
    pT_st = [[], [], [], []]
    ctx_col = [[], []]
    head_done_pe = {}
    cst_tok = {}
    prep_tok = {}
    U = [dict() for _ in range(NU)]
    gidx = 0

    def jblocks(kt):
        out, o = [], S - kt
        while o < S:
            w = min(512, S - o)
            out.append((o, w))
            o += w
        return out

    def front(u):
        h, t = divmod(u, QT)
        bb = (h % 2) * 64
        kt = 128 * (t + 1)
        st = U[u]
        st.update(h=h, t=t, bb=bb, kt=kt, par=u % 3, scs=u % 3, par2=u % 2,
                  slot=u % BD_SLOTS, hc=h // 2)
        if t == 0:
            gate = list(head_done_pe.get(h - 2, []))

            @block.vector
            def _(v, h=h, bb=bb, gate=gate):
                wt(v, gate, "v")
                hc = h // 2
                qsl = A.bf16(q_off + 2 * (hc * S), S, parts=64, p0=bb)
                ub = C.f32(kb.cp_off + 4 * (CP["u_b"] + hc), 1, parts=64, p0=bb)
                vb = C.f32(kb.cp_off + 4 * (CP["v_b"] + hc), 1, parts=64, p0=bb)
                v.tensor_scalar(u2h(bb), qsl, ub, None, ALU.add)
                dv.inc(v.tensor_scalar(v2h(bb), qsl, vb, None, ALU.add))
            uv_tok = (dv, dv.n)

            @block.scalar
            def _(s, h=h, bb=bb, gate=gate):
                wt(s, gate, "s")
                s.dma_start(out=rh(bb), in_=rbuf.ap()[h * DH:(h + 1) * DH, :]).then_inc(ca.h, 16)
                ca.n += 16
            prep_tok[h] = [uv_tok, (ca, ca.n)]

        jb = jblocks(kt)
        evac_toks = []
        for bi, (jo, wb_) in enumerate(jb):
            bank = bi % 2

            @block.tensor
            def _(tq, u=u, t=t, bb=bb, jo=jo, wb_=wb_, bank=bank, bi=bi, h=h):
                toks = list(bd_bank[bank])
                if bi == 0:
                    toks += prep_tok[h]
                wt(tq, toks, "t")
                pe.inc(tq.matmul(PS[bank][:, 0:wb_],
                                 v2h(bb)[:, t * 128:(t + 1) * 128],
                                 rh(bb)[:, jo:jo + wb_], start=True, stop=True))
            pe_t = pe.n

            @block.vector
            def _(v, jo=jo, wb_=wb_, bank=bank, pe_t=pe_t, kt=kt, par2=u % 2, bi=bi):
                toks = [(pe, pe_t)]
                if bi == 0:
                    toks += bdst_st[par2]
                wt(v, toks, "v")
                dv.inc(v.tensor_copy(out=bdst_ap(par2, jo - (S - kt), wb_),
                                     in_=PS[bank][:, 0:wb_]))
            bd_bank[bank] = [(dv, dv.n)]
        st["bdev"] = (dv, dv.n)

        @block.sync
        def _(q, u=u, kt=kt, par2=u % 2, slot=u % BD_SLOTS):
            wt(q, [U[u]["bdev"]] + slot_st[slot], "sq")
            q.dma_start(out=bdscr.ap()[slot, :, S - kt:S],
                        in_=bdst_ap(par2, 0, kt)).then_inc(cs.h, 16)
            cs.n += 16
        bdst_st[u % 2] = [(cs, cs.n)]
        st["bdwr"] = (cs, cs.n)

    def mid(u):
        st = U[u]
        kt, par, slot = st["kt"], st["par"], st["slot"]

        @block.gpsimd
        def _(g, u=u, kt=kt, par=par, slot=slot):
            toks = [U[u]["bdwr"]] + p_st[par]
            if kb.xl_first and u < BD_SLOTS:
                toks = toks + [kb.bdtail_tok]
            wt(g, toks, "g")
            src = bass.AP(bdscr, slot * 128 * BD_W + (S - 1 - 128 * st["t"]),
                          [[BD_W - 1, 128], [1, kt]])
            g.dma_start(out=p_ap(par, 0, kt), in_=src).then_inc(cb.h, 16)
            cb.n += 16
        st["bdrd"] = (cb, cb.n)

    def back1(u):
        st = U[u]
        h, t, bb, kt, par, scs = st["h"], st["t"], st["bb"], st["kt"], st["par"], st["scs"]
        hc = st["hc"]
        nblk = (kt + 511) // 512
        ttr_done = []
        for bi in range(nblk):
            ko = bi * 512
            wb_ = min(512, kt - ko)
            bank = bi % 2

            @block.tensor
            def _(tq, t=t, ko=ko, wb_=wb_, bank=bank, bb=bb, hc=hc):
                wt(tq, ac_bank[bank], "t")
                pe.inc(tq.matmul(PS[2 + bank][:, 0:wb_],
                                 u2h(bb)[:, t * 128:(t + 1) * 128],
                                 A.bf16(k_off, KC * S, parts=64, p0=bb)[:, hc * S + ko:hc * S + ko + wb_],
                                 start=True, stop=True))
            pe_t = pe.n

            @block.vector
            def _(v, u=u, bi=bi, ko=ko, wb_=wb_, bank=bank, pe_t=pe_t, par=par, scs=scs):
                toks = [(pe, pe_t)]
                if bi == 0:
                    toks = toks + [U[u]["bdrd"]] + sc_st[scs]
                wt(v, toks, "v")
                dv.inc(v.tensor_tensor(out=sc_ap(scs, ko, wb_), in0=PS[2 + bank][:, 0:wb_],
                                       in1=p_ap(par, ko, wb_), op=ALU.add))
            ac_bank[bank] = [(dv, dv.n)]
            ttr_done.append((dv, dv.n))
        st["ttr"] = (dv, dv.n)
        if t == QT - 1:
            head_done_pe[h] = [(pe, pe.n)]

        @block.vector
        def _(v, par=par, scs=scs, kt=kt):
            dv.inc(v.reduce_max(nm_ap(18 + par), sc_ap(scs, 0, kt), axis=AX.X))
            v.wait_ge(dv.h, dv.n)
            dv.inc(v.tensor_scalar_mul(nm_ap(21 + par), nm_ap(18 + par), -0.125))
        st["negm"] = (dv, dv.n)

        @block.scalar
        def _(sq, u=u, par=par, scs=scs, kt=kt):
            wt(sq, [U[u]["negm"]], "s")
            ac.inc(sq.activation(out=p_ap(par, 0, kt), in_=sc_ap(scs, 0, kt), func=AF.Exp,
                                 bias=nm_ap(21 + par), scale=0.125,
                                 accum_out=nm_ap(24 + par)))
        sc_st[scs] = [(ac, ac.n)]
        st["exp"] = (ac, ac.n)

    def back2(u):
        nonlocal gidx
        st = U[u]
        h, t, bb, kt, par = st["h"], st["t"], st["bb"], st["kt"], st["par"]
        hc, hp = st["hc"], h % 2

        @block.vector
        def _(v, u=u, par=par, kt=kt):
            v.wait_ge(ac.h, U[u]["exp"][1])
            dv.inc(v.reciprocal(nm_ap(27 + par), nm_ap(24 + par)))
            v.wait_ge(dv.h, dv.n)
            dv.inc(v.tensor_scalar(p_ap(par, 0, kt), p_ap(par, 0, kt), nm_ap(27 + par),
                                   None, ALU.mult))
        psc_tok = (dv, dv.n)

        ngrp = (t + 1 + 7) // 8
        grp_sl = []
        for g_ in range(ngrp):
            n_in_g = min(8, t + 1 - g_ * 8)
            bankb = gidx % 2
            sl = gidx % 4
            gidx += 1

            @block.tensor
            def _(tq, g_=g_, n_in_g=n_in_g, bankb=bankb, par=par, psc_tok=psc_tok):
                wt(tq, [psc_tok] + PTb_st[bankb], "t")
                for i in range(n_in_g):
                    kbk = g_ * 8 + i
                    ins = tq.transpose(PT[bankb][:, i * 128:(i + 1) * 128],
                                       p_ap(par, kbk * 128, 128), ident)
                pe.inc(ins)
            pe_tr = pe.n

            @block.vector
            def _(v, sl=sl, n_in_g=n_in_g, bankb=bankb, pe_tr=pe_tr):
                wt(v, [(pe, pe_tr)] + pT_st[sl])
                dv.inc(v.tensor_copy(out=pT_ap(sl, n_in_g * 128),
                                     in_=PT[bankb][:, 0:n_in_g * 128]))
            PTb_st[bankb] = [(dv, dv.n)]
            grp_sl.append((sl, (dv, dv.n)))
        p_st[par] = [(pe, pe.n)]

        chalf = u % 2
        c0 = chalf * 128

        @block.tensor
        def _(tq, t=t, h=h, grp_sl=grp_sl, c0=c0, chalf=chalf):
            for sl, tok in grp_sl:
                tq.wait_ge(tok[0].h, tok[1])
            wt(tq, ctx_col[chalf])
            for kbk in range(t + 1):
                sl = grp_sl[kbk // 8][0]
                ins = tq.matmul(PS[4][0:64, c0:c0 + 128],
                                A.bf16(vt_off + 2 * (kbk * 1024 + h * DH), DH),
                                pT_ap(sl, 1024)[:, (kbk % 8) * 128:(kbk % 8 + 1) * 128],
                                start=(kbk == 0), stop=(kbk == t))
            pe.inc(ins)
        pe_ctx = pe.n
        for kbk in range(t + 1):
            sl = grp_sl[kbk // 8][0]
            pT_st[sl] = [(pe, pe.n)]
        p_st[par] = [(pe, pe.n)]

        @block.vector
        def _(v, t=t, hp=hp, pe_ctx=pe_ctx, c0=c0, h=h):
            toks = [(pe, pe_ctx)]
            if t == 0:
                toks = toks + list(cst_tok.get(h - 2, []))
            wt(v, toks)
            dv.inc(v.tensor_copy(out=cst(hp)[:, t * 128:(t + 1) * 128],
                                 in_=PS[4][0:64, c0:c0 + 128]))
        ctx_col[chalf] = [(dv, dv.n)]

        if t == QT - 1:
            dv_last = dv.n

            @block.scalar
            def _(s, h=h, bb=bb, hc=hc, hp=hp, dv_last=dv_last):
                s.wait_ge(dv.h, dv_last)
                dst = A.bf16(ctx_off + 2 * (hc * S), S, parts=64, p0=bb)
                s.dma_start(out=dst, in_=cst(hp)[:, :]).then_inc(cs.h, 16)
                cs.n += 16
            cst_tok[h] = [(cs, cs.n)]

    for it in range(NU + 3):
        if 3 <= it < NU + 3:
            back2(it - 3)
        if 2 <= it < NU + 2:
            back1(it - 2)
        if 1 <= it < NU + 1:
            mid(it - 1)
        if it < NU:
            front(it)
        if 1 <= it < NU + 1:
            uu = it - 1
            slot_st[uu % BD_SLOTS] = [U[uu]["bdrd"]]

    kb.xl_done = [(cs, cs.n), (dv, dv.n)]


def _p_woh(kb, elem, ctx_off, ready):
    A, C, block, nc, PS = kb.A, kb.C, kb.block, kb.nc, kb.PS
    cw, ca, cs, pe, dv, ac = (kb.sem(n) for n in ["cw", "ca", "cs", "pe", "dv", "ac"])
    A.mark()
    wo_off = A.alloc(KC * D * 2)
    y32_off = A.alloc(KC * 512 * 4)
    s2_off = A.alloc(KC * 512 * 4)
    h1c_off = A.alloc(2 * 512 * 2)
    scr = ln_scr_alloc(A)
    wo = A.bf16(wo_off, KC * D)
    y32 = A.f32(y32_off, KC * 512)
    s2 = A.f32(s2_off, KC * 512)

    @block.gpsimd
    def _(q):
        for s, n in ready:
            q.wait_ge(s.h, n)
        src = kb.w["ffn_Wo"].rearrange("(k p) c -> p k c", p=128)
        q.dma_start(out=wo.rearrange("p (k c) -> p k c", k=KC), in_=src).then_inc(cw.h, 16)
        cw.n += 16
    wo_tok = (cw, cw.n)

    state = {"ps_free": list(ready)}
    bank_free = [[] for _ in range(4)]
    y32_free = []
    s2_free = []
    h1c_busy = [[], []]
    for tb in range(S // 512):
        @block.sync
        def _(q, tb=tb, elem=elem):
            for s, n in y32_free:
                q.wait_ge(s.h, n)
            src = kb.ds["ybuf"].ap()[elem][:, tb * 512:(tb + 1) * 512].rearrange(
                "(c p) w -> p c w", p=128)
            q.dma_start(out=y32.rearrange("p (c w) -> p c w", c=KC), in_=src).then_inc(ca.h, 16)
            ca.n += 16
        y_tok = (ca, ca.n)

        dc_toks = []
        for dc in range(KC):
            bank = dc % 4

            @block.tensor
            def _(t, tb=tb, dc=dc, bank=bank):
                if tb == 0 and dc == 0:
                    t.wait_ge(cw.h, wo_tok[1])
                    for s, n in ready:
                        t.wait_ge(s.h, n)
                for s, n in bank_free[bank]:
                    t.wait_ge(s.h, n)
                for k in range(KC):
                    ins = t.matmul(PS[bank][:, 0:512],
                                   wo[:, k * D + dc * 128:k * D + (dc + 1) * 128],
                                   A.bf16(ctx_off + 2 * (k * S + tb * 512), 512),
                                   start=(k == 0), stop=(k == KC - 1))
                pe.inc(ins)
            pe_t = pe.n

            @block.vector
            def _(v, dc=dc, bank=bank, pe_t=pe_t, y_tok=y_tok):
                v.wait_ge(pe.h, pe_t)
                v.wait_ge(ca.h, y_tok[1])
                for s, n in s2_free:
                    v.wait_ge(s.h, n)
                dv.inc(v.tensor_add(s2[:, dc * 512:(dc + 1) * 512],
                                    PS[bank][:, 0:512], y32[:, dc * 512:(dc + 1) * 512]))
            bank_free[bank] = [(dv, dv.n)]
            dc_toks.append((dv, dv.n))
        y32_free = [(dv, dv.n)]
        s2_free = []

        def y_put(c, tmp, tok, tb=tb):
            h1c = A.bf16(h1c_off + (c % 2) * 512 * 2, 512)
            toks = []

            @block.vector
            def _(v, c=c):
                for s, n in h1c_busy[c % 2]:
                    v.wait_ge(s.h, n)
                dv.inc(v.tensor_copy(out=h1c, in_=tmp))
            bf_tok = (dv, dv.n)

            @block.sync
            def _(q, c=c, bf_tok=bf_tok, tok=tok):
                q.wait_ge(tok[0].h, tok[1])
                q.dma_start(out=kb.ds["h1f"].ap()[elem, c * 128:(c + 1) * 128,
                                                  tb * 512:(tb + 1) * 512],
                            in_=tmp).then_inc(cs.h, 16)
                cs.n += 16
                toks.append((cs, cs.n))
                q.wait_ge(bf_tok[0].h, bf_tok[1])
                q.dma_start(out=kb.ds["h1b"].ap()[elem, c * 128:(c + 1) * 128,
                                                  tb * 512:(tb + 1) * 512],
                            in_=h1c).then_inc(cs.h, 16)
                cs.n += 16
            h1c_busy[c % 2] = [(cs, cs.n)]
            return toks

        state = _ln_block(kb, s2, scr, y_put, gamma=CP["ln1_g"], beta=CP["ln1_b"],
                          ready=dc_toks, state=state)
        s2_free = [(dv, kb.sem("dv").n), (ac, kb.sem("ac").n), (pe, kb.sem("pe").n)]
    A.release()
    kb.ln_ps_free = state["ps_free"]
    kb.woh_done = [(cs, cs.n)]


def _p_ffg(kb, ready):
    A, C, block, nc, PS = kb.A, kb.C, kb.block, kb.nc, kb.PS
    cw, ca, cs, pe, dv, ac = (kb.sem(n) for n in ["cw", "ca", "cs", "pe", "dv", "ac"])
    A.mark()
    w1_off = A.alloc(KC * FFD * 2)
    w2_off = A.alloc(FFC * D * 2)
    hid_off = A.alloc(FFC * 512 * 2)
    h1b_off = A.alloc(KC * 512 * 2)
    h132_off = A.alloc(2 * 512 * 4)
    s2_off = A.alloc(KC * 512 * 4)
    h2c_off = A.alloc(2 * 512 * 2)
    scr = {"xbf": hid_off, "rows": A.alloc(3 * LNW * 4), "tmp": A.alloc(2 * LNW * 4)}
    w1 = A.bf16(w1_off, KC * FFD)
    w2 = A.bf16(w2_off, FFC * D)
    hid = A.bf16(hid_off, FFC * 512)
    h1btb = A.bf16(h1b_off, KC * 512)
    s2 = A.f32(s2_off, KC * 512)

    @block.gpsimd
    def _(q):
        for s, n in ready:
            q.wait_ge(s.h, n)
        q.dma_start(out=w1.rearrange("p (k c) -> p k c", k=KC),
                    in_=kb.w["ffn_Wff1"].rearrange("(k p) c -> p k c", p=128)).then_inc(cw.h, 16)
        cw.n += 16
        q.dma_start(out=w2.rearrange("p (f c) -> p f c", f=FFC),
                    in_=kb.w["ffn_Wff2"].rearrange("(f p) c -> p f c", p=128)).then_inc(cw.h, 16)
        cw.n += 16
    w_tok = (cw, cw.n)

    state = {"ps_free": list(ready)}
    bank_free = [[] for _ in range(4)]
    h1b_free = []
    h132_free = [[], []]
    s2_free = []
    h2c_busy = [[], []]
    hid_writable = []

    @block.tensor
    def _(t):
        for s, n in ready:
            t.wait_ge(s.h, n)
        t.wait_ge(cw.h, w_tok[1])

    for tb in range(NTB):
        elem, tbe = tb // 4, tb % 4

        @block.sync
        def _(q, elem=elem, tbe=tbe):
            for s, n in h1b_free:
                q.wait_ge(s.h, n)
            src = kb.ds["h1b"].ap()[elem, :, tbe * 512:(tbe + 1) * 512].rearrange(
                "(c p) w -> p c w", p=128)
            q.dma_start(out=h1btb.rearrange("p (c w) -> p c w", c=KC), in_=src).then_inc(ca.h, 16)
            ca.n += 16
        h1b_tok = (ca, ca.n)

        hid_toks = []
        for ffc in range(FFC):
            bank = ffc % 4

            @block.tensor
            def _(t, ffc=ffc, bank=bank, h1b_tok=h1b_tok):
                if ffc == 0:
                    t.wait_ge(ca.h, h1b_tok[1])
                for s, n in bank_free[bank]:
                    t.wait_ge(s.h, n)
                for k in range(KC):
                    ins = t.matmul(PS[bank][:, 0:512],
                                   w1[:, k * FFD + ffc * 128:k * FFD + (ffc + 1) * 128],
                                   h1btb[:, k * 512:(k + 1) * 512],
                                   start=(k == 0), stop=(k == KC - 1))
                pe.inc(ins)
            pe_t = pe.n

            @block.vector
            def _(v, ffc=ffc, bank=bank, pe_t=pe_t, hw=list(hid_writable)):
                v.wait_ge(pe.h, pe_t)
                for s, n in hw:
                    v.wait_ge(s.h, n)
                b_ap = C.f32(kb.cp_off + 4 * (CP["bff1"] + ffc), 1)
                dv.inc(v.tensor_scalar(hid[:, ffc * 512:(ffc + 1) * 512],
                                       PS[bank][:, 0:512], b_ap, 0.0,
                                       ALU.add, ALU.max))
            bank_free[bank] = [(dv, dv.n)]
            hid_toks.append((dv, dv.n))
        h1b_free = [(pe, pe.n)]

        dc_toks = []
        for dc in range(KC):
            bank = dc % 4
            par = dc % 2

            @block.sync
            def _(q, elem=elem, tbe=tbe, dc=dc, par=par):
                for s, n in h132_free[par]:
                    q.wait_ge(s.h, n)
                q.dma_start(out=A.f32(h132_off + par * 512 * 4, 512),
                            in_=kb.ds["h1f"].ap()[elem, dc * 128:(dc + 1) * 128,
                                                  tbe * 512:(tbe + 1) * 512]).then_inc(ca.h, 16)
                ca.n += 16
            h132_tok = (ca, ca.n)

            @block.tensor
            def _(t, dc=dc, bank=bank, hid_toks=list(hid_toks)):
                if dc == 0:
                    for s, n in hid_toks:
                        t.wait_ge(s.h, n)
                for s, n in bank_free[bank]:
                    t.wait_ge(s.h, n)
                for ffc in range(FFC):
                    ins = t.matmul(PS[bank][:, 0:512],
                                   w2[:, ffc * D + dc * 128:ffc * D + (dc + 1) * 128],
                                   hid[:, ffc * 512:(ffc + 1) * 512],
                                   start=(ffc == 0), stop=(ffc == FFC - 1))
                pe.inc(ins)
            pe_t = pe.n

            @block.vector
            def _(v, dc=dc, bank=bank, par=par, pe_t=pe_t, h132_tok=h132_tok):
                v.wait_ge(pe.h, pe_t)
                v.wait_ge(ca.h, h132_tok[1])
                for s, n in s2_free:
                    v.wait_ge(s.h, n)
                b2 = C.f32(kb.cp_off + 4 * (CP["bff2"] + dc), 1)
                dv.inc(v.scalar_tensor_tensor(s2[:, dc * 512:(dc + 1) * 512],
                                              PS[bank][:, 0:512], b2,
                                              A.f32(h132_off + par * 512 * 4, 512),
                                              ALU.add, ALU.add))
            bank_free[bank] = [(dv, dv.n)]
            h132_free[par] = [(dv, dv.n)]
            dc_toks.append((dv, dv.n))
        s2_free = []

        def y_put(c, tmp, tok, tb=tb, elem=elem, tbe=tbe):
            h2c = A.bf16(h2c_off + (c % 2) * 512 * 2, 512)
            toks = []

            @block.vector
            def _(v, c=c):
                for s, n in h2c_busy[c % 2]:
                    v.wait_ge(s.h, n)
                dv.inc(v.tensor_copy(out=h2c, in_=tmp))
            bf_tok = (dv, dv.n)

            @block.sync
            def _(q, c=c, bf_tok=bf_tok, tok=tok):
                q.wait_ge(tok[0].h, tok[1])
                q.dma_start(out=kb.ds["h2f"].ap()[elem, c * 128:(c + 1) * 128,
                                                  tbe * 512:(tbe + 1) * 512],
                            in_=tmp).then_inc(cs.h, 16)
                cs.n += 16
                toks.append((cs, cs.n))
                q.wait_ge(bf_tok[0].h, bf_tok[1])
                q.dma_start(out=kb.ds["h2b"].ap()[elem, c * 128:(c + 1) * 128,
                                                  tbe * 512:(tbe + 1) * 512],
                            in_=h2c).then_inc(cs.h, 16)
                cs.n += 16
            h2c_busy[c % 2] = [(cs, cs.n)]
            return toks

        state = _ln_block(kb, s2, scr, y_put, gamma=CP["ln2_g"], beta=CP["ln2_b"],
                          ready=dc_toks, state=state)
        s2_free = [(dv, kb.sem("dv").n), (ac, kb.sem("ac").n)]
        hid_writable = state["xbf_free"]
    A.release()
    kb.ln_ps_free = state["ps_free"]
    kb.ffg_done = [(cs, cs.n)]


def _p_gate(kb, ready):
    A, C, block, nc, PS = kb.A, kb.C, kb.block, kb.nc, kb.PS
    cw, ca, cs, pe, dv, ac = (kb.sem(n) for n in ["cw", "ca", "cs", "pe", "dv", "ac"])
    A.mark()
    wg_off = A.alloc(KC * NE * 2)
    h2b_off = A.alloc(KC * 512 * 2)
    lg_off = A.alloc(512 * 4)
    gt_off = A.alloc(32 * 4)
    gg_off = A.alloc(32 * 4)
    sm_off = A.alloc(16 * 4)
    gr_off = A.alloc(512 * 2)
    wg = A.bf16(wg_off, KC * NE)
    h2btb = A.bf16(h2b_off, KC * 512)
    lg = A.f32(lg_off, 512, parts=NE)
    gt = A.f32(gt_off, 32)
    gg = A.f32(gg_off, 32)
    gr = A.bf16(gr_off, 512, parts=NE)
    identf = C.f32(kb.identf_off, 128)
    gbf = [[] for _ in range(4)]

    @block.gpsimd
    def _(q):
        for s, n in ready:
            q.wait_ge(s.h, n)
        q.dma_start(out=wg.rearrange("p (k c) -> p k c", k=KC),
                    in_=kb.w["ffn_Wg"].rearrange("(k p) c -> p k c", p=128)).then_inc(cw.h, 16)
        cw.n += 16
    wg_tok = (cw, cw.n)

    h2b_free = []
    lg_free = []
    gr_free = []
    for tb in range(NTB):
        elem, tbe = tb // 4, tb % 4
        bankA = tb % 2
        bankB = 2 + tb % 2

        @block.sync
        def _(q, elem=elem, tbe=tbe):
            for s, n in h2b_free:
                q.wait_ge(s.h, n)
            src = kb.ds["h2b"].ap()[elem, :, tbe * 512:(tbe + 1) * 512].rearrange(
                "(c p) w -> p c w", p=128)
            q.dma_start(out=h2btb.rearrange("p (c w) -> p c w", c=KC), in_=src).then_inc(ca.h, 16)
            ca.n += 16
        h_tok = (ca, ca.n)

        @block.tensor
        def _(t, bankA=bankA, h_tok=h_tok, tb=tb):
            t.wait_ge(ca.h, h_tok[1])
            if tb == 0:
                t.wait_ge(cw.h, wg_tok[1])
                for s, n in ready:
                    t.wait_ge(s.h, n)
            for s, n in gbf[bankA]:
                t.wait_ge(s.h, n)
            for k in range(KC):
                ins = t.matmul(PS[bankA][0:NE, 0:512], wg[:, k * NE:(k + 1) * NE],
                               h2btb[:, k * 512:(k + 1) * 512],
                               start=(k == 0), stop=(k == KC - 1))
            pe.inc(ins)
        pe_lg = pe.n
        h2b_free = [(pe, pe.n)]

        @block.vector
        def _(v, pe_lg=pe_lg, lgf=list(lg_free), bankA=bankA):
            v.wait_ge(pe.h, pe_lg)
            for s, n in lgf:
                v.wait_ge(s.h, n)
            dv.inc(v.tensor_copy(out=lg, in_=PS[bankA][0:NE, 0:512]))
        ac_lg = dv.n
        gbf[bankA] = [(dv, dv.n)]

        @block.tensor
        def _(t, bankB=bankB, ac_lg=ac_lg):
            t.wait_ge(dv.h, ac_lg)
            for s, n in gbf[bankB]:
                t.wait_ge(s.h, n)
            for g in range(4):
                ins = t.transpose(PS[bankB][:, g * 8:(g + 1) * 8],
                                  lg[:, g * 128:(g + 1) * 128], identf[0:NE, 0:NE])
            pe.inc(ins)
        pe_tr = pe.n
        lg_free = [(pe, pe.n)]

        @block.vector
        def _(v, bankB=bankB, pe_tr=pe_tr):
            v.wait_ge(pe.h, pe_tr)
            dv.inc(v.tensor_copy(out=gt, in_=PS[bankB][:, 0:32]))
        ac_gt = dv.n
        gbf[bankB] = [(dv, dv.n)]

        for g in range(4):
            @block.vector
            def _(v, g=g, ac_gt=ac_gt):
                if g == 0:
                    v.wait_ge(dv.h, ac_gt)
                gsl = gt[:, g * 8:(g + 1) * 8]
                dv.inc(v.reduce_max(A.f32(sm_off, 1), gsl, axis=AX.X))
                v.wait_ge(dv.h, dv.n)
                dv.inc(v.tensor_scalar_mul(A.f32(sm_off + 4, 1), A.f32(sm_off, 1), -1.0))
            dv_mx = dv.n

            @block.scalar
            def _(sq, g=g, dv_mx=dv_mx):
                sq.wait_ge(dv.h, dv_mx)
                ac.inc(sq.activation(out=gg[:, g * 8:(g + 1) * 8],
                                     in_=gt[:, g * 8:(g + 1) * 8], func=AF.Exp,
                                     bias=A.f32(sm_off + 4, 1), scale=1.0,
                                     accum_out=A.f32(sm_off + 8, 1)))
            ac_ex = ac.n

            @block.vector
            def _(v, g=g, ac_ex=ac_ex):
                v.wait_ge(ac.h, ac_ex)
                dv.inc(v.reciprocal(A.f32(sm_off + 12, 1), A.f32(sm_off + 8, 1)))
                v.wait_ge(dv.h, dv.n)
                gsl = gt[:, g * 8:(g + 1) * 8]
                v.tensor_scalar(gg[:, g * 8:(g + 1) * 8], gsl, A.f32(sm_off, 1), None,
                                ALU.is_equal)
                dv.inc(v.tensor_scalar(gg[:, g * 8:(g + 1) * 8], gg[:, g * 8:(g + 1) * 8],
                                       A.f32(sm_off + 12, 1), None, ALU.mult))
            dv_gg = dv.n

        @block.tensor
        def _(t, bankB=bankB, dv_gg=dv_gg):
            t.wait_ge(dv.h, dv_gg)
            for s, n in gbf[bankB]:
                t.wait_ge(s.h, n)
            for g in range(4):
                ins = t.transpose(PS[bankB][0:NE, g * 128:(g + 1) * 128],
                                  gg[:, g * 8:(g + 1) * 8], identf)
            pe.inc(ins)
        pe_gb = pe.n

        @block.vector
        def _(v, bankB=bankB, pe_gb=pe_gb, grf=list(gr_free)):
            v.wait_ge(pe.h, pe_gb)
            for s, n in grf:
                v.wait_ge(s.h, n)
            dv.inc(v.tensor_copy(out=gr, in_=PS[bankB][0:NE, 0:512]))
        ac_gr = dv.n
        gbf[bankB] = [(dv, dv.n)]

        @block.sync
        def _(q, tb=tb, ac_gr=ac_gr):
            q.wait_ge(dv.h, ac_gr)
            q.dma_start(out=kb.ds["gall"].ap()[:, tb * 512:(tb + 1) * 512],
                        in_=gr).then_inc(cs.h, 16)
            cs.n += 16
        gr_free = [(cs, cs.n)]
    A.release()
    kb.gate_done = [(cs, cs.n)]


def _p_moe(kb, ready):
    A, C, block, nc, PS = kb.A, kb.C, kb.block, kb.nc, kb.PS
    cw, ca, cs, cb, pe, dv, ac = (kb.sem(n) for n in ["cw", "ca", "cs", "cb", "pe", "dv", "ac"])
    A.mark()
    w1_off = A.alloc(KC * FFD * 2)
    w2_off = A.alloc(FFC * D * 2)
    hid_off = A.alloc(FFC * 512 * 2)
    h2tb_off = A.alloc(KC * 512 * 2)
    gb_off = A.alloc(2 * 512 * 2)
    stage_off = A.alloc(KC * 512 * 4)
    be1_off = A.alloc(FFD * 2)
    ge_off = A.alloc(2 * 512 * 2)
    ga_off = A.alloc(2 * 512 * 2)
    be2_off = A.alloc(D * 2)
    w1 = A.bf16(w1_off, KC * FFD)
    w2 = A.bf16(w2_off, FFC * D)
    hid = A.bf16(hid_off, FFC * 512)
    h2tb = A.bf16(h2tb_off, KC * 512)
    gb_ap = lambda p_: A.bf16(gb_off + p_ * 512 * 2, 512)
    stage = A.f32(stage_off, KC * 512)
    be1r = A.bf16(be1_off, FFD, parts=1)
    ge_ap = lambda p_: A.bf16(ge_off + p_ * 512 * 2, 512, parts=1)
    ga_ap = lambda p_: A.bf16(ga_off + p_ * 512 * 2, 512, parts=NE)
    be2 = A.bf16(be2_off, D, parts=NE)
    ones_row = C.bf16(kb.ones_row_bf_off, 128, parts=1)

    @block.gpsimd
    def _(q):
        for s, n in ready:
            q.wait_ge(s.h, n)
        q.dma_start(out=be2, in_=kb.w["ffn_be2"][:, :]).then_inc(cw.h, 16)
        cw.n += 16
    be2_tok = (cw, cw.n)

    bank_free = [[] for _ in range(4)]
    w_free = []
    h2_free = []
    gb_free = [[], []]
    ge_free = [[], []]
    ga_free = [[], []]
    stage_free = []
    acc_state = [[] for _ in range(NTB)]
    hid_writable = []

    for e in range(NE):
        @block.gpsimd
        def _(q, e=e, wf=list(w_free)):
            if e == 0:
                for s, n in ready:
                    q.wait_ge(s.h, n)
            for s, n in wf:
                q.wait_ge(s.h, n)
            q.dma_start(out=w1.rearrange("p (k c) -> p k c", k=KC),
                        in_=kb.w["ffn_We1"][e].rearrange("(k p) c -> p k c", p=128)).then_inc(cw.h, 16)
            cw.n += 16
            q.dma_start(out=w2.rearrange("p (f c) -> p f c", f=FFC),
                        in_=kb.w["ffn_We2"][e].rearrange("(f p) c -> p f c", p=128)).then_inc(cw.h, 16)
            cw.n += 16
            q.dma_start(out=be1r, in_=kb.w["ffn_be1"][e][None, :]).then_inc(cw.h, 16)
            cw.n += 16
        w_tok = (cw, cw.n)

        for tb in range(NTB):
            elem, tbe = tb // 4, tb % 4
            par = tb % 2

            @block.sync
            def _(q, e=e, tb=tb, elem=elem, tbe=tbe, par=par, h2f=list(h2_free)):
                for s, n in h2f + ge_free[par] + ga_free[par]:
                    q.wait_ge(s.h, n)
                src = kb.ds["h2b"].ap()[elem, :, tbe * 512:(tbe + 1) * 512].rearrange(
                    "(c p) w -> p c w", p=128)
                q.dma_start(out=h2tb.rearrange("p (c w) -> p c w", c=KC),
                            in_=src).then_inc(ca.h, 16)
                ca.n += 16
                q.dma_start(out=ge_ap(par), in_=kb.ds["gall"].ap()[e:e + 1, tb * 512:(tb + 1) * 512]
                            ).then_inc(ca.h, 16)
                ca.n += 16
                if e == NE - 1:
                    q.dma_start(out=ga_ap(par), in_=kb.ds["gall"].ap()[:, tb * 512:(tb + 1) * 512]
                                ).then_inc(ca.h, 16)
                    ca.n += 16
            ld_tok = (ca, ca.n)

            bankg = 3

            @block.tensor
            def _(t, e=e, tb=tb, par=par, bankg=bankg, ld_tok=ld_tok, w_tok=w_tok,
                  gbf=list(gb_free[par])):
                t.wait_ge(ca.h, ld_tok[1])
                if tb == 0:
                    t.wait_ge(cw.h, w_tok[1])
                    if e == 0:
                        t.wait_ge(cw.h, be2_tok[1])
                for s, n in bank_free[bankg]:
                    t.wait_ge(s.h, n)
                for s, n in gbf:
                    t.wait_ge(s.h, n)
                pe.inc(t.matmul(PS[bankg][:, 0:512], ones_row, ge_ap(par),
                                start=True, stop=True))
            pe_gb = pe.n

            @block.vector
            def _(v, par=par, bankg=bankg, pe_gb=pe_gb):
                v.wait_ge(pe.h, pe_gb)
                dv.inc(v.tensor_copy(out=gb_ap(par), in_=PS[bankg][:, 0:512]))
            bank_free[bankg] = [(dv, dv.n)]
            gb_tok = (dv, dv.n)

            @block.vector
            def _(v, par=par, gb_tok=gb_tok):
                v.wait_ge(dv.h, gb_tok[1])
                for k in range(KC):
                    ins = v.tensor_mul(h2tb[:, k * 512:(k + 1) * 512],
                                       h2tb[:, k * 512:(k + 1) * 512], gb_ap(par))
                dv.inc(ins)
            gh_tok = (dv, dv.n)
            gb_free[par] = [(dv, dv.n)]

            hid_toks = []
            for ffc in range(FFC):
                bank = ffc % 3

                @block.tensor
                def _(t, e=e, ffc=ffc, par=par, bank=bank, gh_tok=gh_tok):
                    if ffc == 0:
                        t.wait_ge(dv.h, gh_tok[1])
                    for s, n in bank_free[bank]:
                        t.wait_ge(s.h, n)
                    for k in range(KC):
                        t.matmul(PS[bank][:, 0:512],
                                 w1[:, k * FFD + ffc * 128:k * FFD + (ffc + 1) * 128],
                                 h2tb[:, k * 512:(k + 1) * 512],
                                 start=(k == 0), stop=False)
                    pe.inc(t.matmul(PS[bank][:, 0:512],
                                    be1r[0:1, ffc * 128:(ffc + 1) * 128], ge_ap(par),
                                    start=False, stop=True))
                pe_t = pe.n

                @block.vector
                def _(v, ffc=ffc, bank=bank, pe_t=pe_t, hw=list(hid_writable)):
                    v.wait_ge(pe.h, pe_t)
                    for s, n in hw:
                        v.wait_ge(s.h, n)
                    dv.inc(v.tensor_scalar(hid[:, ffc * 512:(ffc + 1) * 512],
                                           PS[bank][:, 0:512], 0.0, None, ALU.max))
                bank_free[bank] = [(dv, dv.n)]
                hid_toks.append((dv, dv.n))
            hid_writable = []
            h2_free = [(pe, pe.n)]
            ge_free[par] = [(pe, pe.n)]

            dc_toks = []
            for dc in range(KC):
                bank = dc % 3

                @block.tensor
                def _(t, e=e, dc=dc, par=par, bank=bank, hid_toks=list(hid_toks)):
                    if dc == 0:
                        for s, n in hid_toks:
                            t.wait_ge(s.h, n)
                    for s, n in bank_free[bank]:
                        t.wait_ge(s.h, n)
                    last = (e != NE - 1)
                    for ffc in range(FFC):
                        ins = t.matmul(PS[bank][:, 0:512],
                                       w2[:, ffc * D + dc * 128:ffc * D + (dc + 1) * 128],
                                       hid[:, ffc * 512:(ffc + 1) * 512],
                                       start=(ffc == 0),
                                       stop=(last and ffc == FFC - 1))
                    if e == NE - 1:
                        ins = t.matmul(PS[bank][:, 0:512],
                                       be2[:, dc * 128:(dc + 1) * 128], ga_ap(par),
                                       start=False, stop=True)
                    pe.inc(ins)
                pe_t = pe.n

                @block.vector
                def _(v, dc=dc, bank=bank, pe_t=pe_t, sf=list(stage_free)):
                    v.wait_ge(pe.h, pe_t)
                    for s, n in sf:
                        v.wait_ge(s.h, n)
                    dv.inc(v.tensor_copy(out=stage[:, dc * 512:(dc + 1) * 512],
                                         in_=PS[bank][:, 0:512]))
                bank_free[bank] = [(dv, dv.n)]
                dc_toks.append((dv, dv.n))
            if e == NE - 1:
                ga_free[par] = [(pe, pe.n)]
            stage_free = []

            accv = kb.ds["moeacc"].ap()[:, tb * 512:(tb + 1) * 512].rearrange(
                "(c p) w -> p c w", p=128)

            @block.gpsimd
            def _(q, e=e, tb=tb, dc_toks=list(dc_toks), accst=list(acc_state[tb])):
                if e > 0:
                    for s, n in dc_toks + accst:
                        q.wait_ge(s.h, n)
                    q.dma_start(out=stage.rearrange("p (c w) -> p c w", c=KC), in_=accv,
                                accum_op=ALU.add).then_inc(cb.h, 16)
                    cb.n += 16
            acc_tok = (cb, cb.n)

            @block.sync
            def _(q, e=e, tb=tb, dc_toks=list(dc_toks), acc_tok=acc_tok,
                  accst=list(acc_state[tb])):
                if e > 0:
                    q.wait_ge(cb.h, acc_tok[1])
                else:
                    for s, n in dc_toks + accst:
                        q.wait_ge(s.h, n)
                q.dma_start(out=accv, in_=stage.rearrange("p (c w) -> p c w", c=KC)
                            ).then_inc(cs.h, 16)
                cs.n += 16
            acc_state[tb] = [(cs, cs.n)]
            stage_free = [(cs, cs.n)]
        w_free = [(pe, pe.n)]
    A.release()
    kb.moe_done = [(cs, cs.n)]


def _p_out(kb, x_src, out_dst, scale, ready):
    A, C, block, nc, PS = kb.A, kb.C, kb.block, kb.nc, kb.PS
    cw, ca, cs, pe, dv, ac = (kb.sem(n) for n in ["cw", "ca", "cs", "pe", "dv", "ac"])
    A.mark()
    wo_off = A.alloc(KC * D * 2)
    acc_off = A.alloc(KC * 512 * 4)
    h2f_off = A.alloc(KC * 512 * 4)
    s3_off = A.alloc(KC * 512 * 4)
    h3_off = A.alloc(KC * 512 * 2)
    xr_off = A.alloc(KC * 512 * 4)
    oc_off = A.alloc(2 * 512 * 4)
    scr = ln_scr_alloc(A)
    wo = A.bf16(wo_off, KC * D)
    acc = A.f32(acc_off, KC * 512)
    h2f = A.f32(h2f_off, KC * 512)
    s3 = A.f32(s3_off, KC * 512)
    h3 = A.bf16(h3_off, KC * 512)
    xr = A.f32(xr_off, KC * 512)

    @block.gpsimd
    def _(q):
        for s, n in ready:
            q.wait_ge(s.h, n)
        q.dma_start(out=wo.rearrange("p (k c) -> p k c", k=KC),
                    in_=kb.w["ffn_Wout"].rearrange("(k p) c -> p k c", p=128)).then_inc(cw.h, 16)
        cw.n += 16
    wo_tok = (cw, cw.n)

    state = {"ps_free": list(ready)}
    bank_free = [[] for _ in range(4)]
    in_free = []
    s3_free = []
    h3_free = []
    xr_free = []
    oc_busy = [[], []]

    @block.tensor
    def _(t):
        for s, n in ready:
            t.wait_ge(s.h, n)
        t.wait_ge(cw.h, wo_tok[1])

    for tb in range(NTB):
        elem, tbe = tb // 4, tb % 4

        @block.sync
        def _(q, elem=elem, tbe=tbe, tb=tb, inf=list(in_free)):
            for s, n in inf:
                q.wait_ge(s.h, n)
            q.dma_start(out=acc.rearrange("p (c w) -> p c w", c=KC),
                        in_=kb.ds["moeacc"].ap()[:, tb * 512:(tb + 1) * 512].rearrange(
                            "(c p) w -> p c w", p=128)).then_inc(ca.h, 16)
            ca.n += 16
            q.dma_start(out=h2f.rearrange("p (c w) -> p c w", c=KC),
                        in_=kb.ds["h2f"].ap()[elem, :, tbe * 512:(tbe + 1) * 512].rearrange(
                            "(c p) w -> p c w", p=128)).then_inc(ca.h, 16)
            ca.n += 16
        ld_tok = (ca, ca.n)

        @block.vector
        def _(v, ld_tok=ld_tok, s3f=list(s3_free)):
            v.wait_ge(ca.h, ld_tok[1])
            for s, n in s3f:
                v.wait_ge(s.h, n)
            for c in range(KC):
                ins = v.tensor_add(s3[:, c * 512:(c + 1) * 512],
                                   h2f[:, c * 512:(c + 1) * 512],
                                   acc[:, c * 512:(c + 1) * 512])
            dv.inc(ins)
        s3_tok = (dv, dv.n)
        in_free = [(dv, dv.n)]

        def y_put(c, tmp, tok):
            @block.vector
            def _(v, c=c):
                for s, n in h3_free:
                    v.wait_ge(s.h, n)
                dv.inc(v.tensor_copy(out=h3[:, c * 512:(c + 1) * 512], in_=tmp))
            return [(dv, dv.n)]

        state = _ln_block(kb, s3, scr, y_put, gamma=CP["ln3_g"], beta=CP["ln3_b"],
                          ready=[s3_tok], state=state)
        h3_tok = (dv, kb.sem("dv").n)
        s3_free = [(dv, kb.sem("dv").n), (ac, kb.sem("ac").n)]

        @block.sync
        def _(q, elem=elem, tbe=tbe, xrf=list(xr_free)):
            for s, n in xrf:
                q.wait_ge(s.h, n)
            q.dma_start(out=xr.rearrange("p (c w) -> p c w", c=KC),
                        in_=x_src[elem][:, tbe * 512:(tbe + 1) * 512].rearrange(
                            "(c p) w -> p c w", p=128)).then_inc(ca.h, 16)
            ca.n += 16
        xr_tok = (ca, ca.n)

        h3_free = []
        for dc in range(KC):
            bank = dc % 4
            par = dc % 2

            @block.tensor
            def _(t, dc=dc, bank=bank, h3_tok=h3_tok):
                if dc == 0:
                    t.wait_ge(dv.h, h3_tok[1])
                for s, n in bank_free[bank]:
                    t.wait_ge(s.h, n)
                for k in range(KC):
                    ins = t.matmul(PS[bank][:, 0:512],
                                   wo[:, k * D + dc * 128:k * D + (dc + 1) * 128],
                                   h3[:, k * 512:(k + 1) * 512],
                                   start=(k == 0), stop=(k == KC - 1))
                pe.inc(ins)
            pe_t = pe.n
            h3_free.append((pe, pe.n))

            @block.vector
            def _(v, dc=dc, bank=bank, par=par, pe_t=pe_t, xr_tok=xr_tok,
                  ocb=list(oc_busy[par])):
                v.wait_ge(pe.h, pe_t)
                v.wait_ge(ca.h, xr_tok[1])
                for s, n in ocb:
                    v.wait_ge(s.h, n)
                dv.inc(v.scalar_tensor_tensor(A.f32(oc_off + par * 512 * 4, 512),
                                              PS[bank][:, 0:512], scale,
                                              xr[:, dc * 512:(dc + 1) * 512],
                                              ALU.mult, ALU.add))
            bank_free[bank] = [(dv, dv.n)]
            oc_tok = (dv, dv.n)

            @block.sync
            def _(q, elem=elem, tbe=tbe, dc=dc, par=par, oc_tok=oc_tok):
                q.wait_ge(dv.h, oc_tok[1])
                q.dma_start(out=out_dst[elem][dc * 128:(dc + 1) * 128,
                                              tbe * 512:(tbe + 1) * 512],
                            in_=A.f32(oc_off + par * 512 * 4, 512)).then_inc(cs.h, 16)
                cs.n += 16
            oc_busy[par] = [(cs, cs.n)]
        xr_free = [(dv, dv.n)]
    A.release()
    kb.ln_ps_free = state["ps_free"]
    kb.out_done = [(cs, cs.n)]


def _p_sqkv(kb, z_off, q_off, k_off, vt_off, wbuf_off, bvr_off, ready):
    A, C, block, nc, PS = kb.A, kb.C, kb.block, kb.nc, kb.PS
    cw, pe, ac, dv = kb.sem("cw"), kb.sem("pe"), kb.sem("ac"), kb.sem("dv")
    names = ["attn_wq", "attn_wk", "attn_wv"]
    wb = lambda par: A.bf16(wbuf_off + par * KC * 512 * 2, KC * 512)
    z_ap = lambda k: A.bf16(z_off + 2 * k * S, S)
    bvr = A.bf16(bvr_off, D, parts=1)
    ones_row = C.bf16(kb.ones_row_bf_off, 128, parts=1)

    @block.tensor
    def _(t):
        for s, n in ready:
            t.wait_ge(s.h, n)

    @block.gpsimd
    def _(q):
        for s, n in ready:
            q.wait_ge(s.h, n)
        q.dma_start(out=bvr, in_=kb.w["attn_bv"][None, :]).then_inc(cw.h, 16)
        cw.n += 16
    bv_tok = (cw, cw.n)

    wload = {}
    wfree = [[], []]
    bank_free = [[] for _ in range(4)]

    def emit_load(cb):
        par = cb % 2
        mat, half = names[cb // 2], cb % 2

        @block.gpsimd
        def _(q):
            for s, n in wfree[par]:
                q.wait_ge(s.h, n)
            src = kb.w[mat][:, half * 512:(half + 1) * 512].rearrange("(k p) c -> p k c", p=128)
            q.dma_start(out=wb(par).rearrange("p (k c) -> p k c", k=KC), in_=src).then_inc(cw.h, 16)
            cw.n += 16
        wload[cb] = (cw, cw.n)

    emit_load(0)
    emit_load(1)
    nbank = 0
    for cb in range(6):
        par = cb % 2
        if cb < 4:
            dst_off = q_off if cb < 2 else k_off
            bias = "bq" if cb < 2 else "bk"
            fbase = (cb % 2) * 4
            for m in range(4):
                f = fbase + m
                for n in range(4):
                    bank = nbank % 4
                    nbank += 1

                    @block.tensor
                    def _(t, cb=cb, par=par, m=m, n=n, bank=bank):
                        t.wait_ge(cw.h, wload[cb][1])
                        for s, nn in bank_free[bank]:
                            t.wait_ge(s.h, nn)
                        for k in range(KC):
                            ins = t.matmul(PS[bank][:, 0:512],
                                           wb(par)[:, k * 512 + m * 128:k * 512 + (m + 1) * 128],
                                           z_ap(k)[:, n * 512:(n + 1) * 512],
                                           start=(k == 0), stop=(k == KC - 1))
                        pe.inc(ins)
                    pe_t = pe.n

                    @block.vector
                    def _(v, f=f, n=n, bank=bank, pe_t=pe_t, dst_off=dst_off, bias=bias):
                        v.wait_ge(pe.h, pe_t)
                        dst = kb.A.bf16(dst_off + 2 * (f * S + n * 512), 512)
                        b_ap = C.f32(kb.cp_off + 4 * (CP[bias] + f), 1)
                        dv.inc(v.tensor_scalar(dst, PS[bank][:, 0:512], b_ap, None, ALU.add))
                    bank_free[bank] = [(dv, dv.n)]
            wfree[par] = [(pe, pe.n)]
            if cb + 2 < 6:
                emit_load(cb + 2)
        else:
            vb = cb - 4
            for tt in range(16):
                bank = nbank % 4
                nbank += 1

                @block.tensor
                def _(t, cb=cb, par=par, tt=tt, vb=vb, bank=bank):
                    t.wait_ge(cw.h, wload[cb][1])
                    if tt == 0:
                        t.wait_ge(cw.h, bv_tok[1])
                    for s, nn in bank_free[bank]:
                        t.wait_ge(s.h, nn)
                    for k in range(KC):
                        t.matmul(PS[bank][:, 0:512],
                                 z_ap(k)[:, tt * 128:(tt + 1) * 128],
                                 wb(par)[:, k * 512:(k + 1) * 512],
                                 start=(k == 0), stop=False)
                    pe.inc(t.matmul(PS[bank][:, 0:512], ones_row,
                                    bvr[0:1, vb * 512:(vb + 1) * 512],
                                    start=False, stop=True))
                pe_t = pe.n

                @block.vector
                def _(v, tt=tt, vb=vb, bank=bank, pe_t=pe_t):
                    v.wait_ge(pe.h, pe_t)
                    dst = kb.A.bf16(vt_off + 2 * (tt * 1024 + vb * 512), 512)
                    dv.inc(v.tensor_copy(out=dst, in_=PS[bank][:, 0:512]))
                bank_free[bank] = [(dv, dv.n)]
            wfree[par] = [(pe, pe.n)]
            if cb + 2 < 6:
                emit_load(cb + 2)

    kb.sqkv_done = [(ac, ac.n), (dv, dv.n)]


def _sp_scr(kb):
    A = kb.A
    return {
        "sc": A.alloc(3 * WIN * 4), "p": A.alloc(2 * WIN * 2), "pT": A.alloc(2 * WIN * 2),
        "cst": A.alloc(S * 2), "nm": A.alloc(16 * 4),
    }


def _p_satt(kb, q_off, k_off, vt_off, ctx_off, mask_off, scr, ready):
    A, C, block, nc, PS, PT = kb.A, kb.C, kb.block, kb.nc, kb.PS, kb.PT
    pe, dv, ac, cs = kb.sem("pe"), kb.sem("dv"), kb.sem("ac"), kb.sem("cs")
    ident = C.bf16(kb.ident_off, 128)
    sc_ap = lambda sl, w: A.f32(scr["sc"] + sl * WIN * 4, w)
    p_ap = lambda par, w: A.bf16(scr["p"] + par * WIN * 2, w)
    pT_ap = lambda par, w: A.bf16(scr["pT"] + par * WIN * 2, w)
    cst = A.bf16(scr["cst"], S, parts=64)
    nm_ap = lambda i: A.f32(scr["nm"] + i * 4, 1)
    MW = 512 + WIN + 512

    head_tok = list(ready)
    sc_state = [[] for _ in range(3)]
    p_state = [[], []]
    pT_state = [[], []]
    PTb_state = [[], []]
    ab_state = [[] for _ in range(4)]
    ctx_state = []
    nm_state = [[], []]
    cst_free = []
    gidx = 0
    sidx = 0

    for h in range(NH):
        bb = (h % 2) * 64
        hc = h // 2
        for c in range(NCH):
            lo = max(0, 2 * c - 2)
            hi = min(16, 2 * c + 4)
            w128 = hi - lo
            wk = w128 * 128
            mvar = 0 if c == 0 else (2 if c == NCH - 1 else 1)
            mbase_col = [0, 512, 512 + WIN][mvar]
            for qt in range(2):
                scs = sidx % 3
                par = sidx % 2
                b01 = (sidx % 2) * 2
                sidx += 1
                qcol = c * 256 + qt * 128

                blocks = []
                o = 0
                while o < wk:
                    wb_ = min(512, wk - o)
                    blocks.append((o, wb_))
                    o += wb_
                for bi, (ko, wb_) in enumerate(blocks):
                    bank = b01 + bi % 2

                    @block.tensor
                    def _(t, h=h, bb=bb, hc=hc, qcol=qcol, lo=lo, ko=ko, wb_=wb_, bank=bank):
                        if gidx == 0 and ko == 0:
                            for s, n in head_tok:
                                t.wait_ge(s.h, n)
                        for s, n in ab_state[bank]:
                            t.wait_ge(s.h, n)
                        ksl = A.bf16(k_off, KC * S, parts=64, p0=bb)
                        qsl = A.bf16(q_off, KC * S, parts=64, p0=bb)
                        pe.inc(t.matmul(PS[bank][:, 0:wb_],
                                        qsl[:, hc * S + qcol:hc * S + qcol + 128],
                                        ksl[:, hc * S + lo * 128 + ko:hc * S + lo * 128 + ko + wb_],
                                        start=True, stop=True))
                    pe_t = pe.n

                    @block.vector
                    def _(v, scs=scs, ko=ko, wb_=wb_, bank=bank, pe_t=pe_t, qt=qt,
                          mbase_col=mbase_col, scst=list(sc_state[scs])):
                        v.wait_ge(pe.h, pe_t)
                        for s, n in scst:
                            v.wait_ge(s.h, n)
                        mask = A.bf16(mask_off + 2 * (qt * MW + mbase_col + ko), wb_)
                        dv.inc(v.tensor_add(sc_ap(scs, WIN)[:, ko:ko + wb_],
                                            PS[bank][:, 0:wb_], mask))
                    ab_state[bank] = [(dv, dv.n)]
                sc_state[scs] = []
                sc_tok = (dv, dv.n)

                @block.vector
                def _(v, scs=scs, wk=wk, par=par, sc_tok=sc_tok, nmst=list(nm_state[par])):
                    v.wait_ge(dv.h, sc_tok[1])
                    for s, n in nmst:
                        v.wait_ge(s.h, n)
                    dv.inc(v.reduce_max(nm_ap(par), sc_ap(scs, wk), axis=AX.X))
                    v.wait_ge(dv.h, dv.n)
                    dv.inc(v.tensor_scalar_mul(nm_ap(2 + par), nm_ap(par), -0.125))
                dv_nm = dv.n

                @block.scalar
                def _(sq, scs=scs, wk=wk, par=par, dv_nm=dv_nm, pst=list(p_state[par])):
                    sq.wait_ge(dv.h, dv_nm)
                    for s, n in pst:
                        sq.wait_ge(s.h, n)
                    ac.inc(sq.activation(out=p_ap(par, wk), in_=sc_ap(scs, wk), func=AF.Exp,
                                         bias=nm_ap(2 + par), scale=0.125,
                                         accum_out=nm_ap(4 + par)))
                ac_exp = ac.n
                sc_state[scs] = [(ac, ac.n)]
                p_state[par] = []

                @block.vector
                def _(v, par=par, wk=wk, ac_exp=ac_exp):
                    v.wait_ge(ac.h, ac_exp)
                    dv.inc(v.reciprocal(nm_ap(6 + par), nm_ap(4 + par)))
                    v.wait_ge(dv.h, dv.n)
                    dv.inc(v.tensor_scalar(p_ap(par, wk), p_ap(par, wk), nm_ap(6 + par),
                                           None, ALU.mult))
                dv_p = dv.n
                nm_state[par] = [(dv, dv.n)]

                bankb = gidx % 2
                gidx += 1

                @block.tensor
                def _(t, w128=w128, par=par, bankb=bankb, dv_p=dv_p):
                    t.wait_ge(dv.h, dv_p)
                    for s, n in PTb_state[bankb]:
                        t.wait_ge(s.h, n)
                    for i in range(w128):
                        ins = t.transpose(PT[bankb][:, i * 128:(i + 1) * 128],
                                          p_ap(par, WIN)[:, i * 128:(i + 1) * 128], ident)
                    pe.inc(ins)
                pe_tr = pe.n
                p_state[par] = p_state[par] + [(pe, pe.n)]

                @block.vector
                def _(v, w128=w128, par=par, bankb=bankb, pe_tr=pe_tr, ptst=list(pT_state[par])):
                    v.wait_ge(pe.h, pe_tr)
                    for s, n in ptst:
                        v.wait_ge(s.h, n)
                    dv.inc(v.tensor_copy(out=pT_ap(par, w128 * 128),
                                         in_=PT[bankb][:, 0:w128 * 128]))
                PTb_state[bankb] = [(dv, dv.n)]
                pT_tok = (dv, dv.n)

                @block.tensor
                def _(t, h=h, w128=w128, lo=lo, par=par, pT_tok=pT_tok, ctxst=list(ctx_state)):
                    t.wait_ge(dv.h, pT_tok[1])
                    for s, n in ctxst:
                        t.wait_ge(s.h, n)
                    for i in range(w128):
                        kbk = lo + i
                        ins = t.matmul(PS[4][0:64, 0:128],
                                       A.bf16(vt_off + 2 * (kbk * 1024 + h * DH), DH),
                                       pT_ap(par, WIN)[:, i * 128:(i + 1) * 128],
                                       start=(i == 0), stop=(i == w128 - 1))
                    pe.inc(ins)
                pe_ctx = pe.n
                pT_state[par] = [(pe, pe.n)]

                @block.vector
                def _(v, qcol=qcol, pe_ctx=pe_ctx, cstf=list(cst_free), first=(c == 0 and qt == 0)):
                    v.wait_ge(pe.h, pe_ctx)
                    if first:
                        for s, n in cstf:
                            v.wait_ge(s.h, n)
                    dv.inc(v.tensor_copy(out=cst[:, qcol:qcol + 128],
                                         in_=PS[4][0:64, 0:128]))
                ctx_state = [(dv, dv.n)]

        @block.sync
        def _(q, h=h, bb=bb, hc=hc):
            q.wait_ge(dv.h, dv.n)
            dst = A.bf16(ctx_off + 2 * (hc * S), S, parts=64, p0=bb)
            q.dma_start(out=dst, in_=cst[:, :]).then_inc(cs.h, 16)
            cs.n += 16
        cst_free = [(cs, cs.n)]
        head_tok = [(pe, pe.n), (cs, cs.n)]

    kb.satt_done = [(cs, cs.n)]


def _p_swo(kb, elem, ctx_off, x_src, x_dst, ready):
    A, C, block, nc, PS = kb.A, kb.C, kb.block, kb.nc, kb.PS
    cw, ca, cs, pe, dv = (kb.sem(n) for n in ["cw", "ca", "cs", "pe", "dv"])
    A.mark()
    wo_off = A.alloc(KC * D * 2)
    x1_off = A.alloc(KC * 512 * 4)
    oc_off = A.alloc(2 * 512 * 4)
    wo = A.bf16(wo_off, KC * D)
    x1 = A.f32(x1_off, KC * 512)

    @block.gpsimd
    def _(q):
        for s, n in ready:
            q.wait_ge(s.h, n)
        q.dma_start(out=wo.rearrange("p (k c) -> p k c", k=KC),
                    in_=kb.w["attn_wo"].rearrange("(k p) c -> p k c", p=128)).then_inc(cw.h, 16)
        cw.n += 16
    wo_tok = (cw, cw.n)

    bank_free = [[] for _ in range(4)]
    x1_free = []
    oc_busy = [[], []]
    for tb in range(S // 512):
        @block.sync
        def _(q, tb=tb, x1f=list(x1_free)):
            for s, n in x1f:
                q.wait_ge(s.h, n)
            q.dma_start(out=x1.rearrange("p (c w) -> p c w", c=KC),
                        in_=x_src[elem][:, tb * 512:(tb + 1) * 512].rearrange(
                            "(c p) w -> p c w", p=128)).then_inc(ca.h, 16)
            ca.n += 16
        x1_tok = (ca, ca.n)

        for dc in range(KC):
            bank = dc % 4
            par = dc % 2

            @block.tensor
            def _(t, tb=tb, dc=dc, bank=bank):
                if tb == 0 and dc == 0:
                    t.wait_ge(cw.h, wo_tok[1])
                    for s, n in ready:
                        t.wait_ge(s.h, n)
                for s, n in bank_free[bank]:
                    t.wait_ge(s.h, n)
                for k in range(KC):
                    ins = t.matmul(PS[bank][:, 0:512],
                                   wo[:, k * D + dc * 128:k * D + (dc + 1) * 128],
                                   A.bf16(ctx_off + 2 * (k * S + tb * 512), 512),
                                   start=(k == 0), stop=(k == KC - 1))
                pe.inc(ins)
            pe_t = pe.n

            @block.vector
            def _(v, dc=dc, bank=bank, par=par, pe_t=pe_t, x1_tok=x1_tok,
                  ocb=list(oc_busy[par])):
                v.wait_ge(pe.h, pe_t)
                v.wait_ge(ca.h, x1_tok[1])
                for s, n in ocb:
                    v.wait_ge(s.h, n)
                bo = C.f32(kb.cp_off + 4 * (CP["bo"] + dc), 1)
                dv.inc(v.scalar_tensor_tensor(A.f32(oc_off + par * 512 * 4, 512),
                                              PS[bank][:, 0:512], bo,
                                              x1[:, dc * 512:(dc + 1) * 512],
                                              ALU.add, ALU.add))
            bank_free[bank] = [(dv, dv.n)]
            oc_tok = (dv, dv.n)

            @block.sync
            def _(q, tb=tb, dc=dc, par=par, oc_tok=oc_tok):
                q.wait_ge(dv.h, oc_tok[1])
                q.dma_start(out=x_dst[elem][dc * 128:(dc + 1) * 128, tb * 512:(tb + 1) * 512],
                            in_=A.f32(oc_off + par * 512 * 4, 512)).then_inc(cs.h, 16)
                cs.n += 16
            oc_busy[par] = [(cs, cs.n)]
        x1_free = [(dv, dv.n)]
    A.release()
    kb.swo_done = [(cs, cs.n)]


def _p_mln(kb, src, y_sb_off, f32_dram=None, scr=None, ready0=()):
    A, block, nc = kb.A, kb.block, kb.nc
    ca, cs, dv, ac = kb.sem("ca"), kb.sem("cs"), kb.sem("dv"), kb.sem("ac")
    nblk = S // LNW
    state = {"ps_free": list(kb.ln_ps_free), "tmp_busy": None}
    xloaded = []
    xfree = scr.get("xfree", [[], []])

    for b in range(nblk):
        par = b % 2
        x_blk = A.f32(scr["xblk"], KC * LNW)

        @block.sync
        def _(q, b=b, par=par, x_blk=x_blk):
            if b == 0:
                for s, n in ready0:
                    q.wait_ge(s.h, n)
            for s, n in xfree[0] + xfree[1]:
                q.wait_ge(s.h, n)
            src_blk = src[:, b * LNW:(b + 1) * LNW].rearrange("(c p) w -> p c w", p=128)
            out3 = x_blk.rearrange("p (c w) -> p c w", c=KC)
            q.dma_start(out=out3, in_=src_blk).then_inc(ca.h, 16); ca.n += 16
        xloaded.append((ca, ca.n))

        def y_put(c, tmp, tok, b=b, par=par):
            toks = []
            @block.vector
            def _(v):
                y_sb = A.bf16(y_sb_off + 2 * (c * S + b * LNW), LNW)
                v.tensor_copy(out=y_sb, in_=tmp)
            if f32_dram is not None:
                @block.sync
                def _(q):
                    q.wait_ge(tok[0].h, tok[1])
                    q.dma_start(out=f32_dram[c * 128:(c + 1) * 128, b * LNW:(b + 1) * LNW],
                                in_=tmp).then_inc(cs.h, 16)
                    cs.n += 16
                toks.append((cs, cs.n))
            else:
                toks.append(tok)
            return toks

        state["tmp_busy"] = scr.setdefault("tmp_busy", {0: [], 1: []})
        st = _ln_block(kb, x_blk, scr, y_put, ready=[xloaded[-1]] + list(kb.const_tok) + list(ready0), state=state)
        state = st
        scr["tmp_busy"] = st["tmp_busy"]
        xfree[par] = [(dv, kb.sem("dv").n), (ac, kb.sem("ac").n)]
    scr["xfree"] = xfree
    kb.ln_ps_free = state["ps_free"]


def _mln_scr(kb):
    A = kb.A
    scr = ln_scr_alloc(A)
    scr["xblk"] = A.alloc(KC * LNW * 4)
    return scr


def _p_finish(kb, dbg):
    nc, block = kb.nc, kb.block
    cs = kb.sem("cs")

    @block.sync
    def _(q):
        for s in kb._sems.values():
            if s.n > 0 and s.h is not cs.h:
                q.wait_ge(s.h, s.n)
        for name, out_ap in kb.dbg_outs.items():
            t = kb.ds[name]
            q.dma_start(out=out_ap, in_=t.ap()).then_inc(cs.h, 16); cs.n += 16
        q.wait_ge(cs.h, cs.n)


def kernel(**inputs):
    xs = np.asarray(inputs["L0_attn_hidden_states"])
    shards = []
    for c in range(NCORES):
        xc = xs[c * EPC:(c + 1) * EPC]
        shards.append(np.ascontiguousarray(xc.transpose(0, 2, 1)))

    cpack_np = _cpack_host({k: np.asarray(v) for k, v in inputs.items()})
    pe_np = _host_pe_T()
    mask_np = _host_sparse_masks()

    nc = build_program()
    base = {"pe_T": pe_np, "cpack": cpack_np, "spmask": mask_np,
            "ident": np.eye(128, dtype=np.float32)}
    for k in list(inputs.keys()):
        if k == "L0_attn_hidden_states":
            continue
        base[k.replace("L0_", "")] = np.ascontiguousarray(np.asarray(inputs[k], dtype=np.float32))
    in_maps = [dict(base, x=shards[c]) for c in range(NCORES)]
    res = run_bass_kernel_spmd(nc, in_maps, core_ids=list(range(NCORES)))
    outs = []
    for c in range(NCORES):
        yc = res.results[c]["y"]
        outs.append(yc.transpose(0, 2, 1))
    return np.ascontiguousarray(np.concatenate(outs, axis=0))


# version 14
# speedup vs baseline: 1.4760x; 1.1038x over previous
import math
import contextlib
import numpy as np
import concourse.bass as bass
import concourse.mybir as mybir
from concourse.bass_utils import run_bass_kernel_spmd

F32 = mybir.dt.float32
BF16 = mybir.dt.bfloat16
AF = mybir.ActivationFunctionType
ALU = mybir.AluOpType
AX = mybir.AxisListType

B, S, D = 16, 2048, 1024
NH, DH = 16, 64
FFD = 4096
NE = 8
NCORES = 8
EPC = B // NCORES
KC = D // 128
FFC = FFD // 128
NTOK = EPC * S
TB = 512
NTB = NTOK // TB
QT = S // 128
CH = 256
NCH = S // CH
WIN = 768
EPS = 1e-5
NEG = -1.0e30

BD_W = 2304
BD_SLOTS = 4

ARENA_KB = 202
CONST_KB = 4


class Sem:

    def __init__(self, h):
        self.h = h
        self.n = 0

    def inc(self, instr, k=1):
        instr.then_inc(self.h, k)
        self.n += k
        return self.n


class Arena:

    def __init__(self, t, nbytes):
        self.t = t
        self.top = 0
        self.lim = nbytes
        self.marks = []

    def alloc(self, nbytes, align=64):
        off = (self.top + align - 1) // align * align
        self.top = off + nbytes
        assert self.top <= self.lim, f"arena overflow: {self.top} > {self.lim}"
        return off

    def mark(self):
        self.marks.append(self.top)

    def release(self):
        self.top = self.marks.pop()

    def f32(self, off, cols, parts=128, p0=0):
        return self.t[p0:p0 + parts, off:off + 4 * cols].bitcast(F32)

    def bf16(self, off, cols, parts=128, p0=0):
        return self.t[p0:p0 + parts, off:off + 2 * cols].bitcast(BF16)


class KB:

    def __init__(self, nc, es, block):
        self.nc = nc
        self.es = es
        self.block = block
        self._sems = {}

    def sem(self, name):
        if name not in self._sems:
            self._sems[name] = Sem(self.es.enter_context(self.nc.semaphore(name)))
        return self._sems[name]


def _host_pe_T():
    pos = np.arange(S - 1, -1, -1, dtype=np.float32)
    inv_freq = (1.0 / 10000.0 ** (np.arange(0, D, 2, dtype=np.float32) / D)).astype(np.float32)
    sinusoid = pos[:, None].astype(np.float64) * inv_freq[None, :].astype(np.float64)
    pe = np.concatenate([np.sin(sinusoid), np.cos(sinusoid)], axis=-1).astype(np.float32)
    return np.ascontiguousarray(pe.T)


def _host_sparse_masks():
    out = np.zeros((128, 2 * (512 + WIN + 512)), dtype=np.float32)
    i = np.arange(128)[:, None]
    for qt in range(2):
        base = qt * (512 + WIN + 512)
        j = np.arange(512)[None, :]
        qpos = qt * 128 + i
        valid = np.abs(qpos - j) <= 256
        out[:, base:base + 512] = np.where(valid, 0.0, NEG)
        j = np.arange(WIN)[None, :]
        qpos = 256 + qt * 128 + i
        valid = np.abs(qpos - j) <= 256
        out[:, base + 512:base + 512 + WIN] = np.where(valid, 0.0, NEG)
        j = np.arange(512)[None, :]
        qpos = 256 + qt * 128 + i
        valid = np.abs(qpos - j) <= 256
        out[:, base + 512 + WIN:base + 1024 + WIN] = np.where(valid, 0.0, NEG)
    return out


def _to_chunks(v):
    return np.ascontiguousarray(v.reshape(-1, 128).T)


def build_program(dbg=None):
    dbg = dbg or []
    nc = bass.Bass("TRN2", target_bir_lowering=False)

    x_in = nc.dram_tensor("x", [EPC, D, S], F32, kind="ExternalInput").ap()
    w = {}
    w_shapes = {
        "attn_wq": [D, D], "attn_wk": [D, D], "attn_wv": [D, D], "attn_wo": [D, D],
        "attn_bq": [D], "attn_bk": [D], "attn_bv": [D], "attn_bo": [D],
        "ffn_Wqkv": [D, 3 * D], "ffn_Wo": [D, D], "ffn_Wr": [D, D],
        "ffn_u_bias": [NH, DH], "ffn_v_bias": [NH, DH],
        "ffn_ln1_g": [D], "ffn_ln1_b": [D], "ffn_ln2_g": [D], "ffn_ln2_b": [D],
        "ffn_ln3_g": [D], "ffn_ln3_b": [D],
        "ffn_Wff1": [D, FFD], "ffn_bff1": [FFD], "ffn_Wff2": [FFD, D], "ffn_bff2": [D],
        "ffn_Wg": [D, NE], "ffn_We1": [NE, D, FFD], "ffn_be1": [NE, FFD],
        "ffn_We2": [NE, FFD, D], "ffn_be2": [NE, D], "ffn_Wout": [D, D],
    }
    for k, shp in w_shapes.items():
        w[k] = nc.dram_tensor(k, shp, F32, kind="ExternalInput").ap()
    pe_T = nc.dram_tensor("pe_T", [D, S], F32, kind="ExternalInput").ap()
    cpack = nc.dram_tensor("cpack", [128, 160], F32, kind="ExternalInput").ap()
    spmask = nc.dram_tensor("spmask", [128, 2 * (512 + WIN + 512)], F32, kind="ExternalInput").ap()
    ident_in = nc.dram_tensor("ident", [128, 128], F32, kind="ExternalInput").ap()
    x_out = nc.dram_tensor("y", [EPC, D, S], F32, kind="ExternalOutput").ap()

    ds = {}
    def dscr(name, shape, dt=F32):
        ds[name] = nc.dram_tensor(name, shape, dt, kind="Internal")
        return ds[name]

    dscr("xres_a", [EPC, D, S])
    dscr("xres_b", [EPC, D, S])
    dscr("ybuf", [EPC, D, S])
    dscr("h1f", [EPC, D, S])
    dscr("h2f", [EPC, D, S])
    dscr("h1b", [EPC, D, S], BF16)
    dscr("h2b", [EPC, D, S], BF16)
    dscr("rbuf", [D, S], BF16)
    dscr("gall", [NE, NTOK], BF16)
    dscr("moeacc", [D, NTOK])
    dscr("bdscr", [BD_SLOTS, 128, BD_W], BF16)

    dbg_outs = {}
    for name in dbg:
        if name in ds:
            dbg_outs[name] = nc.dram_tensor("dbg_" + name, list(ds[name].shape),
                                            ds[name].dtype, kind="ExternalOutput").ap()

    es = contextlib.ExitStack()
    with es:
        arena_t = es.enter_context(nc.sbuf_tensor("arena", [128, ARENA_KB * 1024], mybir.dt.uint8))
        const_t = es.enter_context(nc.sbuf_tensor("consts", [128, CONST_KB * 1024], mybir.dt.uint8))
        A = Arena(arena_t, ARENA_KB * 1024)
        C = Arena(const_t, CONST_KB * 1024)
        PS = [es.enter_context(nc.psum_tensor(f"ps{i}", [128, 512], F32)) for i in range(6)]
        PT = [es.enter_context(nc.psum_tensor(f"pt{i}", [128, 1024], BF16)) for i in range(2)]
        block = es.enter_context(nc.Block())
        kb = KB(nc, es, block)
        kb.A, kb.C, kb.PS, kb.PT = A, C, PS, PT
        kb.w, kb.ds, kb.dbg_outs = w, ds, dbg_outs
        kb.x_in, kb.x_out = x_in, x_out
        kb.pe_T, kb.cpack, kb.spmask = pe_T, cpack, spmask
        kb.ident_in = ident_in

        for name in ["cw", "ca", "cs", "cb",
                     "pe", "dv", "ac", "po",
                     "gp",
                     "bar"]:
            kb.sem(name)

        _emit_all(kb, dbg)

    _group_engines(nc)
    return nc


def _group_engines(nc):
    order = {}

    def key(ins):
        e = str(ins.engine)
        if e not in order:
            order[e] = len(order) + 1
        return 0 if e == "EngineType.Unassigned" else order[e]

    for f in nc.m.functions:
        for b in f.blocks:
            ins = list(b.instructions)
            ins.sort(key=key)
            b.instructions = ins


_CP_LAYOUT = [("ln1_g", 8), ("ln1_b", 8), ("ln2_g", 8), ("ln2_b", 8), ("ln3_g", 8),
              ("ln3_b", 8), ("u_b", 8), ("v_b", 8), ("bq", 8), ("bk", 8), ("bv", 8),
              ("bo", 8), ("bff1", 32), ("bff2", 8), ("cv", 8)]
_CONST_VALS = [EPS, 0.125, 0.5, 1.0, -1.0, 0.0, NEG, -0.5]
CP = {}
_o = 0
for _n, _w in _CP_LAYOUT:
    CP[_n] = _o
    _o += _w
assert _o <= 160


def _cpack_host(inp):
    out = np.zeros((128, 160), np.float32)
    src = {
        "ln1_g": inp["L0_ffn_ln1_g"], "ln1_b": inp["L0_ffn_ln1_b"],
        "ln2_g": inp["L0_ffn_ln2_g"], "ln2_b": inp["L0_ffn_ln2_b"],
        "ln3_g": inp["L0_ffn_ln3_g"], "ln3_b": inp["L0_ffn_ln3_b"],
        "u_b": np.asarray(inp["L0_ffn_u_bias"]).reshape(-1),
        "v_b": np.asarray(inp["L0_ffn_v_bias"]).reshape(-1),
        "bq": inp["L0_attn_bq"], "bk": inp["L0_attn_bk"], "bv": inp["L0_attn_bv"],
        "bo": inp["L0_attn_bo"], "bff1": inp["L0_ffn_bff1"], "bff2": inp["L0_ffn_bff2"],
    }
    for name, width in _CP_LAYOUT:
        if name == "cv":
            out[:, CP[name]:CP[name] + width] = np.asarray(_CONST_VALS, np.float32)[None, :]
        else:
            a = _to_chunks(np.asarray(src[name]))
            out[:, CP[name]:CP[name] + width] = a
    return out


_XL_DBG = None


def _emit_all(kb, dbg):
    kb.xl_first = True
    kb.xl_dbg = _XL_DBG
    kb.ln_ps_free = []
    kb.r_bank_free = [[] for _ in range(4)]
    kb.dbg_list = dbg
    A = kb.A
    _p_const(kb)
    _p_r(kb)

    def alltok():
        return [(kb.sem(n), kb.sem(n).n)
                for n in ["cw", "ca", "cs", "cb", "pe", "dv", "ac", "gp"]
                if kb.sem(n).n > 0]

    def ffn_call(x_src, x_dst, tag):
        nsc = kb.nc.named_scope
        for e in range(EPC):
            A.mark()
            ctx_off = A.alloc(KC * S * 2)
            A.mark()
            q_off = A.alloc(KC * S * 2)
            k_off = A.alloc(KC * S * 2)
            vt_off = A.alloc(16 * 1024 * 2)
            A.mark()
            y_off = A.alloc(KC * S * 2)
            A.mark()
            scr = _mln_scr(kb)
            with nsc(f"MLN_{tag}{e}", notify=True):
                _p_mln(kb, x_src[e], y_off, f32_dram=kb.ds["ybuf"].ap()[e], scr=scr,
                       ready0=alltok())
            mln_done = alltok()
            A.release()
            wbuf_off = A.alloc(2 * KC * 512 * 2)
            with nsc(f"QKV_{tag}{e}", notify=True):
                _p_qkv(kb, y_off, q_off, k_off, vt_off, wbuf_off, ready=mln_done)
            A.release()
            xscr = _xl_scr(kb)
            with nsc(f"XL_{tag}{e}", notify=True):
                _p_xl(kb, q_off, k_off, vt_off, ctx_off, xscr, ready=alltok())
                kb.xl_first = False
            if tag == "B" and e == 0:
                _dump(kb, "ctx", kb.A.bf16(ctx_off, KC * S), kb.xl_done)
            A.release()
            with nsc(f"WOH_{tag}{e}", notify=True):
                _p_woh(kb, e, ctx_off, ready=alltok())
            A.release()
        with nsc(f"FFG_{tag}", notify=True):
            _p_ffg(kb, ready=alltok())
        with nsc(f"GATE_{tag}", notify=True):
            _p_gate(kb, ready=alltok())
        with nsc(f"MOE_{tag}", notify=True):
            _p_moe(kb, ready=alltok())
        with nsc(f"OUT_{tag}", notify=True):
            _p_out(kb, x_src, x_dst, 0.5, ready=alltok())

    def attn_call(x_src, x_dst):
        A.mark()
        MW = 512 + WIN + 512
        mask_off = A.alloc(2 * MW * 2)
        ca = kb.sem("ca")

        @kb.block.gpsimd
        def _(q):
            for s, n in alltok():
                q.wait_ge(s.h, n)
            q.dma_start(out=A.bf16(mask_off, 2 * MW), in_=kb.spmask).then_inc(ca.h, 16)
            ca.n += 16
        mask_tok = [(ca, ca.n)]
        nsc = kb.nc.named_scope
        for e in range(EPC):
            A.mark()
            ctx_off = A.alloc(KC * S * 2)
            A.mark()
            q_off = A.alloc(KC * S * 2)
            k_off = A.alloc(KC * S * 2)
            vt_off = A.alloc(16 * 1024 * 2)
            A.mark()
            z_off = A.alloc(KC * S * 2)
            A.mark()
            scr = _mln_scr(kb)
            with nsc(f"SLN_{e}", notify=True):
                _p_mln(kb, x_src[e], z_off, f32_dram=None, scr=scr, ready0=alltok())
            mln_done = alltok()
            A.release()
            wbuf_off = A.alloc(2 * KC * 512 * 2)
            bvr_off = A.alloc(D * 2)
            with nsc(f"SQKV_{e}", notify=True):
                _p_sqkv(kb, z_off, q_off, k_off, vt_off, wbuf_off, bvr_off, ready=mln_done)
            A.release()
            spscr = _sp_scr(kb)
            with nsc(f"SATT_{e}", notify=True):
                _p_satt(kb, q_off, k_off, vt_off, ctx_off, mask_off, spscr,
                        ready=alltok() + mask_tok)
            A.release()
            with nsc(f"SWO_{e}", notify=True):
                _p_swo(kb, e, ctx_off, x_src, x_dst, ready=alltok())
            A.release()
        A.release()

    xa = kb.ds["xres_a"].ap()
    xb = kb.ds["xres_b"].ap()
    ffn_call(kb.x_in, xa, "B")
    attn_call(xa, xb)
    ffn_call(xb, kb.x_out, "F")
    _p_finish(kb, dbg)


def _p_const(kb):
    C, block = kb.C, kb.block
    cw, dv = kb.sem("cw"), kb.sem("dv")

    kb.cp_off = C.alloc(160 * 4)
    kb.ident_off = C.alloc(128 * 2)
    kb.ones_col_off = C.alloc(2 * 2)
    kb.ones_row_bf_off = C.alloc(128 * 2)
    kb.ones_row_f_off = C.alloc(128 * 4)
    kb.identf_off = C.alloc(128 * 4)
    kb.neg_off = C.alloc(256 * 4)
    kb.neg_bf_off = C.alloc(256 * 2)

    @block.sync
    def _(q):
        q.dma_start(out=C.f32(kb.cp_off, 160), in_=kb.cpack).then_inc(cw.h, 16); cw.n += 16
        q.dma_start(out=C.f32(kb.identf_off, 128), in_=kb.ident_in).then_inc(cw.h, 16); cw.n += 16

    @block.gpsimd
    def _(q):
        q.dma_start(out=C.bf16(kb.ident_off, 128), in_=kb.ident_in).then_inc(cw.h, 16); cw.n += 16

    @block.vector
    def _(v):
        v.memset(C.bf16(kb.ones_col_off, 2), 1.0)
        v.memset(C.bf16(kb.ones_row_bf_off, 128, parts=1), 1.0)
        v.memset(C.f32(kb.ones_row_f_off, 128, parts=1), 1.0)
        v.memset(C.f32(kb.neg_off, 256), NEG)
        v.memset(C.bf16(kb.neg_bf_off, 256), NEG).then_inc(dv.h, 1); dv.n += 1

    cs = kb.sem("cs")

    @block.sync
    def _(q):
        q.wait_ge(dv.h, dv.n)
        for sl in range(BD_SLOTS):
            q.dma_start(out=kb.ds["bdscr"].ap()[sl, :, S:BD_W],
                        in_=C.bf16(kb.neg_bf_off, BD_W - S)).then_inc(cs.h, 16)
            cs.n += 16
    kb.bdtail_tok = (cs, cs.n)

    for i, val in enumerate(_CONST_VALS):
        ap = C.f32(kb.cp_off + 4 * (CP["cv"] + i), 1)
        kb.nc.const_aps.register(F32, float(val), ap)
    kb.const_tok = [(cw, cw.n), (dv, dv.n)]


LNW = 512


def ln_scr_alloc(A):
    return {
        "xbf": A.alloc(KC * LNW * 2),
        "rows": A.alloc(3 * LNW * 4),
        "tmp": A.alloc(2 * LNW * 4),
    }


def _ln_block(kb, x_blk, scr, y_put, gamma=None, beta=None, ready=(), state=None):
    A, C, block, PS = kb.A, kb.C, kb.block, kb.PS
    nc = kb.nc
    W = LNW
    pe, dv, ac = kb.sem("pe"), kb.sem("dv"), kb.sem("ac")
    ps_s, ps_q = PS[4], PS[5]
    if state is None:
        state = {}
    ps_free = state.get("ps_free", [])
    tmp_busy = state.get("tmp_busy", {0: [], 1: []})

    xbf = A.bf16(scr["xbf"], KC * W)
    xsq = xbf
    rows = scr["rows"]
    r_s = A.f32(rows, W, parts=1)
    r_q = A.f32(rows + 4 * W, W, parts=1)
    r_t = A.f32(rows + 8 * W, W, parts=1)
    tmp_ap = lambda par: A.f32(scr["tmp"] + par * 4 * W, W)
    ones_col = C.bf16(kb.ones_col_off, 2)
    ones_row_f = C.f32(kb.ones_row_f_off, 128, parts=1)

    def wt(eng, toks):
        for s, n in toks:
            eng.wait_ge(s.h, n)

    xbf_free = state.get("xbf_free", [])

    @block.vector
    def _(v):
        wt(v, ready)
        wt(v, xbf_free)
        for c in range(KC):
            ins = v.tensor_copy(out=xbf[:, c * W:(c + 1) * W], in_=x_blk[:, c * W:(c + 1) * W])
        dv.inc(ins)
    dv_cast = dv.n

    @block.tensor
    def _(t):
        t.wait_ge(dv.h, dv_cast)
        wt(t, ps_free)
        for c in range(KC):
            ins = t.matmul(ps_s[0:1, 0:W], ones_col[:, 0:1], xbf[:, c * W:(c + 1) * W],
                           start=(c == 0), stop=(c == KC - 1))
        pe.inc(ins)
    pe_sums = pe.n

    @block.vector
    def _(v):
        wt(v, ready)
        v.wait_ge(pe.h, pe_sums)
        for c in range(KC):
            ins = v.tensor_mul(xsq[:, c * W:(c + 1) * W], x_blk[:, c * W:(c + 1) * W],
                               x_blk[:, c * W:(c + 1) * W])
        dv.inc(ins)
    dv_sq = dv.n

    @block.tensor
    def _(t):
        t.wait_ge(dv.h, dv_sq)
        for c in range(KC):
            ins = t.matmul(ps_q[0:1, 0:W], ones_col[:, 0:1], xsq[:, c * W:(c + 1) * W],
                           start=(c == 0), stop=(c == KC - 1))
        pe.inc(ins)
    pe_stats = pe.n
    state["xbf_free"] = [(pe, pe.n)]

    @block.vector
    def _(v):
        v.wait_ge(pe.h, pe_stats)
        v.tensor_copy(out=r_s, in_=ps_s[0:1, 0:W])
        v.tensor_copy(out=r_q, in_=ps_q[0:1, 0:W])
        v.tensor_scalar_mul(r_s, r_s, -1.0 / D)
        v.tensor_scalar_mul(r_q, r_q, 1.0 / D)
        v.tensor_mul(r_t, r_s, r_s)
        dv.inc(v.tensor_sub(r_q, r_q, r_t))
    dv_var = dv.n

    @block.scalar
    def _(s):
        s.wait_ge(dv.h, dv_var)
        s.activation(out=r_q, in_=r_q, func=AF.Ln, bias=EPS, scale=1.0)
        ac.inc(s.activation(out=r_q, in_=r_q, func=AF.Exp, bias=0.0, scale=-0.5))
    ac_rstd = ac.n

    @block.tensor
    def _(t):
        t.wait_ge(ac.h, ac_rstd)
        t.matmul(ps_s[:, 0:W], ones_row_f, r_s, start=True, stop=True)
        pe.inc(t.matmul(ps_q[:, 0:W], ones_row_f, r_q, start=True, stop=True))
    pe_bc = pe.n

    for c in range(KC):
        par = c % 2
        @block.vector
        def _(v, c=c, par=par):
            if c == 0:
                v.wait_ge(pe.h, pe_bc)
            wt(v, tmp_busy.get(par, []))
            tmp = tmp_ap(par)
            v.tensor_add(tmp, x_blk[:, c * W:(c + 1) * W], ps_s[:, 0:W])
            ins = v.tensor_mul(tmp, tmp, ps_q[:, 0:W])
            if gamma is not None:
                g_ap = C.f32(kb.cp_off + 4 * (gamma + c), 1)
                b_ap = C.f32(kb.cp_off + 4 * (beta + c), 1)
                ins = v.tensor_scalar(tmp, tmp, g_ap, b_ap, ALU.mult, ALU.add)
            dv.inc(ins)
        toks = y_put(c, tmp_ap(par), (dv, dv.n))
        tmp_busy[par] = list(toks)
    state["ps_free"] = [(dv, dv.n)]
    state["tmp_busy"] = tmp_busy
    return state


def _dump(kb, name, ap, waits):
    if name not in kb.dbg_list:
        return None
    cs = kb.sem("cs")
    shp = list(ap.shape)
    out = kb.nc.dram_tensor("dbg_" + name, shp, ap.dtype, kind="ExternalOutput").ap()

    @kb.block.sync
    def _(q):
        for s, n in waits:
            q.wait_ge(s.h, n)
        q.dma_start(out=out, in_=ap).then_inc(cs.h, 16)
        cs.n += 16
    return (cs, cs.n)


def _p_qkv(kb, y_off, q_off, k_off, vt_off, wbuf_off, ready=()):
    A, C, block, nc, PS = kb.A, kb.C, kb.block, kb.nc, kb.PS
    cw, pe, ac, dv = kb.sem("cw"), kb.sem("pe"), kb.sem("ac"), kb.sem("dv")
    Wqkv = kb.w["ffn_Wqkv"]

    @block.tensor
    def _(t):
        for s, n in ready:
            t.wait_ge(s.h, n)

    @block.gpsimd
    def _(q):
        for s, n in ready:
            q.wait_ge(s.h, n)

    wb = lambda par: A.bf16(wbuf_off + par * KC * 512 * 2, KC * 512)
    y_ap = lambda ksl: A.bf16(y_off + 2 * ksl * S, S)
    wload = {}
    wfree = [[], []]
    bank_free = [[] for _ in range(4)]

    def emit_load(cb):
        par = cb % 2

        @block.gpsimd
        def _(q):
            for s, n in wfree[par]:
                q.wait_ge(s.h, n)
            src = Wqkv[:, cb * 512:(cb + 1) * 512].rearrange("(k p) c -> p k c", p=128)
            dst = wb(par).rearrange("p (k c) -> p k c", k=KC)
            q.dma_start(out=dst, in_=src).then_inc(cw.h, 16)
            cw.n += 16
        wload[cb] = (cw, cw.n)

    emit_load(0)
    emit_load(1)
    nbank = 0
    for cb in range(6):
        par = cb % 2
        if cb < 4:
            dst_off = q_off if cb < 2 else k_off
            fbase = (cb % 2) * 4
            for m in range(4):
                f = fbase + m
                for n in range(4):
                    bank = nbank % 4
                    nbank += 1

                    @block.tensor
                    def _(t, cb=cb, par=par, m=m, n=n, bank=bank):
                        t.wait_ge(cw.h, wload[cb][1])
                        for s, nn in bank_free[bank]:
                            t.wait_ge(s.h, nn)
                        for k in range(KC):
                            ins = t.matmul(PS[bank][:, 0:512],
                                           wb(par)[:, k * 512 + m * 128:k * 512 + (m + 1) * 128],
                                           y_ap(k)[:, n * 512:(n + 1) * 512],
                                           start=(k == 0), stop=(k == KC - 1))
                        pe.inc(ins)
                    pe_t = pe.n

                    @block.vector
                    def _(v, f=f, n=n, bank=bank, pe_t=pe_t, dst_off=dst_off):
                        v.wait_ge(pe.h, pe_t)
                        dst = kb.A.bf16(dst_off + 2 * (f * S + n * 512), 512)
                        dv.inc(v.tensor_copy(out=dst, in_=PS[bank][:, 0:512]))
                    bank_free[bank] = [(dv, dv.n)]
            wfree[par] = [(pe, pe.n)]
            if cb + 2 < 6:
                emit_load(cb + 2)
        else:
            vb = cb - 4
            for tt in range(16):
                bank = nbank % 4
                nbank += 1

                @block.tensor
                def _(t, cb=cb, par=par, tt=tt, bank=bank):
                    t.wait_ge(cw.h, wload[cb][1])
                    for s, nn in bank_free[bank]:
                        t.wait_ge(s.h, nn)
                    for k in range(KC):
                        ins = t.matmul(PS[bank][:, 0:512],
                                       y_ap(k)[:, tt * 128:(tt + 1) * 128],
                                       wb(par)[:, k * 512:(k + 1) * 512],
                                       start=(k == 0), stop=(k == KC - 1))
                    pe.inc(ins)
                pe_t = pe.n

                @block.vector
                def _(v, tt=tt, vb=vb, bank=bank, pe_t=pe_t):
                    v.wait_ge(pe.h, pe_t)
                    dst = kb.A.bf16(vt_off + 2 * (tt * 1024 + vb * 512), 512)
                    dv.inc(v.tensor_copy(out=dst, in_=PS[bank][:, 0:512]))
                bank_free[bank] = [(dv, dv.n)]
            wfree[par] = [(pe, pe.n)]
            if cb + 2 < 6:
                emit_load(cb + 2)

    kb.qkv_done = [(ac, ac.n)]


def _p_r(kb):
    A, C, block, nc, PS = kb.A, kb.C, kb.block, kb.nc, kb.PS
    cw, ca, cs, pe, ac, dv = (kb.sem(n) for n in ["cw", "ca", "cs", "pe", "ac", "dv"])
    A.mark()
    peT_off = A.alloc(KC * S * 2)
    wbuf_off = A.alloc(2 * KC * 512 * 2)
    rst_off = A.alloc(2 * 512 * 2)
    wb = lambda par: A.bf16(wbuf_off + par * KC * 512 * 2, KC * 512)
    peT = lambda k: A.bf16(peT_off + 2 * k * S, S)

    @block.gpsimd
    def _(q):
        for k in range(KC):
            q.dma_start(out=peT(k), in_=kb.pe_T[k * 128:(k + 1) * 128, :]).then_inc(ca.h, 16)
            ca.n += 16
        for cb in range(2):
            src = kb.w["ffn_Wr"][:, cb * 512:(cb + 1) * 512].rearrange("(k p) c -> p k c", p=128)
            q.dma_start(out=wb(cb).rearrange("p (k c) -> p k c", k=KC), in_=src).then_inc(cw.h, 16)
            cw.n += 16
    loads = (ca.n, cw.n)
    st_free = [[], []]

    for cb in range(2):
        for m in range(4):
            f = cb * 4 + m
            for n in range(4):
                bank = (f * 4 + n) % 4

                @block.tensor
                def _(t, cb=cb, m=m, n=n, bank=bank):
                    t.wait_ge(ca.h, loads[0])
                    t.wait_ge(cw.h, loads[1])
                    for s, nn in kb.r_bank_free[bank]:
                        t.wait_ge(s.h, nn)
                    for k in range(KC):
                        ins = t.matmul(PS[bank][:, 0:512],
                                       wb(cb)[:, k * 512 + m * 128:k * 512 + (m + 1) * 128],
                                       peT(k)[:, n * 512:(n + 1) * 512],
                                       start=(k == 0), stop=(k == KC - 1))
                    pe.inc(ins)
                pe_t = pe.n
                par = n % 2

                @block.vector
                def _(v, bank=bank, par=par, pe_t=pe_t):
                    v.wait_ge(pe.h, pe_t)
                    for s, nn in st_free[par]:
                        v.wait_ge(s.h, nn)
                    dv.inc(v.tensor_copy(out=A.bf16(rst_off + par * 512 * 2, 512),
                                         in_=PS[bank][:, 0:512]))
                kb.r_bank_free[bank] = [(dv, dv.n)]
                ac_t = dv.n

                @block.sync
                def _(q, f=f, n=n, par=par, ac_t=ac_t):
                    q.wait_ge(dv.h, ac_t)
                    q.dma_start(out=kb.ds["rbuf"].ap()[f * 128:(f + 1) * 128, n * 512:(n + 1) * 512],
                                in_=A.bf16(rst_off + par * 512 * 2, 512)).then_inc(cs.h, 16)
                    cs.n += 16
                st_free[par] = [(cs, cs.n)]
    A.release()
    kb.r_done = [(cs, cs.n)]


def _xl_scr(kb):
    A = kb.A
    return {
        "u2h": A.alloc(S * 2), "v2h": A.alloc(S * 2), "rh": A.alloc(S * 2),
        "sc": A.alloc(3 * S * 4), "bdst": A.alloc(2 * S * 2),
        "p": A.alloc(3 * S * 2), "pT": A.alloc(4 * 1024 * 2),
        "cst": A.alloc(2 * S * 2), "nm": A.alloc(32 * 4),
    }


def _p_xl(kb, q_off, k_off, vt_off, ctx_off, scr, ready):
    A, C, block, nc, PS, PT = kb.A, kb.C, kb.block, kb.nc, kb.PS, kb.PT
    pe, dv, ac = kb.sem("pe"), kb.sem("dv"), kb.sem("ac")
    ca, cs, cb, gp = kb.sem("ca"), kb.sem("cs"), kb.sem("cb"), kb.sem("gp")
    bdscr = kb.ds["bdscr"]
    rbuf = kb.ds["rbuf"]
    ident = C.bf16(kb.ident_off, 128)

    u2h = lambda bb: A.bf16(scr["u2h"], S, parts=64, p0=bb)
    v2h = lambda bb: A.bf16(scr["v2h"], S, parts=64, p0=bb)
    rh = lambda bb: A.bf16(scr["rh"], S, parts=64, p0=bb)
    sc_ap = lambda sl, o, w: A.f32(scr["sc"] + sl * S * 4 + o * 4, w)
    bdst_ap = lambda par, o, w: A.bf16(scr["bdst"] + par * S * 2 + o * 2, w)
    p_ap = lambda par, o, w: A.bf16(scr["p"] + par * S * 2 + o * 2, w)
    pT_ap = lambda sl, w: A.bf16(scr["pT"] + sl * 1024 * 2, w)
    cst = lambda hp: A.bf16(scr["cst"] + hp * S * 2, S, parts=64)
    nm_ap = lambda i, w=1: A.f32(scr["nm"] + i * 4, w)

    NU = NH * QT
    first = {"t": True, "v": True, "s": True, "g": True, "sq": True}

    def wt(eng, toks, key=None):
        if key is not None and first[key]:
            first[key] = False
            toks = list(toks) + list(ready)
        for s, n in toks:
            eng.wait_ge(s.h, n)

    bd_bank = [[], []]
    ac_bank = [[], []]
    bdst_st = [[], []]
    slot_st = [[] for _ in range(BD_SLOTS)]
    p_st = [[], [], []]
    sc_st = [[], [], []]
    PTb_st = [[], []]
    pT_st = [[], [], [], []]
    ctx_col = [[], []]
    head_done_pe = {}
    cst_tok = {}
    prep_tok = {}
    U = [dict() for _ in range(NU)]
    gidx = 0

    def jblocks(kt):
        out, o = [], S - kt
        while o < S:
            w = min(512, S - o)
            out.append((o, w))
            o += w
        return out

    def front(u):
        h, t = divmod(u, QT)
        bb = (h % 2) * 64
        kt = 128 * (t + 1)
        st = U[u]
        st.update(h=h, t=t, bb=bb, kt=kt, par=u % 3, scs=u % 3, par2=u % 2,
                  slot=u % BD_SLOTS, hc=h // 2)
        if t == 0:
            gate = list(head_done_pe.get(h - 2, []))

            @block.vector
            def _(v, h=h, bb=bb, gate=gate):
                wt(v, gate, "v")
                hc = h // 2
                qsl = A.bf16(q_off + 2 * (hc * S), S, parts=64, p0=bb)
                ub = C.f32(kb.cp_off + 4 * (CP["u_b"] + hc), 1, parts=64, p0=bb)
                vb = C.f32(kb.cp_off + 4 * (CP["v_b"] + hc), 1, parts=64, p0=bb)
                v.tensor_scalar(u2h(bb), qsl, ub, None, ALU.add)
                dv.inc(v.tensor_scalar(v2h(bb), qsl, vb, None, ALU.add))
            uv_tok = (dv, dv.n)

            @block.scalar
            def _(s, h=h, bb=bb, gate=gate):
                wt(s, gate, "s")
                s.dma_start(out=rh(bb), in_=rbuf.ap()[h * DH:(h + 1) * DH, :]).then_inc(ca.h, 16)
                ca.n += 16
            prep_tok[h] = [uv_tok, (ca, ca.n)]

        jb = jblocks(kt)
        evac_toks = []
        for bi, (jo, wb_) in enumerate(jb):
            bank = bi % 2

            @block.tensor
            def _(tq, u=u, t=t, bb=bb, jo=jo, wb_=wb_, bank=bank, bi=bi, h=h):
                toks = list(bd_bank[bank])
                if bi == 0:
                    toks += prep_tok[h]
                wt(tq, toks, "t")
                pe.inc(tq.matmul(PS[bank][:, 0:wb_],
                                 v2h(bb)[:, t * 128:(t + 1) * 128],
                                 rh(bb)[:, jo:jo + wb_], start=True, stop=True))
            pe_t = pe.n

            @block.vector
            def _(v, jo=jo, wb_=wb_, bank=bank, pe_t=pe_t, kt=kt, par2=u % 2, bi=bi):
                toks = [(pe, pe_t)]
                if bi == 0:
                    toks += bdst_st[par2]
                wt(v, toks, "v")
                dv.inc(v.tensor_copy(out=bdst_ap(par2, jo - (S - kt), wb_),
                                     in_=PS[bank][:, 0:wb_]))
            bd_bank[bank] = [(dv, dv.n)]
        st["bdev"] = (dv, dv.n)

        @block.sync
        def _(q, u=u, kt=kt, par2=u % 2, slot=u % BD_SLOTS):
            wt(q, [U[u]["bdev"]] + slot_st[slot], "sq")
            q.dma_start(out=bdscr.ap()[slot, :, S - kt:S],
                        in_=bdst_ap(par2, 0, kt)).then_inc(cs.h, 16)
            cs.n += 16
        bdst_st[u % 2] = [(cs, cs.n)]
        st["bdwr"] = (cs, cs.n)

    def mid(u):
        st = U[u]
        kt, par, slot = st["kt"], st["par"], st["slot"]

        @block.gpsimd
        def _(g, u=u, kt=kt, par=par, slot=slot):
            toks = [U[u]["bdwr"]] + p_st[par]
            if kb.xl_first and u < BD_SLOTS:
                toks = toks + [kb.bdtail_tok]
            wt(g, toks, "g")
            src = bass.AP(bdscr, slot * 128 * BD_W + (S - 1 - 128 * st["t"]),
                          [[BD_W - 1, 128], [1, kt]])
            g.dma_start(out=p_ap(par, 0, kt), in_=src).then_inc(cb.h, 16)
            cb.n += 16
        st["bdrd"] = (cb, cb.n)

    def back1(u):
        st = U[u]
        h, t, bb, kt, par, scs = st["h"], st["t"], st["bb"], st["kt"], st["par"], st["scs"]
        hc = st["hc"]
        nblk = (kt + 511) // 512
        ttr_done = []
        for bi in range(nblk):
            ko = bi * 512
            wb_ = min(512, kt - ko)
            bank = bi % 2

            @block.tensor
            def _(tq, t=t, ko=ko, wb_=wb_, bank=bank, bb=bb, hc=hc):
                wt(tq, ac_bank[bank], "t")
                pe.inc(tq.matmul(PS[2 + bank][:, 0:wb_],
                                 u2h(bb)[:, t * 128:(t + 1) * 128],
                                 A.bf16(k_off, KC * S, parts=64, p0=bb)[:, hc * S + ko:hc * S + ko + wb_],
                                 start=True, stop=True))
            pe_t = pe.n

            @block.vector
            def _(v, u=u, bi=bi, ko=ko, wb_=wb_, bank=bank, pe_t=pe_t, par=par, scs=scs):
                toks = [(pe, pe_t)]
                if bi == 0:
                    toks = toks + [U[u]["bdrd"]] + sc_st[scs]
                wt(v, toks, "v")
                dv.inc(v.tensor_tensor(out=sc_ap(scs, ko, wb_), in0=PS[2 + bank][:, 0:wb_],
                                       in1=p_ap(par, ko, wb_), op=ALU.add))
            ac_bank[bank] = [(dv, dv.n)]
            ttr_done.append((dv, dv.n))
        st["ttr"] = (dv, dv.n)
        if t == QT - 1:
            head_done_pe[h] = [(pe, pe.n)]

        @block.vector
        def _(v, par=par, scs=scs, kt=kt):
            dv.inc(v.reduce_max(nm_ap(18 + par), sc_ap(scs, 0, kt), axis=AX.X))
            v.wait_ge(dv.h, dv.n)
            dv.inc(v.tensor_scalar_mul(nm_ap(21 + par), nm_ap(18 + par), -0.125))
        st["negm"] = (dv, dv.n)

        @block.scalar
        def _(sq, u=u, par=par, scs=scs, kt=kt):
            wt(sq, [U[u]["negm"]], "s")
            ac.inc(sq.activation(out=p_ap(par, 0, kt), in_=sc_ap(scs, 0, kt), func=AF.Exp,
                                 bias=nm_ap(21 + par), scale=0.125,
                                 accum_out=nm_ap(24 + par)))
        sc_st[scs] = [(ac, ac.n)]
        st["exp"] = (ac, ac.n)

    def back2(u):
        nonlocal gidx
        st = U[u]
        h, t, bb, kt, par = st["h"], st["t"], st["bb"], st["kt"], st["par"]
        hc, hp = st["hc"], h % 2

        @block.vector
        def _(v, u=u, par=par, kt=kt):
            v.wait_ge(ac.h, U[u]["exp"][1])
            dv.inc(v.reciprocal(nm_ap(27 + par), nm_ap(24 + par)))
            v.wait_ge(dv.h, dv.n)
            dv.inc(v.tensor_scalar(p_ap(par, 0, kt), p_ap(par, 0, kt), nm_ap(27 + par),
                                   None, ALU.mult))
        psc_tok = (dv, dv.n)

        ngrp = (t + 1 + 7) // 8
        grp_sl = []
        for g_ in range(ngrp):
            n_in_g = min(8, t + 1 - g_ * 8)
            bankb = gidx % 2
            sl = gidx % 4
            gidx += 1

            @block.tensor
            def _(tq, g_=g_, n_in_g=n_in_g, bankb=bankb, par=par, psc_tok=psc_tok):
                wt(tq, [psc_tok] + PTb_st[bankb], "t")
                for i in range(n_in_g):
                    kbk = g_ * 8 + i
                    ins = tq.transpose(PT[bankb][:, i * 128:(i + 1) * 128],
                                       p_ap(par, kbk * 128, 128), ident)
                pe.inc(ins)
            pe_tr = pe.n

            @block.vector
            def _(v, sl=sl, n_in_g=n_in_g, bankb=bankb, pe_tr=pe_tr):
                wt(v, [(pe, pe_tr)] + pT_st[sl])
                dv.inc(v.tensor_copy(out=pT_ap(sl, n_in_g * 128),
                                     in_=PT[bankb][:, 0:n_in_g * 128]))
            PTb_st[bankb] = [(dv, dv.n)]
            grp_sl.append((sl, (dv, dv.n)))
        p_st[par] = [(pe, pe.n)]

        chalf = u % 2
        c0 = chalf * 128

        @block.tensor
        def _(tq, t=t, h=h, grp_sl=grp_sl, c0=c0, chalf=chalf):
            for sl, tok in grp_sl:
                tq.wait_ge(tok[0].h, tok[1])
            wt(tq, ctx_col[chalf])
            for kbk in range(t + 1):
                sl = grp_sl[kbk // 8][0]
                ins = tq.matmul(PS[4][0:64, c0:c0 + 128],
                                A.bf16(vt_off + 2 * (kbk * 1024 + h * DH), DH),
                                pT_ap(sl, 1024)[:, (kbk % 8) * 128:(kbk % 8 + 1) * 128],
                                start=(kbk == 0), stop=(kbk == t))
            pe.inc(ins)
        pe_ctx = pe.n
        for kbk in range(t + 1):
            sl = grp_sl[kbk // 8][0]
            pT_st[sl] = [(pe, pe.n)]
        p_st[par] = [(pe, pe.n)]

        @block.vector
        def _(v, t=t, hp=hp, pe_ctx=pe_ctx, c0=c0, h=h):
            toks = [(pe, pe_ctx)]
            if t == 0:
                toks = toks + list(cst_tok.get(h - 2, []))
            wt(v, toks)
            dv.inc(v.tensor_copy(out=cst(hp)[:, t * 128:(t + 1) * 128],
                                 in_=PS[4][0:64, c0:c0 + 128]))
        ctx_col[chalf] = [(dv, dv.n)]

        if t == QT - 1:
            dv_last = dv.n

            @block.scalar
            def _(s, h=h, bb=bb, hc=hc, hp=hp, dv_last=dv_last):
                s.wait_ge(dv.h, dv_last)
                dst = A.bf16(ctx_off + 2 * (hc * S), S, parts=64, p0=bb)
                s.dma_start(out=dst, in_=cst(hp)[:, :]).then_inc(cs.h, 16)
                cs.n += 16
            cst_tok[h] = [(cs, cs.n)]

    for it in range(NU + 3):
        if 3 <= it < NU + 3:
            back2(it - 3)
        if 2 <= it < NU + 2:
            back1(it - 2)
        if 1 <= it < NU + 1:
            mid(it - 1)
        if it < NU:
            front(it)
        if 1 <= it < NU + 1:
            uu = it - 1
            slot_st[uu % BD_SLOTS] = [U[uu]["bdrd"]]

    kb.xl_done = [(cs, cs.n), (dv, dv.n)]


def _p_woh(kb, elem, ctx_off, ready):
    A, C, block, nc, PS = kb.A, kb.C, kb.block, kb.nc, kb.PS
    cw, ca, cs, pe, dv, ac = (kb.sem(n) for n in ["cw", "ca", "cs", "pe", "dv", "ac"])
    A.mark()
    wo_off = A.alloc(KC * D * 2)
    y32_off = A.alloc(KC * 512 * 4)
    s2_off = A.alloc(KC * 512 * 4)
    h1c_off = A.alloc(2 * 512 * 2)
    scr = ln_scr_alloc(A)
    wo = A.bf16(wo_off, KC * D)
    y32 = A.f32(y32_off, KC * 512)
    s2 = A.f32(s2_off, KC * 512)

    @block.gpsimd
    def _(q):
        for s, n in ready:
            q.wait_ge(s.h, n)
        src = kb.w["ffn_Wo"].rearrange("(k p) c -> p k c", p=128)
        q.dma_start(out=wo.rearrange("p (k c) -> p k c", k=KC), in_=src).then_inc(cw.h, 16)
        cw.n += 16
    wo_tok = (cw, cw.n)

    state = {"ps_free": list(ready)}
    bank_free = [[] for _ in range(4)]
    y32_free = []
    s2_free = []
    h1c_busy = [[], []]
    for tb in range(S // 512):
        @block.sync
        def _(q, tb=tb, elem=elem):
            for s, n in y32_free:
                q.wait_ge(s.h, n)
            src = kb.ds["ybuf"].ap()[elem][:, tb * 512:(tb + 1) * 512].rearrange(
                "(c p) w -> p c w", p=128)
            q.dma_start(out=y32.rearrange("p (c w) -> p c w", c=KC), in_=src).then_inc(ca.h, 16)
            ca.n += 16
        y_tok = (ca, ca.n)

        dc_toks = []
        for dc in range(KC):
            bank = dc % 4

            @block.tensor
            def _(t, tb=tb, dc=dc, bank=bank):
                if tb == 0 and dc == 0:
                    t.wait_ge(cw.h, wo_tok[1])
                    for s, n in ready:
                        t.wait_ge(s.h, n)
                for s, n in bank_free[bank]:
                    t.wait_ge(s.h, n)
                for k in range(KC):
                    ins = t.matmul(PS[bank][:, 0:512],
                                   wo[:, k * D + dc * 128:k * D + (dc + 1) * 128],
                                   A.bf16(ctx_off + 2 * (k * S + tb * 512), 512),
                                   start=(k == 0), stop=(k == KC - 1))
                pe.inc(ins)
            pe_t = pe.n

            @block.vector
            def _(v, dc=dc, bank=bank, pe_t=pe_t, y_tok=y_tok):
                v.wait_ge(pe.h, pe_t)
                v.wait_ge(ca.h, y_tok[1])
                for s, n in s2_free:
                    v.wait_ge(s.h, n)
                dv.inc(v.tensor_add(s2[:, dc * 512:(dc + 1) * 512],
                                    PS[bank][:, 0:512], y32[:, dc * 512:(dc + 1) * 512]))
            bank_free[bank] = [(dv, dv.n)]
            dc_toks.append((dv, dv.n))
        y32_free = [(dv, dv.n)]
        s2_free = []

        def y_put(c, tmp, tok, tb=tb):
            h1c = A.bf16(h1c_off + (c % 2) * 512 * 2, 512)
            toks = []

            @block.vector
            def _(v, c=c):
                for s, n in h1c_busy[c % 2]:
                    v.wait_ge(s.h, n)
                dv.inc(v.tensor_copy(out=h1c, in_=tmp))
            bf_tok = (dv, dv.n)

            @block.sync
            def _(q, c=c, bf_tok=bf_tok, tok=tok):
                q.wait_ge(tok[0].h, tok[1])
                q.dma_start(out=kb.ds["h1f"].ap()[elem, c * 128:(c + 1) * 128,
                                                  tb * 512:(tb + 1) * 512],
                            in_=tmp).then_inc(cs.h, 16)
                cs.n += 16
                toks.append((cs, cs.n))
                q.wait_ge(bf_tok[0].h, bf_tok[1])
                q.dma_start(out=kb.ds["h1b"].ap()[elem, c * 128:(c + 1) * 128,
                                                  tb * 512:(tb + 1) * 512],
                            in_=h1c).then_inc(cs.h, 16)
                cs.n += 16
            h1c_busy[c % 2] = [(cs, cs.n)]
            return toks

        state = _ln_block(kb, s2, scr, y_put, gamma=CP["ln1_g"], beta=CP["ln1_b"],
                          ready=dc_toks, state=state)
        s2_free = [(dv, kb.sem("dv").n), (ac, kb.sem("ac").n), (pe, kb.sem("pe").n)]
    A.release()
    kb.ln_ps_free = state["ps_free"]
    kb.woh_done = [(cs, cs.n)]


def _p_ffg(kb, ready):
    A, C, block, nc, PS = kb.A, kb.C, kb.block, kb.nc, kb.PS
    cw, ca, cs, pe, dv, ac = (kb.sem(n) for n in ["cw", "ca", "cs", "pe", "dv", "ac"])
    A.mark()
    w1_off = A.alloc(KC * FFD * 2)
    w2_off = A.alloc(FFC * D * 2)
    hid_off = A.alloc(FFC * 512 * 2)
    h1b_off = A.alloc(KC * 512 * 2)
    h132_off = A.alloc(2 * 512 * 4)
    s2_off = A.alloc(KC * 512 * 4)
    h2c_off = A.alloc(2 * 512 * 2)
    scr = {"xbf": hid_off, "rows": A.alloc(3 * LNW * 4), "tmp": A.alloc(2 * LNW * 4)}
    w1 = A.bf16(w1_off, KC * FFD)
    w2 = A.bf16(w2_off, FFC * D)
    hid = A.bf16(hid_off, FFC * 512)
    h1btb = A.bf16(h1b_off, KC * 512)
    s2 = A.f32(s2_off, KC * 512)

    @block.gpsimd
    def _(q):
        for s, n in ready:
            q.wait_ge(s.h, n)
        q.dma_start(out=w1.rearrange("p (k c) -> p k c", k=KC),
                    in_=kb.w["ffn_Wff1"].rearrange("(k p) c -> p k c", p=128)).then_inc(cw.h, 16)
        cw.n += 16
        q.dma_start(out=w2.rearrange("p (f c) -> p f c", f=FFC),
                    in_=kb.w["ffn_Wff2"].rearrange("(f p) c -> p f c", p=128)).then_inc(cw.h, 16)
        cw.n += 16
    w_tok = (cw, cw.n)

    state = {"ps_free": list(ready)}
    bank_free = [[] for _ in range(4)]
    h1b_free = list(ready)
    h132_free = [[], []]
    s2_free = []
    h2c_busy = [[], []]
    hid_writable = []

    @block.tensor
    def _(t):
        for s, n in ready:
            t.wait_ge(s.h, n)
        t.wait_ge(cw.h, w_tok[1])

    for tb in range(NTB):
        elem, tbe = tb // 4, tb % 4

        @block.sync
        def _(q, elem=elem, tbe=tbe):
            for s, n in h1b_free:
                q.wait_ge(s.h, n)
            src = kb.ds["h1b"].ap()[elem, :, tbe * 512:(tbe + 1) * 512].rearrange(
                "(c p) w -> p c w", p=128)
            q.dma_start(out=h1btb.rearrange("p (c w) -> p c w", c=KC), in_=src).then_inc(ca.h, 16)
            ca.n += 16
        h1b_tok = (ca, ca.n)

        hid_toks = []
        for ffc in range(FFC):
            bank = ffc % 4

            @block.tensor
            def _(t, ffc=ffc, bank=bank, h1b_tok=h1b_tok):
                if ffc == 0:
                    t.wait_ge(ca.h, h1b_tok[1])
                for s, n in bank_free[bank]:
                    t.wait_ge(s.h, n)
                for k in range(KC):
                    ins = t.matmul(PS[bank][:, 0:512],
                                   w1[:, k * FFD + ffc * 128:k * FFD + (ffc + 1) * 128],
                                   h1btb[:, k * 512:(k + 1) * 512],
                                   start=(k == 0), stop=(k == KC - 1))
                pe.inc(ins)
            pe_t = pe.n

            @block.vector
            def _(v, ffc=ffc, bank=bank, pe_t=pe_t, hw=list(hid_writable)):
                v.wait_ge(pe.h, pe_t)
                for s, n in hw:
                    v.wait_ge(s.h, n)
                b_ap = C.f32(kb.cp_off + 4 * (CP["bff1"] + ffc), 1)
                dv.inc(v.tensor_scalar(hid[:, ffc * 512:(ffc + 1) * 512],
                                       PS[bank][:, 0:512], b_ap, 0.0,
                                       ALU.add, ALU.max))
            bank_free[bank] = [(dv, dv.n)]
            hid_toks.append((dv, dv.n))
        h1b_free = [(pe, pe.n)]

        dc_toks = []
        for dc in range(KC):
            bank = dc % 4
            par = dc % 2

            @block.sync
            def _(q, elem=elem, tbe=tbe, dc=dc, par=par):
                for s, n in h132_free[par]:
                    q.wait_ge(s.h, n)
                q.dma_start(out=A.f32(h132_off + par * 512 * 4, 512),
                            in_=kb.ds["h1f"].ap()[elem, dc * 128:(dc + 1) * 128,
                                                  tbe * 512:(tbe + 1) * 512]).then_inc(ca.h, 16)
                ca.n += 16
            h132_tok = (ca, ca.n)

            @block.tensor
            def _(t, dc=dc, bank=bank, hid_toks=list(hid_toks)):
                if dc == 0:
                    for s, n in hid_toks:
                        t.wait_ge(s.h, n)
                for s, n in bank_free[bank]:
                    t.wait_ge(s.h, n)
                for ffc in range(FFC):
                    ins = t.matmul(PS[bank][:, 0:512],
                                   w2[:, ffc * D + dc * 128:ffc * D + (dc + 1) * 128],
                                   hid[:, ffc * 512:(ffc + 1) * 512],
                                   start=(ffc == 0), stop=(ffc == FFC - 1))
                pe.inc(ins)
            pe_t = pe.n

            @block.vector
            def _(v, dc=dc, bank=bank, par=par, pe_t=pe_t, h132_tok=h132_tok):
                v.wait_ge(pe.h, pe_t)
                v.wait_ge(ca.h, h132_tok[1])
                for s, n in s2_free:
                    v.wait_ge(s.h, n)
                b2 = C.f32(kb.cp_off + 4 * (CP["bff2"] + dc), 1)
                dv.inc(v.scalar_tensor_tensor(s2[:, dc * 512:(dc + 1) * 512],
                                              PS[bank][:, 0:512], b2,
                                              A.f32(h132_off + par * 512 * 4, 512),
                                              ALU.add, ALU.add))
            bank_free[bank] = [(dv, dv.n)]
            h132_free[par] = [(dv, dv.n)]
            dc_toks.append((dv, dv.n))
        s2_free = []

        def y_put(c, tmp, tok, tb=tb, elem=elem, tbe=tbe):
            h2c = A.bf16(h2c_off + (c % 2) * 512 * 2, 512)
            toks = []

            @block.vector
            def _(v, c=c):
                for s, n in h2c_busy[c % 2]:
                    v.wait_ge(s.h, n)
                dv.inc(v.tensor_copy(out=h2c, in_=tmp))
            bf_tok = (dv, dv.n)

            @block.scalar
            def _(q, c=c, bf_tok=bf_tok, tok=tok):
                q.wait_ge(tok[0].h, tok[1])
                q.dma_start(out=kb.ds["h2f"].ap()[elem, c * 128:(c + 1) * 128,
                                                  tbe * 512:(tbe + 1) * 512],
                            in_=tmp).then_inc(cs.h, 16)
                cs.n += 16
                toks.append((cs, cs.n))
                q.wait_ge(bf_tok[0].h, bf_tok[1])
                q.dma_start(out=kb.ds["h2b"].ap()[elem, c * 128:(c + 1) * 128,
                                                  tbe * 512:(tbe + 1) * 512],
                            in_=h2c).then_inc(cs.h, 16)
                cs.n += 16
            h2c_busy[c % 2] = [(cs, cs.n)]
            return toks

        state = _ln_block(kb, s2, scr, y_put, gamma=CP["ln2_g"], beta=CP["ln2_b"],
                          ready=dc_toks, state=state)
        s2_free = [(dv, kb.sem("dv").n), (ac, kb.sem("ac").n)]
        hid_writable = state["xbf_free"]
    A.release()
    kb.ln_ps_free = state["ps_free"]
    kb.ffg_done = [(cs, cs.n)]


def _p_gate(kb, ready):
    A, C, block, nc, PS = kb.A, kb.C, kb.block, kb.nc, kb.PS
    cw, ca, cs, pe, dv, ac = (kb.sem(n) for n in ["cw", "ca", "cs", "pe", "dv", "ac"])
    A.mark()
    wg_off = A.alloc(KC * NE * 2)
    h2b_off = A.alloc(KC * 512 * 2)
    lg_off = A.alloc(512 * 4)
    gt_off = A.alloc(32 * 4)
    gg_off = A.alloc(32 * 4)
    sm_off = A.alloc(16 * 4)
    gr_off = A.alloc(512 * 2)
    wg = A.bf16(wg_off, KC * NE)
    h2btb = A.bf16(h2b_off, KC * 512)
    lg = A.f32(lg_off, 512, parts=NE)
    gt = A.f32(gt_off, 32)
    gg = A.f32(gg_off, 32)
    gr = A.bf16(gr_off, 512, parts=NE)
    identf = C.f32(kb.identf_off, 128)
    gbf = [[] for _ in range(4)]

    @block.gpsimd
    def _(q):
        for s, n in ready:
            q.wait_ge(s.h, n)
        q.dma_start(out=wg.rearrange("p (k c) -> p k c", k=KC),
                    in_=kb.w["ffn_Wg"].rearrange("(k p) c -> p k c", p=128)).then_inc(cw.h, 16)
        cw.n += 16
    wg_tok = (cw, cw.n)

    h2b_free = list(ready)
    lg_free = []
    gr_free = []
    for tb in range(NTB):
        elem, tbe = tb // 4, tb % 4
        bankA = tb % 2
        bankB = 2 + tb % 2

        @block.sync
        def _(q, elem=elem, tbe=tbe):
            for s, n in h2b_free:
                q.wait_ge(s.h, n)
            src = kb.ds["h2b"].ap()[elem, :, tbe * 512:(tbe + 1) * 512].rearrange(
                "(c p) w -> p c w", p=128)
            q.dma_start(out=h2btb.rearrange("p (c w) -> p c w", c=KC), in_=src).then_inc(ca.h, 16)
            ca.n += 16
        h_tok = (ca, ca.n)

        @block.tensor
        def _(t, bankA=bankA, h_tok=h_tok, tb=tb):
            t.wait_ge(ca.h, h_tok[1])
            if tb == 0:
                t.wait_ge(cw.h, wg_tok[1])
                for s, n in ready:
                    t.wait_ge(s.h, n)
            for s, n in gbf[bankA]:
                t.wait_ge(s.h, n)
            for k in range(KC):
                ins = t.matmul(PS[bankA][0:NE, 0:512], wg[:, k * NE:(k + 1) * NE],
                               h2btb[:, k * 512:(k + 1) * 512],
                               start=(k == 0), stop=(k == KC - 1))
            pe.inc(ins)
        pe_lg = pe.n
        h2b_free = [(pe, pe.n)]

        @block.vector
        def _(v, pe_lg=pe_lg, lgf=list(lg_free), bankA=bankA):
            v.wait_ge(pe.h, pe_lg)
            for s, n in lgf:
                v.wait_ge(s.h, n)
            dv.inc(v.tensor_copy(out=lg, in_=PS[bankA][0:NE, 0:512]))
        ac_lg = dv.n
        gbf[bankA] = [(dv, dv.n)]

        @block.tensor
        def _(t, bankB=bankB, ac_lg=ac_lg):
            t.wait_ge(dv.h, ac_lg)
            for s, n in gbf[bankB]:
                t.wait_ge(s.h, n)
            for g in range(4):
                ins = t.transpose(PS[bankB][:, g * 8:(g + 1) * 8],
                                  lg[:, g * 128:(g + 1) * 128], identf[0:NE, 0:NE])
            pe.inc(ins)
        pe_tr = pe.n
        lg_free = [(pe, pe.n)]

        @block.vector
        def _(v, bankB=bankB, pe_tr=pe_tr):
            v.wait_ge(pe.h, pe_tr)
            dv.inc(v.tensor_copy(out=gt, in_=PS[bankB][:, 0:32]))
        ac_gt = dv.n
        gbf[bankB] = [(dv, dv.n)]

        for g in range(4):
            @block.vector
            def _(v, g=g, ac_gt=ac_gt):
                if g == 0:
                    v.wait_ge(dv.h, ac_gt)
                gsl = gt[:, g * 8:(g + 1) * 8]
                dv.inc(v.reduce_max(A.f32(sm_off, 1), gsl, axis=AX.X))
                v.wait_ge(dv.h, dv.n)
                dv.inc(v.tensor_scalar_mul(A.f32(sm_off + 4, 1), A.f32(sm_off, 1), -1.0))
            dv_mx = dv.n

            @block.scalar
            def _(sq, g=g, dv_mx=dv_mx):
                sq.wait_ge(dv.h, dv_mx)
                ac.inc(sq.activation(out=gg[:, g * 8:(g + 1) * 8],
                                     in_=gt[:, g * 8:(g + 1) * 8], func=AF.Exp,
                                     bias=A.f32(sm_off + 4, 1), scale=1.0,
                                     accum_out=A.f32(sm_off + 8, 1)))
            ac_ex = ac.n

            @block.vector
            def _(v, g=g, ac_ex=ac_ex):
                v.wait_ge(ac.h, ac_ex)
                dv.inc(v.reciprocal(A.f32(sm_off + 12, 1), A.f32(sm_off + 8, 1)))
                v.wait_ge(dv.h, dv.n)
                gsl = gt[:, g * 8:(g + 1) * 8]
                v.tensor_scalar(gg[:, g * 8:(g + 1) * 8], gsl, A.f32(sm_off, 1), None,
                                ALU.is_equal)
                dv.inc(v.tensor_scalar(gg[:, g * 8:(g + 1) * 8], gg[:, g * 8:(g + 1) * 8],
                                       A.f32(sm_off + 12, 1), None, ALU.mult))
            dv_gg = dv.n

        @block.tensor
        def _(t, bankB=bankB, dv_gg=dv_gg):
            t.wait_ge(dv.h, dv_gg)
            for s, n in gbf[bankB]:
                t.wait_ge(s.h, n)
            for g in range(4):
                ins = t.transpose(PS[bankB][0:NE, g * 128:(g + 1) * 128],
                                  gg[:, g * 8:(g + 1) * 8], identf)
            pe.inc(ins)
        pe_gb = pe.n

        @block.vector
        def _(v, bankB=bankB, pe_gb=pe_gb, grf=list(gr_free)):
            v.wait_ge(pe.h, pe_gb)
            for s, n in grf:
                v.wait_ge(s.h, n)
            dv.inc(v.tensor_copy(out=gr, in_=PS[bankB][0:NE, 0:512]))
        ac_gr = dv.n
        gbf[bankB] = [(dv, dv.n)]

        @block.sync
        def _(q, tb=tb, ac_gr=ac_gr):
            q.wait_ge(dv.h, ac_gr)
            q.dma_start(out=kb.ds["gall"].ap()[:, tb * 512:(tb + 1) * 512],
                        in_=gr).then_inc(cs.h, 16)
            cs.n += 16
        gr_free = [(cs, cs.n)]
    A.release()
    kb.gate_done = [(cs, cs.n)]


def _p_moe(kb, ready):
    A, C, block, nc, PS = kb.A, kb.C, kb.block, kb.nc, kb.PS
    cw, ca, cs, cb, pe, dv, ac = (kb.sem(n) for n in ["cw", "ca", "cs", "cb", "pe", "dv", "ac"])
    A.mark()
    w1_off = A.alloc(KC * FFD * 2)
    w2_off = A.alloc(FFC * D * 2)
    hid_off = A.alloc(FFC * 512 * 2)
    h2tb_off = A.alloc(KC * 512 * 2)
    gb_off = A.alloc(2 * 512 * 2)
    stage_off = A.alloc(KC * 512 * 4)
    be1_off = A.alloc(FFD * 2)
    ge_off = A.alloc(2 * 512 * 2)
    ga_off = A.alloc(2 * 512 * 2)
    be2_off = A.alloc(D * 2)
    w1 = A.bf16(w1_off, KC * FFD)
    w2 = A.bf16(w2_off, FFC * D)
    hid = A.bf16(hid_off, FFC * 512)
    h2tb = A.bf16(h2tb_off, KC * 512)
    gb_ap = lambda p_: A.bf16(gb_off + p_ * 512 * 2, 512)
    stage = A.f32(stage_off, KC * 512)
    be1r = A.bf16(be1_off, FFD, parts=1)
    ge_ap = lambda p_: A.bf16(ge_off + p_ * 512 * 2, 512, parts=1)
    ga_ap = lambda p_: A.bf16(ga_off + p_ * 512 * 2, 512, parts=NE)
    be2 = A.bf16(be2_off, D, parts=NE)
    ones_row = C.bf16(kb.ones_row_bf_off, 128, parts=1)

    @block.gpsimd
    def _(q):
        for s, n in ready:
            q.wait_ge(s.h, n)
        q.dma_start(out=be2, in_=kb.w["ffn_be2"][:, :]).then_inc(cw.h, 16)
        cw.n += 16
    be2_tok = (cw, cw.n)

    bank_free = [[] for _ in range(4)]
    h2_free = list(ready)
    gb_free = [[], []]
    ge_free = [[], []]
    ga_free = [[], []]
    stage_free = []
    acc_state = [[] for _ in range(NTB)]
    hid_writable = []
    w1_tok = {}
    w2_tok = {}

    def emit_w1(e, wf):
        @block.gpsimd
        def _(q, e=e, wf=list(wf)):
            for s, n in wf:
                q.wait_ge(s.h, n)
            q.dma_start(out=w1.rearrange("p (k c) -> p k c", k=KC),
                        in_=kb.w["ffn_We1"][e].rearrange("(k p) c -> p k c", p=128)).then_inc(cw.h, 16)
            cw.n += 16
            q.dma_start(out=be1r, in_=kb.w["ffn_be1"][e][None, :]).then_inc(cw.h, 16)
            cw.n += 16
        w1_tok[e] = (cw, cw.n)

    def emit_w2(e, wf):
        @block.gpsimd
        def _(q, e=e, wf=list(wf)):
            for s, n in wf:
                q.wait_ge(s.h, n)
            q.dma_start(out=w2.rearrange("p (f c) -> p f c", f=FFC),
                        in_=kb.w["ffn_We2"][e].rearrange("(f p) c -> p f c", p=128)).then_inc(cw.h, 16)
            cw.n += 16
        w2_tok[e] = (cw, cw.n)

    emit_w1(0, ready)
    emit_w2(0, ready)

    for e in range(NE):
        for tb in range(NTB):
            elem, tbe = tb // 4, tb % 4
            par = tb % 2

            @block.sync
            def _(q, e=e, tb=tb, elem=elem, tbe=tbe, par=par, h2f=list(h2_free)):
                for s, n in h2f + ge_free[par] + ga_free[par]:
                    q.wait_ge(s.h, n)
                src = kb.ds["h2b"].ap()[elem, :, tbe * 512:(tbe + 1) * 512].rearrange(
                    "(c p) w -> p c w", p=128)
                q.dma_start(out=h2tb.rearrange("p (c w) -> p c w", c=KC),
                            in_=src).then_inc(ca.h, 16)
                ca.n += 16
                q.dma_start(out=ge_ap(par), in_=kb.ds["gall"].ap()[e:e + 1, tb * 512:(tb + 1) * 512]
                            ).then_inc(ca.h, 16)
                ca.n += 16
                if e == NE - 1:
                    q.dma_start(out=ga_ap(par), in_=kb.ds["gall"].ap()[:, tb * 512:(tb + 1) * 512]
                                ).then_inc(ca.h, 16)
                    ca.n += 16
            ld_tok = (ca, ca.n)

            bankg = 3

            @block.tensor
            def _(t, e=e, tb=tb, par=par, bankg=bankg, ld_tok=ld_tok,
                  gbf=list(gb_free[par])):
                t.wait_ge(ca.h, ld_tok[1])
                for s, n in bank_free[bankg]:
                    t.wait_ge(s.h, n)
                for s, n in gbf:
                    t.wait_ge(s.h, n)
                pe.inc(t.matmul(PS[bankg][:, 0:512], ones_row, ge_ap(par),
                                start=True, stop=True))
            pe_gb = pe.n

            @block.vector
            def _(v, par=par, bankg=bankg, pe_gb=pe_gb):
                v.wait_ge(pe.h, pe_gb)
                dv.inc(v.tensor_copy(out=gb_ap(par), in_=PS[bankg][:, 0:512]))
            bank_free[bankg] = [(dv, dv.n)]
            gb_tok = (dv, dv.n)

            @block.vector
            def _(v, par=par, gb_tok=gb_tok):
                v.wait_ge(dv.h, gb_tok[1])
                for k in range(KC):
                    ins = v.tensor_mul(h2tb[:, k * 512:(k + 1) * 512],
                                       h2tb[:, k * 512:(k + 1) * 512], gb_ap(par))
                dv.inc(ins)
            gh_tok = (dv, dv.n)
            gb_free[par] = [(dv, dv.n)]

            hid_toks = []
            for ffc in range(FFC):
                bank = ffc % 3

                @block.tensor
                def _(t, e=e, tb=tb, ffc=ffc, par=par, bank=bank, gh_tok=gh_tok,
                      w1t=w1_tok[e]):
                    if ffc == 0:
                        t.wait_ge(dv.h, gh_tok[1])
                        if tb == 0:
                            t.wait_ge(cw.h, w1t[1])
                    for s, n in bank_free[bank]:
                        t.wait_ge(s.h, n)
                    for k in range(KC):
                        t.matmul(PS[bank][:, 0:512],
                                 w1[:, k * FFD + ffc * 128:k * FFD + (ffc + 1) * 128],
                                 h2tb[:, k * 512:(k + 1) * 512],
                                 start=(k == 0), stop=False)
                    pe.inc(t.matmul(PS[bank][:, 0:512],
                                    be1r[0:1, ffc * 128:(ffc + 1) * 128], ge_ap(par),
                                    start=False, stop=True))
                pe_t = pe.n

                @block.vector
                def _(v, ffc=ffc, bank=bank, pe_t=pe_t, hw=list(hid_writable)):
                    v.wait_ge(pe.h, pe_t)
                    for s, n in hw:
                        v.wait_ge(s.h, n)
                    dv.inc(v.tensor_scalar(hid[:, ffc * 512:(ffc + 1) * 512],
                                           PS[bank][:, 0:512], 0.0, None, ALU.max))
                bank_free[bank] = [(dv, dv.n)]
                hid_toks.append((dv, dv.n))
            hid_writable = []
            h2_free = [(pe, pe.n)]
            ge_free[par] = [(pe, pe.n)]
            if tb == NTB - 1 and e + 1 < NE:
                emit_w1(e + 1, [(pe, pe.n)])

            dc_toks = []
            for dc in range(KC):
                bank = dc % 3

                @block.tensor
                def _(t, e=e, tb=tb, dc=dc, par=par, bank=bank, hid_toks=list(hid_toks),
                      w2t=w2_tok[e]):
                    if dc == 0:
                        for s, n in hid_toks:
                            t.wait_ge(s.h, n)
                        if tb == 0:
                            t.wait_ge(cw.h, w2t[1])
                            if e == NE - 1:
                                t.wait_ge(cw.h, be2_tok[1])
                    for s, n in bank_free[bank]:
                        t.wait_ge(s.h, n)
                    last = (e != NE - 1)
                    for ffc in range(FFC):
                        ins = t.matmul(PS[bank][:, 0:512],
                                       w2[:, ffc * D + dc * 128:ffc * D + (dc + 1) * 128],
                                       hid[:, ffc * 512:(ffc + 1) * 512],
                                       start=(ffc == 0),
                                       stop=(last and ffc == FFC - 1))
                    if e == NE - 1:
                        ins = t.matmul(PS[bank][:, 0:512],
                                       be2[:, dc * 128:(dc + 1) * 128], ga_ap(par),
                                       start=False, stop=True)
                    pe.inc(ins)
                pe_t = pe.n

                @block.vector
                def _(v, dc=dc, bank=bank, pe_t=pe_t, sf=list(stage_free)):
                    v.wait_ge(pe.h, pe_t)
                    for s, n in sf:
                        v.wait_ge(s.h, n)
                    dv.inc(v.tensor_copy(out=stage[:, dc * 512:(dc + 1) * 512],
                                         in_=PS[bank][:, 0:512]))
                bank_free[bank] = [(dv, dv.n)]
                dc_toks.append((dv, dv.n))
            if e == NE - 1:
                ga_free[par] = [(pe, pe.n)]
            if tb == NTB - 1 and e + 1 < NE:
                emit_w2(e + 1, [(pe, pe.n)])
            stage_free = []

            accv = kb.ds["moeacc"].ap()[:, tb * 512:(tb + 1) * 512].rearrange(
                "(c p) w -> p c w", p=128)

            @block.gpsimd
            def _(q, e=e, tb=tb, dc_toks=list(dc_toks), accst=list(acc_state[tb])):
                if e > 0:
                    for s, n in dc_toks + accst:
                        q.wait_ge(s.h, n)
                    q.dma_start(out=stage.rearrange("p (c w) -> p c w", c=KC), in_=accv,
                                accum_op=ALU.add).then_inc(cb.h, 16)
                    cb.n += 16
            acc_tok = (cb, cb.n)

            @block.scalar
            def _(q, e=e, tb=tb, dc_toks=list(dc_toks), acc_tok=acc_tok,
                  accst=list(acc_state[tb])):
                if e > 0:
                    q.wait_ge(cb.h, acc_tok[1])
                else:
                    for s, n in dc_toks + accst:
                        q.wait_ge(s.h, n)
                q.dma_start(out=accv, in_=stage.rearrange("p (c w) -> p c w", c=KC)
                            ).then_inc(cs.h, 16)
                cs.n += 16
            acc_state[tb] = [(cs, cs.n)]
            stage_free = [(cs, cs.n)]
    A.release()
    kb.moe_done = [(cs, cs.n)]


def _p_out(kb, x_src, out_dst, scale, ready):
    A, C, block, nc, PS = kb.A, kb.C, kb.block, kb.nc, kb.PS
    cw, ca, cs, pe, dv, ac = (kb.sem(n) for n in ["cw", "ca", "cs", "pe", "dv", "ac"])
    A.mark()
    wo_off = A.alloc(KC * D * 2)
    acc_off = A.alloc(KC * 512 * 4)
    h2f_off = A.alloc(KC * 512 * 4)
    s3_off = A.alloc(KC * 512 * 4)
    h3_off = A.alloc(KC * 512 * 2)
    xr_off = A.alloc(KC * 512 * 4)
    oc_off = A.alloc(2 * 512 * 4)
    scr = ln_scr_alloc(A)
    wo = A.bf16(wo_off, KC * D)
    acc = A.f32(acc_off, KC * 512)
    h2f = A.f32(h2f_off, KC * 512)
    s3 = A.f32(s3_off, KC * 512)
    h3 = A.bf16(h3_off, KC * 512)
    xr = A.f32(xr_off, KC * 512)

    @block.gpsimd
    def _(q):
        for s, n in ready:
            q.wait_ge(s.h, n)
        q.dma_start(out=wo.rearrange("p (k c) -> p k c", k=KC),
                    in_=kb.w["ffn_Wout"].rearrange("(k p) c -> p k c", p=128)).then_inc(cw.h, 16)
        cw.n += 16
    wo_tok = (cw, cw.n)

    state = {"ps_free": list(ready)}
    bank_free = [[] for _ in range(4)]
    in_free = list(ready)
    s3_free = []
    h3_free = []
    xr_free = list(ready)
    oc_busy = [[], []]

    @block.tensor
    def _(t):
        for s, n in ready:
            t.wait_ge(s.h, n)
        t.wait_ge(cw.h, wo_tok[1])

    for tb in range(NTB):
        elem, tbe = tb // 4, tb % 4

        @block.sync
        def _(q, elem=elem, tbe=tbe, tb=tb, inf=list(in_free)):
            for s, n in inf:
                q.wait_ge(s.h, n)
            q.dma_start(out=acc.rearrange("p (c w) -> p c w", c=KC),
                        in_=kb.ds["moeacc"].ap()[:, tb * 512:(tb + 1) * 512].rearrange(
                            "(c p) w -> p c w", p=128)).then_inc(ca.h, 16)
            ca.n += 16
            q.dma_start(out=h2f.rearrange("p (c w) -> p c w", c=KC),
                        in_=kb.ds["h2f"].ap()[elem, :, tbe * 512:(tbe + 1) * 512].rearrange(
                            "(c p) w -> p c w", p=128)).then_inc(ca.h, 16)
            ca.n += 16
        ld_tok = (ca, ca.n)

        @block.vector
        def _(v, ld_tok=ld_tok, s3f=list(s3_free)):
            v.wait_ge(ca.h, ld_tok[1])
            for s, n in s3f:
                v.wait_ge(s.h, n)
            for c in range(KC):
                ins = v.tensor_add(s3[:, c * 512:(c + 1) * 512],
                                   h2f[:, c * 512:(c + 1) * 512],
                                   acc[:, c * 512:(c + 1) * 512])
            dv.inc(ins)
        s3_tok = (dv, dv.n)
        in_free = [(dv, dv.n)]

        def y_put(c, tmp, tok):
            @block.vector
            def _(v, c=c):
                for s, n in h3_free:
                    v.wait_ge(s.h, n)
                dv.inc(v.tensor_copy(out=h3[:, c * 512:(c + 1) * 512], in_=tmp))
            return [(dv, dv.n)]

        state = _ln_block(kb, s3, scr, y_put, gamma=CP["ln3_g"], beta=CP["ln3_b"],
                          ready=[s3_tok], state=state)
        h3_tok = (dv, kb.sem("dv").n)
        s3_free = [(dv, kb.sem("dv").n), (ac, kb.sem("ac").n)]

        @block.sync
        def _(q, elem=elem, tbe=tbe, xrf=list(xr_free)):
            for s, n in xrf:
                q.wait_ge(s.h, n)
            q.dma_start(out=xr.rearrange("p (c w) -> p c w", c=KC),
                        in_=x_src[elem][:, tbe * 512:(tbe + 1) * 512].rearrange(
                            "(c p) w -> p c w", p=128)).then_inc(ca.h, 16)
            ca.n += 16
        xr_tok = (ca, ca.n)

        h3_free = []
        for dc in range(KC):
            bank = dc % 4
            par = dc % 2

            @block.tensor
            def _(t, dc=dc, bank=bank, h3_tok=h3_tok):
                if dc == 0:
                    t.wait_ge(dv.h, h3_tok[1])
                for s, n in bank_free[bank]:
                    t.wait_ge(s.h, n)
                for k in range(KC):
                    ins = t.matmul(PS[bank][:, 0:512],
                                   wo[:, k * D + dc * 128:k * D + (dc + 1) * 128],
                                   h3[:, k * 512:(k + 1) * 512],
                                   start=(k == 0), stop=(k == KC - 1))
                pe.inc(ins)
            pe_t = pe.n
            h3_free.append((pe, pe.n))

            @block.vector
            def _(v, dc=dc, bank=bank, par=par, pe_t=pe_t, xr_tok=xr_tok,
                  ocb=list(oc_busy[par])):
                v.wait_ge(pe.h, pe_t)
                v.wait_ge(ca.h, xr_tok[1])
                for s, n in ocb:
                    v.wait_ge(s.h, n)
                dv.inc(v.scalar_tensor_tensor(A.f32(oc_off + par * 512 * 4, 512),
                                              PS[bank][:, 0:512], scale,
                                              xr[:, dc * 512:(dc + 1) * 512],
                                              ALU.mult, ALU.add))
            bank_free[bank] = [(dv, dv.n)]
            oc_tok = (dv, dv.n)

            @block.sync
            def _(q, elem=elem, tbe=tbe, dc=dc, par=par, oc_tok=oc_tok):
                q.wait_ge(dv.h, oc_tok[1])
                q.dma_start(out=out_dst[elem][dc * 128:(dc + 1) * 128,
                                              tbe * 512:(tbe + 1) * 512],
                            in_=A.f32(oc_off + par * 512 * 4, 512)).then_inc(cs.h, 16)
                cs.n += 16
            oc_busy[par] = [(cs, cs.n)]
        xr_free = [(dv, dv.n)]
    A.release()
    kb.ln_ps_free = state["ps_free"]
    kb.out_done = [(cs, cs.n)]


def _p_sqkv(kb, z_off, q_off, k_off, vt_off, wbuf_off, bvr_off, ready):
    A, C, block, nc, PS = kb.A, kb.C, kb.block, kb.nc, kb.PS
    cw, pe, ac, dv = kb.sem("cw"), kb.sem("pe"), kb.sem("ac"), kb.sem("dv")
    names = ["attn_wq", "attn_wk", "attn_wv"]
    wb = lambda par: A.bf16(wbuf_off + par * KC * 512 * 2, KC * 512)
    z_ap = lambda k: A.bf16(z_off + 2 * k * S, S)
    bvr = A.bf16(bvr_off, D, parts=1)
    ones_row = C.bf16(kb.ones_row_bf_off, 128, parts=1)

    @block.tensor
    def _(t):
        for s, n in ready:
            t.wait_ge(s.h, n)

    @block.gpsimd
    def _(q):
        for s, n in ready:
            q.wait_ge(s.h, n)
        q.dma_start(out=bvr, in_=kb.w["attn_bv"][None, :]).then_inc(cw.h, 16)
        cw.n += 16
    bv_tok = (cw, cw.n)

    wload = {}
    wfree = [[], []]
    bank_free = [[] for _ in range(4)]

    def emit_load(cb):
        par = cb % 2
        mat, half = names[cb // 2], cb % 2

        @block.gpsimd
        def _(q):
            for s, n in wfree[par]:
                q.wait_ge(s.h, n)
            src = kb.w[mat][:, half * 512:(half + 1) * 512].rearrange("(k p) c -> p k c", p=128)
            q.dma_start(out=wb(par).rearrange("p (k c) -> p k c", k=KC), in_=src).then_inc(cw.h, 16)
            cw.n += 16
        wload[cb] = (cw, cw.n)

    emit_load(0)
    emit_load(1)
    nbank = 0
    for cb in range(6):
        par = cb % 2
        if cb < 4:
            dst_off = q_off if cb < 2 else k_off
            bias = "bq" if cb < 2 else "bk"
            fbase = (cb % 2) * 4
            for m in range(4):
                f = fbase + m
                for n in range(4):
                    bank = nbank % 4
                    nbank += 1

                    @block.tensor
                    def _(t, cb=cb, par=par, m=m, n=n, bank=bank):
                        t.wait_ge(cw.h, wload[cb][1])
                        for s, nn in bank_free[bank]:
                            t.wait_ge(s.h, nn)
                        for k in range(KC):
                            ins = t.matmul(PS[bank][:, 0:512],
                                           wb(par)[:, k * 512 + m * 128:k * 512 + (m + 1) * 128],
                                           z_ap(k)[:, n * 512:(n + 1) * 512],
                                           start=(k == 0), stop=(k == KC - 1))
                        pe.inc(ins)
                    pe_t = pe.n

                    @block.vector
                    def _(v, f=f, n=n, bank=bank, pe_t=pe_t, dst_off=dst_off, bias=bias):
                        v.wait_ge(pe.h, pe_t)
                        dst = kb.A.bf16(dst_off + 2 * (f * S + n * 512), 512)
                        b_ap = C.f32(kb.cp_off + 4 * (CP[bias] + f), 1)
                        dv.inc(v.tensor_scalar(dst, PS[bank][:, 0:512], b_ap, None, ALU.add))
                    bank_free[bank] = [(dv, dv.n)]
            wfree[par] = [(pe, pe.n)]
            if cb + 2 < 6:
                emit_load(cb + 2)
        else:
            vb = cb - 4
            for tt in range(16):
                bank = nbank % 4
                nbank += 1

                @block.tensor
                def _(t, cb=cb, par=par, tt=tt, vb=vb, bank=bank):
                    t.wait_ge(cw.h, wload[cb][1])
                    if tt == 0:
                        t.wait_ge(cw.h, bv_tok[1])
                    for s, nn in bank_free[bank]:
                        t.wait_ge(s.h, nn)
                    for k in range(KC):
                        t.matmul(PS[bank][:, 0:512],
                                 z_ap(k)[:, tt * 128:(tt + 1) * 128],
                                 wb(par)[:, k * 512:(k + 1) * 512],
                                 start=(k == 0), stop=False)
                    pe.inc(t.matmul(PS[bank][:, 0:512], ones_row,
                                    bvr[0:1, vb * 512:(vb + 1) * 512],
                                    start=False, stop=True))
                pe_t = pe.n

                @block.vector
                def _(v, tt=tt, vb=vb, bank=bank, pe_t=pe_t):
                    v.wait_ge(pe.h, pe_t)
                    dst = kb.A.bf16(vt_off + 2 * (tt * 1024 + vb * 512), 512)
                    dv.inc(v.tensor_copy(out=dst, in_=PS[bank][:, 0:512]))
                bank_free[bank] = [(dv, dv.n)]
            wfree[par] = [(pe, pe.n)]
            if cb + 2 < 6:
                emit_load(cb + 2)

    kb.sqkv_done = [(ac, ac.n), (dv, dv.n)]


def _sp_scr(kb):
    A = kb.A
    return {
        "sc": A.alloc(3 * WIN * 4), "p": A.alloc(2 * WIN * 2), "pT": A.alloc(2 * WIN * 2),
        "cst": A.alloc(S * 2), "nm": A.alloc(16 * 4),
    }


def _p_satt(kb, q_off, k_off, vt_off, ctx_off, mask_off, scr, ready):
    A, C, block, nc, PS, PT = kb.A, kb.C, kb.block, kb.nc, kb.PS, kb.PT
    pe, dv, ac, cs = kb.sem("pe"), kb.sem("dv"), kb.sem("ac"), kb.sem("cs")
    ident = C.bf16(kb.ident_off, 128)
    sc_ap = lambda sl, w: A.f32(scr["sc"] + sl * WIN * 4, w)
    p_ap = lambda par, w: A.bf16(scr["p"] + par * WIN * 2, w)
    pT_ap = lambda par, w: A.bf16(scr["pT"] + par * WIN * 2, w)
    cst = A.bf16(scr["cst"], S, parts=64)
    nm_ap = lambda i: A.f32(scr["nm"] + i * 4, 1)
    MW = 512 + WIN + 512

    head_tok = list(ready)
    sc_state = [[] for _ in range(3)]
    p_state = [[], []]
    pT_state = [[], []]
    PTb_state = [[], []]
    ab_state = [[] for _ in range(4)]
    ctx_state = []
    nm_state = [[], []]
    cst_free = []
    gidx = 0
    sidx = 0

    for h in range(NH):
        bb = (h % 2) * 64
        hc = h // 2
        for c in range(NCH):
            lo = max(0, 2 * c - 2)
            hi = min(16, 2 * c + 4)
            w128 = hi - lo
            wk = w128 * 128
            mvar = 0 if c == 0 else (2 if c == NCH - 1 else 1)
            mbase_col = [0, 512, 512 + WIN][mvar]
            for qt in range(2):
                scs = sidx % 3
                par = sidx % 2
                b01 = (sidx % 2) * 2
                sidx += 1
                qcol = c * 256 + qt * 128

                blocks = []
                o = 0
                while o < wk:
                    wb_ = min(512, wk - o)
                    blocks.append((o, wb_))
                    o += wb_
                for bi, (ko, wb_) in enumerate(blocks):
                    bank = b01 + bi % 2

                    @block.tensor
                    def _(t, h=h, bb=bb, hc=hc, qcol=qcol, lo=lo, ko=ko, wb_=wb_, bank=bank):
                        if gidx == 0 and ko == 0:
                            for s, n in head_tok:
                                t.wait_ge(s.h, n)
                        for s, n in ab_state[bank]:
                            t.wait_ge(s.h, n)
                        ksl = A.bf16(k_off, KC * S, parts=64, p0=bb)
                        qsl = A.bf16(q_off, KC * S, parts=64, p0=bb)
                        pe.inc(t.matmul(PS[bank][:, 0:wb_],
                                        qsl[:, hc * S + qcol:hc * S + qcol + 128],
                                        ksl[:, hc * S + lo * 128 + ko:hc * S + lo * 128 + ko + wb_],
                                        start=True, stop=True))
                    pe_t = pe.n

                    @block.vector
                    def _(v, scs=scs, ko=ko, wb_=wb_, bank=bank, pe_t=pe_t, qt=qt,
                          mbase_col=mbase_col, scst=list(sc_state[scs])):
                        v.wait_ge(pe.h, pe_t)
                        for s, n in scst:
                            v.wait_ge(s.h, n)
                        mask = A.bf16(mask_off + 2 * (qt * MW + mbase_col + ko), wb_)
                        dv.inc(v.tensor_add(sc_ap(scs, WIN)[:, ko:ko + wb_],
                                            PS[bank][:, 0:wb_], mask))
                    ab_state[bank] = [(dv, dv.n)]
                sc_state[scs] = []
                sc_tok = (dv, dv.n)

                @block.vector
                def _(v, scs=scs, wk=wk, par=par, sc_tok=sc_tok, nmst=list(nm_state[par])):
                    v.wait_ge(dv.h, sc_tok[1])
                    for s, n in nmst:
                        v.wait_ge(s.h, n)
                    dv.inc(v.reduce_max(nm_ap(par), sc_ap(scs, wk), axis=AX.X))
                    v.wait_ge(dv.h, dv.n)
                    dv.inc(v.tensor_scalar_mul(nm_ap(2 + par), nm_ap(par), -0.125))
                dv_nm = dv.n

                @block.scalar
                def _(sq, scs=scs, wk=wk, par=par, dv_nm=dv_nm, pst=list(p_state[par])):
                    sq.wait_ge(dv.h, dv_nm)
                    for s, n in pst:
                        sq.wait_ge(s.h, n)
                    ac.inc(sq.activation(out=p_ap(par, wk), in_=sc_ap(scs, wk), func=AF.Exp,
                                         bias=nm_ap(2 + par), scale=0.125,
                                         accum_out=nm_ap(4 + par)))
                ac_exp = ac.n
                sc_state[scs] = [(ac, ac.n)]
                p_state[par] = []

                @block.vector
                def _(v, par=par, wk=wk, ac_exp=ac_exp):
                    v.wait_ge(ac.h, ac_exp)
                    dv.inc(v.reciprocal(nm_ap(6 + par), nm_ap(4 + par)))
                    v.wait_ge(dv.h, dv.n)
                    dv.inc(v.tensor_scalar(p_ap(par, wk), p_ap(par, wk), nm_ap(6 + par),
                                           None, ALU.mult))
                dv_p = dv.n
                nm_state[par] = [(dv, dv.n)]

                bankb = gidx % 2
                gidx += 1

                @block.tensor
                def _(t, w128=w128, par=par, bankb=bankb, dv_p=dv_p):
                    t.wait_ge(dv.h, dv_p)
                    for s, n in PTb_state[bankb]:
                        t.wait_ge(s.h, n)
                    for i in range(w128):
                        ins = t.transpose(PT[bankb][:, i * 128:(i + 1) * 128],
                                          p_ap(par, WIN)[:, i * 128:(i + 1) * 128], ident)
                    pe.inc(ins)
                pe_tr = pe.n
                p_state[par] = p_state[par] + [(pe, pe.n)]

                @block.vector
                def _(v, w128=w128, par=par, bankb=bankb, pe_tr=pe_tr, ptst=list(pT_state[par])):
                    v.wait_ge(pe.h, pe_tr)
                    for s, n in ptst:
                        v.wait_ge(s.h, n)
                    dv.inc(v.tensor_copy(out=pT_ap(par, w128 * 128),
                                         in_=PT[bankb][:, 0:w128 * 128]))
                PTb_state[bankb] = [(dv, dv.n)]
                pT_tok = (dv, dv.n)

                @block.tensor
                def _(t, h=h, w128=w128, lo=lo, par=par, pT_tok=pT_tok, ctxst=list(ctx_state)):
                    t.wait_ge(dv.h, pT_tok[1])
                    for s, n in ctxst:
                        t.wait_ge(s.h, n)
                    for i in range(w128):
                        kbk = lo + i
                        ins = t.matmul(PS[4][0:64, 0:128],
                                       A.bf16(vt_off + 2 * (kbk * 1024 + h * DH), DH),
                                       pT_ap(par, WIN)[:, i * 128:(i + 1) * 128],
                                       start=(i == 0), stop=(i == w128 - 1))
                    pe.inc(ins)
                pe_ctx = pe.n
                pT_state[par] = [(pe, pe.n)]

                @block.vector
                def _(v, qcol=qcol, pe_ctx=pe_ctx, cstf=list(cst_free), first=(c == 0 and qt == 0)):
                    v.wait_ge(pe.h, pe_ctx)
                    if first:
                        for s, n in cstf:
                            v.wait_ge(s.h, n)
                    dv.inc(v.tensor_copy(out=cst[:, qcol:qcol + 128],
                                         in_=PS[4][0:64, 0:128]))
                ctx_state = [(dv, dv.n)]

        @block.sync
        def _(q, h=h, bb=bb, hc=hc):
            q.wait_ge(dv.h, dv.n)
            dst = A.bf16(ctx_off + 2 * (hc * S), S, parts=64, p0=bb)
            q.dma_start(out=dst, in_=cst[:, :]).then_inc(cs.h, 16)
            cs.n += 16
        cst_free = [(cs, cs.n)]
        head_tok = [(pe, pe.n), (cs, cs.n)]

    kb.satt_done = [(cs, cs.n)]


def _p_swo(kb, elem, ctx_off, x_src, x_dst, ready):
    A, C, block, nc, PS = kb.A, kb.C, kb.block, kb.nc, kb.PS
    cw, ca, cs, pe, dv = (kb.sem(n) for n in ["cw", "ca", "cs", "pe", "dv"])
    A.mark()
    wo_off = A.alloc(KC * D * 2)
    x1_off = A.alloc(KC * 512 * 4)
    oc_off = A.alloc(2 * 512 * 4)
    wo = A.bf16(wo_off, KC * D)
    x1 = A.f32(x1_off, KC * 512)

    @block.gpsimd
    def _(q):
        for s, n in ready:
            q.wait_ge(s.h, n)
        q.dma_start(out=wo.rearrange("p (k c) -> p k c", k=KC),
                    in_=kb.w["attn_wo"].rearrange("(k p) c -> p k c", p=128)).then_inc(cw.h, 16)
        cw.n += 16
    wo_tok = (cw, cw.n)

    bank_free = [[] for _ in range(4)]
    x1_free = []
    oc_busy = [[], []]
    for tb in range(S // 512):
        @block.sync
        def _(q, tb=tb, x1f=list(x1_free)):
            for s, n in x1f:
                q.wait_ge(s.h, n)
            q.dma_start(out=x1.rearrange("p (c w) -> p c w", c=KC),
                        in_=x_src[elem][:, tb * 512:(tb + 1) * 512].rearrange(
                            "(c p) w -> p c w", p=128)).then_inc(ca.h, 16)
            ca.n += 16
        x1_tok = (ca, ca.n)

        for dc in range(KC):
            bank = dc % 4
            par = dc % 2

            @block.tensor
            def _(t, tb=tb, dc=dc, bank=bank):
                if tb == 0 and dc == 0:
                    t.wait_ge(cw.h, wo_tok[1])
                    for s, n in ready:
                        t.wait_ge(s.h, n)
                for s, n in bank_free[bank]:
                    t.wait_ge(s.h, n)
                for k in range(KC):
                    ins = t.matmul(PS[bank][:, 0:512],
                                   wo[:, k * D + dc * 128:k * D + (dc + 1) * 128],
                                   A.bf16(ctx_off + 2 * (k * S + tb * 512), 512),
                                   start=(k == 0), stop=(k == KC - 1))
                pe.inc(ins)
            pe_t = pe.n

            @block.vector
            def _(v, dc=dc, bank=bank, par=par, pe_t=pe_t, x1_tok=x1_tok,
                  ocb=list(oc_busy[par])):
                v.wait_ge(pe.h, pe_t)
                v.wait_ge(ca.h, x1_tok[1])
                for s, n in ocb:
                    v.wait_ge(s.h, n)
                bo = C.f32(kb.cp_off + 4 * (CP["bo"] + dc), 1)
                dv.inc(v.scalar_tensor_tensor(A.f32(oc_off + par * 512 * 4, 512),
                                              PS[bank][:, 0:512], bo,
                                              x1[:, dc * 512:(dc + 1) * 512],
                                              ALU.add, ALU.add))
            bank_free[bank] = [(dv, dv.n)]
            oc_tok = (dv, dv.n)

            @block.sync
            def _(q, tb=tb, dc=dc, par=par, oc_tok=oc_tok):
                q.wait_ge(dv.h, oc_tok[1])
                q.dma_start(out=x_dst[elem][dc * 128:(dc + 1) * 128, tb * 512:(tb + 1) * 512],
                            in_=A.f32(oc_off + par * 512 * 4, 512)).then_inc(cs.h, 16)
                cs.n += 16
            oc_busy[par] = [(cs, cs.n)]
        x1_free = [(dv, dv.n)]
    A.release()
    kb.swo_done = [(cs, cs.n)]


def _p_mln(kb, src, y_sb_off, f32_dram=None, scr=None, ready0=()):
    A, block, nc = kb.A, kb.block, kb.nc
    ca, cs, dv, ac = kb.sem("ca"), kb.sem("cs"), kb.sem("dv"), kb.sem("ac")
    nblk = S // LNW
    state = {"ps_free": list(kb.ln_ps_free), "tmp_busy": None}
    xloaded = []
    xfree = scr.get("xfree", [[], []])

    for b in range(nblk):
        par = b % 2
        x_blk = A.f32(scr["xblk"], KC * LNW)

        @block.sync
        def _(q, b=b, par=par, x_blk=x_blk):
            if b == 0:
                for s, n in ready0:
                    q.wait_ge(s.h, n)
            for s, n in xfree[0] + xfree[1]:
                q.wait_ge(s.h, n)
            src_blk = src[:, b * LNW:(b + 1) * LNW].rearrange("(c p) w -> p c w", p=128)
            out3 = x_blk.rearrange("p (c w) -> p c w", c=KC)
            q.dma_start(out=out3, in_=src_blk).then_inc(ca.h, 16); ca.n += 16
        xloaded.append((ca, ca.n))

        def y_put(c, tmp, tok, b=b, par=par):
            toks = []
            @block.vector
            def _(v):
                y_sb = A.bf16(y_sb_off + 2 * (c * S + b * LNW), LNW)
                v.tensor_copy(out=y_sb, in_=tmp)
            if f32_dram is not None:
                @block.sync
                def _(q):
                    q.wait_ge(tok[0].h, tok[1])
                    q.dma_start(out=f32_dram[c * 128:(c + 1) * 128, b * LNW:(b + 1) * LNW],
                                in_=tmp).then_inc(cs.h, 16)
                    cs.n += 16
                toks.append((cs, cs.n))
            else:
                toks.append(tok)
            return toks

        state["tmp_busy"] = scr.setdefault("tmp_busy", {0: [], 1: []})
        st = _ln_block(kb, x_blk, scr, y_put, ready=[xloaded[-1]] + list(kb.const_tok) + list(ready0), state=state)
        state = st
        scr["tmp_busy"] = st["tmp_busy"]
        xfree[par] = [(dv, kb.sem("dv").n), (ac, kb.sem("ac").n)]
    scr["xfree"] = xfree
    kb.ln_ps_free = state["ps_free"]


def _mln_scr(kb):
    A = kb.A
    scr = ln_scr_alloc(A)
    scr["xblk"] = A.alloc(KC * LNW * 4)
    return scr


def _p_finish(kb, dbg):
    nc, block = kb.nc, kb.block
    cs = kb.sem("cs")

    @block.sync
    def _(q):
        for s in kb._sems.values():
            if s.n > 0 and s.h is not cs.h:
                q.wait_ge(s.h, s.n)
        for name, out_ap in kb.dbg_outs.items():
            t = kb.ds[name]
            q.dma_start(out=out_ap, in_=t.ap()).then_inc(cs.h, 16); cs.n += 16
        q.wait_ge(cs.h, cs.n)


def kernel(**inputs):
    xs = np.asarray(inputs["L0_attn_hidden_states"])
    shards = []
    for c in range(NCORES):
        xc = xs[c * EPC:(c + 1) * EPC]
        shards.append(np.ascontiguousarray(xc.transpose(0, 2, 1)))

    cpack_np = _cpack_host({k: np.asarray(v) for k, v in inputs.items()})
    pe_np = _host_pe_T()
    mask_np = _host_sparse_masks()

    nc = build_program()
    base = {"pe_T": pe_np, "cpack": cpack_np, "spmask": mask_np,
            "ident": np.eye(128, dtype=np.float32)}
    for k in list(inputs.keys()):
        if k == "L0_attn_hidden_states":
            continue
        base[k.replace("L0_", "")] = np.ascontiguousarray(np.asarray(inputs[k], dtype=np.float32))
    in_maps = [dict(base, x=shards[c]) for c in range(NCORES)]
    res = run_bass_kernel_spmd(nc, in_maps, core_ids=list(range(NCORES)))
    outs = []
    for c in range(NCORES):
        yc = res.results[c]["y"]
        outs.append(yc.transpose(0, 2, 1))
    return np.ascontiguousarray(np.concatenate(outs, axis=0))


# version 17
# speedup vs baseline: 1.4882x; 1.0082x over previous
import math
import contextlib
import numpy as np
import concourse.bass as bass
import concourse.mybir as mybir
from concourse.bass_utils import run_bass_kernel_spmd

F32 = mybir.dt.float32
BF16 = mybir.dt.bfloat16
AF = mybir.ActivationFunctionType
ALU = mybir.AluOpType
AX = mybir.AxisListType

B, S, D = 16, 2048, 1024
NH, DH = 16, 64
FFD = 4096
NE = 8
NCORES = 8
EPC = B // NCORES
KC = D // 128
FFC = FFD // 128
NTOK = EPC * S
TB = 512
NTB = NTOK // TB
QT = S // 128
CH = 256
NCH = S // CH
WIN = 768
EPS = 1e-5
NEG = -1.0e30

BD_W = 2304
BD_SLOTS = 4

ARENA_KB = 202
CONST_KB = 4


class Sem:

    def __init__(self, h):
        self.h = h
        self.n = 0

    def inc(self, instr, k=1):
        instr.then_inc(self.h, k)
        self.n += k
        return self.n


class Arena:

    def __init__(self, t, nbytes):
        self.t = t
        self.top = 0
        self.lim = nbytes
        self.marks = []

    def alloc(self, nbytes, align=64):
        off = (self.top + align - 1) // align * align
        self.top = off + nbytes
        assert self.top <= self.lim, f"arena overflow: {self.top} > {self.lim}"
        return off

    def mark(self):
        self.marks.append(self.top)

    def release(self):
        self.top = self.marks.pop()

    def f32(self, off, cols, parts=128, p0=0):
        return self.t[p0:p0 + parts, off:off + 4 * cols].bitcast(F32)

    def bf16(self, off, cols, parts=128, p0=0):
        return self.t[p0:p0 + parts, off:off + 2 * cols].bitcast(BF16)


class KB:

    def __init__(self, nc, es, block):
        self.nc = nc
        self.es = es
        self.block = block
        self._sems = {}

    def sem(self, name):
        if name not in self._sems:
            self._sems[name] = Sem(self.es.enter_context(self.nc.semaphore(name)))
        return self._sems[name]


def _host_pe_T():
    pos = np.arange(S - 1, -1, -1, dtype=np.float32)
    inv_freq = (1.0 / 10000.0 ** (np.arange(0, D, 2, dtype=np.float32) / D)).astype(np.float32)
    sinusoid = pos[:, None].astype(np.float64) * inv_freq[None, :].astype(np.float64)
    pe = np.concatenate([np.sin(sinusoid), np.cos(sinusoid)], axis=-1).astype(np.float32)
    return np.ascontiguousarray(pe.T)


def _host_sparse_masks():
    out = np.zeros((128, 2 * (512 + WIN + 512)), dtype=np.float32)
    i = np.arange(128)[:, None]
    for qt in range(2):
        base = qt * (512 + WIN + 512)
        j = np.arange(512)[None, :]
        qpos = qt * 128 + i
        valid = np.abs(qpos - j) <= 256
        out[:, base:base + 512] = np.where(valid, 0.0, NEG)
        j = np.arange(WIN)[None, :]
        qpos = 256 + qt * 128 + i
        valid = np.abs(qpos - j) <= 256
        out[:, base + 512:base + 512 + WIN] = np.where(valid, 0.0, NEG)
        j = np.arange(512)[None, :]
        qpos = 256 + qt * 128 + i
        valid = np.abs(qpos - j) <= 256
        out[:, base + 512 + WIN:base + 1024 + WIN] = np.where(valid, 0.0, NEG)
    return out


def _to_chunks(v):
    return np.ascontiguousarray(v.reshape(-1, 128).T)


def build_program(dbg=None):
    dbg = dbg or []
    nc = bass.Bass("TRN2", target_bir_lowering=False)

    x_in = nc.dram_tensor("x", [EPC, D, S], F32, kind="ExternalInput").ap()
    w = {}
    w_shapes = {
        "attn_wq": [D, D], "attn_wk": [D, D], "attn_wv": [D, D], "attn_wo": [D, D],
        "attn_bq": [D], "attn_bk": [D], "attn_bv": [D], "attn_bo": [D],
        "ffn_Wqkv": [D, 3 * D], "ffn_Wo": [D, D], "ffn_Wr": [D, D],
        "ffn_u_bias": [NH, DH], "ffn_v_bias": [NH, DH],
        "ffn_ln1_g": [D], "ffn_ln1_b": [D], "ffn_ln2_g": [D], "ffn_ln2_b": [D],
        "ffn_ln3_g": [D], "ffn_ln3_b": [D],
        "ffn_Wff1": [D, FFD], "ffn_bff1": [FFD], "ffn_Wff2": [FFD, D], "ffn_bff2": [D],
        "ffn_Wg": [D, NE], "ffn_We1": [NE, D, FFD], "ffn_be1": [NE, FFD],
        "ffn_We2": [NE, FFD, D], "ffn_be2": [NE, D], "ffn_Wout": [D, D],
    }
    for k, shp in w_shapes.items():
        w[k] = nc.dram_tensor(k, shp, F32, kind="ExternalInput").ap()
    pe_T = nc.dram_tensor("pe_T", [D, S], F32, kind="ExternalInput").ap()
    cpack = nc.dram_tensor("cpack", [128, 160], F32, kind="ExternalInput").ap()
    spmask = nc.dram_tensor("spmask", [128, 2 * (512 + WIN + 512)], F32, kind="ExternalInput").ap()
    ident_in = nc.dram_tensor("ident", [128, 128], F32, kind="ExternalInput").ap()
    x_out = nc.dram_tensor("y", [EPC, D, S], F32, kind="ExternalOutput").ap()

    ds = {}
    def dscr(name, shape, dt=F32):
        ds[name] = nc.dram_tensor(name, shape, dt, kind="Internal")
        return ds[name]

    dscr("xres_a", [EPC, D, S])
    dscr("xres_b", [EPC, D, S])
    dscr("ybuf", [EPC, D, S])
    dscr("h1f", [EPC, D, S])
    dscr("h2f", [EPC, D, S])
    dscr("h1b", [EPC, D, S], BF16)
    dscr("h2b", [EPC, D, S], BF16)
    dscr("rbuf", [D, S], BF16)
    dscr("gall", [NE, NTOK], BF16)
    dscr("moeacc", [D, NTOK])
    dscr("bdscr", [BD_SLOTS, 128, BD_W], BF16)

    dbg_outs = {}
    for name in dbg:
        if name in ds:
            dbg_outs[name] = nc.dram_tensor("dbg_" + name, list(ds[name].shape),
                                            ds[name].dtype, kind="ExternalOutput").ap()

    es = contextlib.ExitStack()
    with es:
        arena_t = es.enter_context(nc.sbuf_tensor("arena", [128, ARENA_KB * 1024], mybir.dt.uint8))
        const_t = es.enter_context(nc.sbuf_tensor("consts", [128, CONST_KB * 1024], mybir.dt.uint8))
        A = Arena(arena_t, ARENA_KB * 1024)
        C = Arena(const_t, CONST_KB * 1024)
        PS = [es.enter_context(nc.psum_tensor(f"ps{i}", [128, 512], F32)) for i in range(6)]
        PT = [es.enter_context(nc.psum_tensor(f"pt{i}", [128, 1024], BF16)) for i in range(2)]
        block = es.enter_context(nc.Block())
        kb = KB(nc, es, block)
        kb.A, kb.C, kb.PS, kb.PT = A, C, PS, PT
        kb.w, kb.ds, kb.dbg_outs = w, ds, dbg_outs
        kb.x_in, kb.x_out = x_in, x_out
        kb.pe_T, kb.cpack, kb.spmask = pe_T, cpack, spmask
        kb.ident_in = ident_in

        for name in ["cw", "ca", "cs", "cb",
                     "pe", "dv", "ac", "po",
                     "gp",
                     "bar"]:
            kb.sem(name)

        _emit_all(kb, dbg)

    _group_engines(nc)
    return nc


def _group_engines(nc):
    order = {}

    def key(ins):
        e = str(ins.engine)
        if e not in order:
            order[e] = len(order) + 1
        return 0 if e == "EngineType.Unassigned" else order[e]

    for f in nc.m.functions:
        for b in f.blocks:
            ins = list(b.instructions)
            ins.sort(key=key)
            b.instructions = ins


_CP_LAYOUT = [("ln1_g", 8), ("ln1_b", 8), ("ln2_g", 8), ("ln2_b", 8), ("ln3_g", 8),
              ("ln3_b", 8), ("u_b", 8), ("v_b", 8), ("bq", 8), ("bk", 8), ("bv", 8),
              ("bo", 8), ("bff1", 32), ("bff2", 8), ("cv", 8)]
_CONST_VALS = [EPS, 0.125, 0.5, 1.0, -1.0, 0.0, NEG, -0.5]
CP = {}
_o = 0
for _n, _w in _CP_LAYOUT:
    CP[_n] = _o
    _o += _w
assert _o <= 160


def _cpack_host(inp):
    out = np.zeros((128, 160), np.float32)
    src = {
        "ln1_g": inp["L0_ffn_ln1_g"], "ln1_b": inp["L0_ffn_ln1_b"],
        "ln2_g": inp["L0_ffn_ln2_g"], "ln2_b": inp["L0_ffn_ln2_b"],
        "ln3_g": inp["L0_ffn_ln3_g"], "ln3_b": inp["L0_ffn_ln3_b"],
        "u_b": np.asarray(inp["L0_ffn_u_bias"]).reshape(-1),
        "v_b": np.asarray(inp["L0_ffn_v_bias"]).reshape(-1),
        "bq": inp["L0_attn_bq"], "bk": inp["L0_attn_bk"], "bv": inp["L0_attn_bv"],
        "bo": inp["L0_attn_bo"], "bff1": inp["L0_ffn_bff1"], "bff2": inp["L0_ffn_bff2"],
    }
    for name, width in _CP_LAYOUT:
        if name == "cv":
            out[:, CP[name]:CP[name] + width] = np.asarray(_CONST_VALS, np.float32)[None, :]
        else:
            a = _to_chunks(np.asarray(src[name]))
            out[:, CP[name]:CP[name] + width] = a
    return out


_XL_DBG = None


def _emit_all(kb, dbg):
    kb.xl_first = True
    kb.xl_dbg = _XL_DBG
    kb.ln_ps_free = []
    kb.r_bank_free = [[] for _ in range(4)]
    kb.dbg_list = dbg
    A = kb.A
    _p_const(kb)
    _p_r(kb)

    def alltok():
        return [(kb.sem(n), kb.sem(n).n)
                for n in ["cw", "ca", "cs", "cb", "pe", "dv", "ac", "gp"]
                if kb.sem(n).n > 0]

    def ffn_call(x_src, x_dst, tag):
        nsc = kb.nc.named_scope
        for e in range(EPC):
            A.mark()
            ctx_off = A.alloc(KC * S * 2)
            A.mark()
            q_off = A.alloc(KC * S * 2)
            k_off = A.alloc(KC * S * 2)
            vt_off = A.alloc(16 * 1024 * 2)
            A.mark()
            y_off = A.alloc(KC * S * 2)
            A.mark()
            scr = _mln_scr(kb)
            with nsc(f"MLN_{tag}{e}", notify=True):
                _p_mln(kb, x_src[e], y_off, f32_dram=kb.ds["ybuf"].ap()[e], scr=scr,
                       ready0=alltok())
            mln_done = alltok()
            A.release()
            wbuf_off = A.alloc(2 * KC * 512 * 2)
            with nsc(f"QKV_{tag}{e}", notify=True):
                _p_qkv(kb, y_off, q_off, k_off, vt_off, wbuf_off, ready=mln_done)
            A.release()
            xscr = _xl_scr(kb)
            with nsc(f"XL_{tag}{e}", notify=True):
                _p_xl(kb, q_off, k_off, vt_off, ctx_off, xscr, ready=alltok())
                kb.xl_first = False
            if tag == "B" and e == 0:
                _dump(kb, "ctx", kb.A.bf16(ctx_off, KC * S), kb.xl_done)
            A.release()
            with nsc(f"WOH_{tag}{e}", notify=True):
                _p_woh(kb, e, ctx_off, ready=alltok())
            A.release()
        with nsc(f"FFG_{tag}", notify=True):
            _p_ffg(kb, ready=alltok())
        with nsc(f"GATE_{tag}", notify=True):
            _p_gate(kb, ready=alltok())
        with nsc(f"MOE_{tag}", notify=True):
            _p_moe(kb, ready=alltok())
        with nsc(f"OUT_{tag}", notify=True):
            _p_out(kb, x_src, x_dst, 0.5, ready=alltok())

    def attn_call(x_src, x_dst):
        A.mark()
        MW = 512 + WIN + 512
        mask_off = A.alloc(2 * MW * 2)
        ca = kb.sem("ca")

        @kb.block.gpsimd
        def _(q):
            for s, n in alltok():
                q.wait_ge(s.h, n)
            q.dma_start(out=A.bf16(mask_off, 2 * MW), in_=kb.spmask).then_inc(ca.h, 16)
            ca.n += 16
        mask_tok = [(ca, ca.n)]
        nsc = kb.nc.named_scope
        for e in range(EPC):
            A.mark()
            ctx_off = A.alloc(KC * S * 2)
            A.mark()
            q_off = A.alloc(KC * S * 2)
            k_off = A.alloc(KC * S * 2)
            vt_off = A.alloc(16 * 1024 * 2)
            A.mark()
            z_off = A.alloc(KC * S * 2)
            A.mark()
            scr = _mln_scr(kb)
            with nsc(f"SLN_{e}", notify=True):
                _p_mln(kb, x_src[e], z_off, f32_dram=None, scr=scr, ready0=alltok())
            mln_done = alltok()
            A.release()
            wbuf_off = A.alloc(2 * KC * 512 * 2)
            bvr_off = A.alloc(D * 2)
            with nsc(f"SQKV_{e}", notify=True):
                _p_sqkv(kb, z_off, q_off, k_off, vt_off, wbuf_off, bvr_off, ready=mln_done)
            A.release()
            spscr = _sp_scr(kb)
            with nsc(f"SATT_{e}", notify=True):
                _p_satt(kb, q_off, k_off, vt_off, ctx_off, mask_off, spscr,
                        ready=alltok() + mask_tok)
            if e == 1:
                _dump(kb, "sctx", kb.A.bf16(ctx_off, KC * S), kb.satt_done)
                _dump(kb, "sq", kb.A.bf16(q_off, KC * S), kb.satt_done)
                _dump(kb, "sk", kb.A.bf16(k_off, KC * S), kb.satt_done)
            A.release()
            with nsc(f"SWO_{e}", notify=True):
                _p_swo(kb, e, ctx_off, x_src, x_dst, ready=alltok())
            A.release()
        A.release()

    xa = kb.ds["xres_a"].ap()
    xb = kb.ds["xres_b"].ap()
    ffn_call(kb.x_in, xa, "B")
    attn_call(xa, xb)
    ffn_call(xb, kb.x_out, "F")
    _p_finish(kb, dbg)


def _p_const(kb):
    C, block = kb.C, kb.block
    cw, dv = kb.sem("cw"), kb.sem("dv")

    kb.cp_off = C.alloc(160 * 4)
    kb.ident_off = C.alloc(128 * 2)
    kb.ones_col_off = C.alloc(2 * 2)
    kb.ones_row_bf_off = C.alloc(128 * 2)
    kb.ones_row_f_off = C.alloc(128 * 4)
    kb.identf_off = C.alloc(128 * 4)
    kb.neg_off = C.alloc(256 * 4)
    kb.neg_bf_off = C.alloc(256 * 2)

    @block.sync
    def _(q):
        q.dma_start(out=C.f32(kb.cp_off, 160), in_=kb.cpack).then_inc(cw.h, 16); cw.n += 16
        q.dma_start(out=C.f32(kb.identf_off, 128), in_=kb.ident_in).then_inc(cw.h, 16); cw.n += 16

    @block.gpsimd
    def _(q):
        q.dma_start(out=C.bf16(kb.ident_off, 128), in_=kb.ident_in).then_inc(cw.h, 16); cw.n += 16

    @block.vector
    def _(v):
        v.memset(C.bf16(kb.ones_col_off, 2), 1.0)
        v.memset(C.bf16(kb.ones_row_bf_off, 128, parts=1), 1.0)
        v.memset(C.f32(kb.ones_row_f_off, 128, parts=1), 1.0)
        v.memset(C.f32(kb.neg_off, 256), NEG)
        v.memset(C.bf16(kb.neg_bf_off, 256), NEG).then_inc(dv.h, 1); dv.n += 1

    cs = kb.sem("cs")

    @block.sync
    def _(q):
        q.wait_ge(dv.h, dv.n)
        for sl in range(BD_SLOTS):
            q.dma_start(out=kb.ds["bdscr"].ap()[sl, :, S:BD_W],
                        in_=C.bf16(kb.neg_bf_off, BD_W - S)).then_inc(cs.h, 16)
            cs.n += 16
    kb.bdtail_tok = (cs, cs.n)

    for i, val in enumerate(_CONST_VALS):
        ap = C.f32(kb.cp_off + 4 * (CP["cv"] + i), 1)
        kb.nc.const_aps.register(F32, float(val), ap)
    kb.const_tok = [(cw, cw.n), (dv, dv.n)]


LNW = 512


def ln_scr_alloc(A):
    return {
        "xbf": A.alloc(KC * LNW * 2),
        "rows": A.alloc(3 * LNW * 4),
        "tmp": A.alloc(2 * LNW * 4),
    }


def _ln_block(kb, x_blk, scr, y_put, gamma=None, beta=None, ready=(), state=None):
    A, C, block, PS = kb.A, kb.C, kb.block, kb.PS
    nc = kb.nc
    W = LNW
    pe, dv, ac = kb.sem("pe"), kb.sem("dv"), kb.sem("ac")
    ps_s, ps_q = PS[4], PS[5]
    if state is None:
        state = {}
    ps_free = state.get("ps_free", [])
    tmp_busy = state.get("tmp_busy", {0: [], 1: []})

    xbf = A.bf16(scr["xbf"], KC * W)
    xsq = xbf
    rows = scr["rows"]
    r_s = A.f32(rows, W, parts=1)
    r_q = A.f32(rows + 4 * W, W, parts=1)
    r_t = A.f32(rows + 8 * W, W, parts=1)
    tmp_ap = lambda par: A.f32(scr["tmp"] + par * 4 * W, W)
    ones_col = C.bf16(kb.ones_col_off, 2)
    ones_row_f = C.f32(kb.ones_row_f_off, 128, parts=1)

    def wt(eng, toks):
        for s, n in toks:
            eng.wait_ge(s.h, n)

    xbf_free = state.get("xbf_free", [])

    @block.vector
    def _(v):
        wt(v, ready)
        wt(v, xbf_free)
        for c in range(KC):
            ins = v.tensor_copy(out=xbf[:, c * W:(c + 1) * W], in_=x_blk[:, c * W:(c + 1) * W])
        dv.inc(ins)
    dv_cast = dv.n

    @block.tensor
    def _(t):
        t.wait_ge(dv.h, dv_cast)
        wt(t, ps_free)
        for c in range(KC):
            ins = t.matmul(ps_s[0:1, 0:W], ones_col[:, 0:1], xbf[:, c * W:(c + 1) * W],
                           start=(c == 0), stop=(c == KC - 1))
        pe.inc(ins)
    pe_sums = pe.n

    @block.vector
    def _(v):
        wt(v, ready)
        v.wait_ge(pe.h, pe_sums)
        for c in range(KC):
            ins = v.tensor_mul(xsq[:, c * W:(c + 1) * W], x_blk[:, c * W:(c + 1) * W],
                               x_blk[:, c * W:(c + 1) * W])
        dv.inc(ins)
    dv_sq = dv.n

    @block.tensor
    def _(t):
        t.wait_ge(dv.h, dv_sq)
        for c in range(KC):
            ins = t.matmul(ps_q[0:1, 0:W], ones_col[:, 0:1], xsq[:, c * W:(c + 1) * W],
                           start=(c == 0), stop=(c == KC - 1))
        pe.inc(ins)
    pe_stats = pe.n
    state["xbf_free"] = [(pe, pe.n)]

    @block.vector
    def _(v):
        v.wait_ge(pe.h, pe_stats)
        v.tensor_copy(out=r_s, in_=ps_s[0:1, 0:W])
        v.tensor_copy(out=r_q, in_=ps_q[0:1, 0:W])
        v.tensor_scalar_mul(r_s, r_s, -1.0 / D)
        v.tensor_scalar_mul(r_q, r_q, 1.0 / D)
        v.tensor_mul(r_t, r_s, r_s)
        dv.inc(v.tensor_sub(r_q, r_q, r_t))
    dv_var = dv.n

    @block.scalar
    def _(s):
        s.wait_ge(dv.h, dv_var)
        s.activation(out=r_q, in_=r_q, func=AF.Ln, bias=EPS, scale=1.0)
        ac.inc(s.activation(out=r_q, in_=r_q, func=AF.Exp, bias=0.0, scale=-0.5))
    ac_rstd = ac.n

    @block.tensor
    def _(t):
        t.wait_ge(ac.h, ac_rstd)
        t.matmul(ps_s[:, 0:W], ones_row_f, r_s, start=True, stop=True)
        pe.inc(t.matmul(ps_q[:, 0:W], ones_row_f, r_q, start=True, stop=True))
    pe_bc = pe.n

    for c in range(KC):
        par = c % 2
        @block.vector
        def _(v, c=c, par=par):
            if c == 0:
                v.wait_ge(pe.h, pe_bc)
            wt(v, tmp_busy.get(par, []))
            tmp = tmp_ap(par)
            v.tensor_add(tmp, x_blk[:, c * W:(c + 1) * W], ps_s[:, 0:W])
            ins = v.tensor_mul(tmp, tmp, ps_q[:, 0:W])
            if gamma is not None:
                g_ap = C.f32(kb.cp_off + 4 * (gamma + c), 1)
                b_ap = C.f32(kb.cp_off + 4 * (beta + c), 1)
                ins = v.tensor_scalar(tmp, tmp, g_ap, b_ap, ALU.mult, ALU.add)
            dv.inc(ins)
        toks = y_put(c, tmp_ap(par), (dv, dv.n))
        tmp_busy[par] = list(toks)
    state["ps_free"] = [(dv, dv.n)]
    state["tmp_busy"] = tmp_busy
    return state


def _dump(kb, name, ap, waits):
    if name not in kb.dbg_list:
        return None
    cs = kb.sem("cs")
    shp = list(ap.shape)
    out = kb.nc.dram_tensor("dbg_" + name, shp, ap.dtype, kind="ExternalOutput").ap()

    @kb.block.sync
    def _(q):
        for s, n in waits:
            q.wait_ge(s.h, n)
        q.dma_start(out=out, in_=ap).then_inc(cs.h, 16)
        cs.n += 16
    return (cs, cs.n)


def _p_qkv(kb, y_off, q_off, k_off, vt_off, wbuf_off, ready=()):
    A, C, block, nc, PS = kb.A, kb.C, kb.block, kb.nc, kb.PS
    cw, pe, ac, dv = kb.sem("cw"), kb.sem("pe"), kb.sem("ac"), kb.sem("dv")
    Wqkv = kb.w["ffn_Wqkv"]

    @block.tensor
    def _(t):
        for s, n in ready:
            t.wait_ge(s.h, n)

    @block.gpsimd
    def _(q):
        for s, n in ready:
            q.wait_ge(s.h, n)

    wb = lambda par: A.bf16(wbuf_off + par * KC * 512 * 2, KC * 512)
    y_ap = lambda ksl: A.bf16(y_off + 2 * ksl * S, S)
    wload = {}
    wfree = [[], []]
    bank_free = [[] for _ in range(4)]

    def emit_load(cb):
        par = cb % 2

        @block.gpsimd
        def _(q):
            for s, n in wfree[par]:
                q.wait_ge(s.h, n)
            src = Wqkv[:, cb * 512:(cb + 1) * 512].rearrange("(k p) c -> p k c", p=128)
            dst = wb(par).rearrange("p (k c) -> p k c", k=KC)
            q.dma_start(out=dst, in_=src).then_inc(cw.h, 16)
            cw.n += 16
        wload[cb] = (cw, cw.n)

    emit_load(0)
    emit_load(1)
    nbank = 0
    for cb in range(6):
        par = cb % 2
        if cb < 4:
            dst_off = q_off if cb < 2 else k_off
            fbase = (cb % 2) * 4
            for m in range(4):
                f = fbase + m
                for n in range(4):
                    bank = nbank % 4
                    nbank += 1

                    @block.tensor
                    def _(t, cb=cb, par=par, m=m, n=n, bank=bank):
                        t.wait_ge(cw.h, wload[cb][1])
                        for s, nn in bank_free[bank]:
                            t.wait_ge(s.h, nn)
                        for k in range(KC):
                            ins = t.matmul(PS[bank][:, 0:512],
                                           wb(par)[:, k * 512 + m * 128:k * 512 + (m + 1) * 128],
                                           y_ap(k)[:, n * 512:(n + 1) * 512],
                                           start=(k == 0), stop=(k == KC - 1))
                        pe.inc(ins)
                    pe_t = pe.n

                    @block.vector
                    def _(v, f=f, n=n, bank=bank, pe_t=pe_t, dst_off=dst_off):
                        v.wait_ge(pe.h, pe_t)
                        dst = kb.A.bf16(dst_off + 2 * (f * S + n * 512), 512)
                        dv.inc(v.tensor_copy(out=dst, in_=PS[bank][:, 0:512]))
                    bank_free[bank] = [(dv, dv.n)]
            wfree[par] = [(pe, pe.n)]
            if cb + 2 < 6:
                emit_load(cb + 2)
        else:
            vb = cb - 4
            for tt in range(16):
                bank = nbank % 4
                nbank += 1

                @block.tensor
                def _(t, cb=cb, par=par, tt=tt, bank=bank):
                    t.wait_ge(cw.h, wload[cb][1])
                    for s, nn in bank_free[bank]:
                        t.wait_ge(s.h, nn)
                    for k in range(KC):
                        ins = t.matmul(PS[bank][:, 0:512],
                                       y_ap(k)[:, tt * 128:(tt + 1) * 128],
                                       wb(par)[:, k * 512:(k + 1) * 512],
                                       start=(k == 0), stop=(k == KC - 1))
                    pe.inc(ins)
                pe_t = pe.n

                @block.vector
                def _(v, tt=tt, vb=vb, bank=bank, pe_t=pe_t):
                    v.wait_ge(pe.h, pe_t)
                    dst = kb.A.bf16(vt_off + 2 * (tt * 1024 + vb * 512), 512)
                    dv.inc(v.tensor_copy(out=dst, in_=PS[bank][:, 0:512]))
                bank_free[bank] = [(dv, dv.n)]
            wfree[par] = [(pe, pe.n)]
            if cb + 2 < 6:
                emit_load(cb + 2)

    kb.qkv_done = [(ac, ac.n)]


def _p_r(kb):
    A, C, block, nc, PS = kb.A, kb.C, kb.block, kb.nc, kb.PS
    cw, ca, cs, pe, ac, dv = (kb.sem(n) for n in ["cw", "ca", "cs", "pe", "ac", "dv"])
    A.mark()
    peT_off = A.alloc(KC * S * 2)
    wbuf_off = A.alloc(2 * KC * 512 * 2)
    rst_off = A.alloc(2 * 512 * 2)
    wb = lambda par: A.bf16(wbuf_off + par * KC * 512 * 2, KC * 512)
    peT = lambda k: A.bf16(peT_off + 2 * k * S, S)

    @block.gpsimd
    def _(q):
        for k in range(KC):
            q.dma_start(out=peT(k), in_=kb.pe_T[k * 128:(k + 1) * 128, :]).then_inc(ca.h, 16)
            ca.n += 16
        for cb in range(2):
            src = kb.w["ffn_Wr"][:, cb * 512:(cb + 1) * 512].rearrange("(k p) c -> p k c", p=128)
            q.dma_start(out=wb(cb).rearrange("p (k c) -> p k c", k=KC), in_=src).then_inc(cw.h, 16)
            cw.n += 16
    loads = (ca.n, cw.n)
    st_free = [[], []]

    for cb in range(2):
        for m in range(4):
            f = cb * 4 + m
            for n in range(4):
                bank = (f * 4 + n) % 4

                @block.tensor
                def _(t, cb=cb, m=m, n=n, bank=bank):
                    t.wait_ge(ca.h, loads[0])
                    t.wait_ge(cw.h, loads[1])
                    for s, nn in kb.r_bank_free[bank]:
                        t.wait_ge(s.h, nn)
                    for k in range(KC):
                        ins = t.matmul(PS[bank][:, 0:512],
                                       wb(cb)[:, k * 512 + m * 128:k * 512 + (m + 1) * 128],
                                       peT(k)[:, n * 512:(n + 1) * 512],
                                       start=(k == 0), stop=(k == KC - 1))
                    pe.inc(ins)
                pe_t = pe.n
                par = n % 2

                @block.vector
                def _(v, bank=bank, par=par, pe_t=pe_t):
                    v.wait_ge(pe.h, pe_t)
                    for s, nn in st_free[par]:
                        v.wait_ge(s.h, nn)
                    dv.inc(v.tensor_copy(out=A.bf16(rst_off + par * 512 * 2, 512),
                                         in_=PS[bank][:, 0:512]))
                kb.r_bank_free[bank] = [(dv, dv.n)]
                ac_t = dv.n

                @block.sync
                def _(q, f=f, n=n, par=par, ac_t=ac_t):
                    q.wait_ge(dv.h, ac_t)
                    q.dma_start(out=kb.ds["rbuf"].ap()[f * 128:(f + 1) * 128, n * 512:(n + 1) * 512],
                                in_=A.bf16(rst_off + par * 512 * 2, 512)).then_inc(cs.h, 16)
                    cs.n += 16
                st_free[par] = [(cs, cs.n)]
    A.release()
    kb.r_done = [(cs, cs.n)]


def _xl_scr(kb):
    A = kb.A
    return {
        "u2h": A.alloc(S * 2), "v2h": A.alloc(S * 2), "rh": A.alloc(S * 2),
        "sc": A.alloc(3 * S * 4), "bdst": A.alloc(2 * S * 2),
        "p": A.alloc(3 * S * 2), "pT": A.alloc(4 * 1024 * 2),
        "cst": A.alloc(2 * S * 2), "nm": A.alloc(32 * 4),
    }


def _p_xl(kb, q_off, k_off, vt_off, ctx_off, scr, ready):
    A, C, block, nc, PS, PT = kb.A, kb.C, kb.block, kb.nc, kb.PS, kb.PT
    pe, dv, ac = kb.sem("pe"), kb.sem("dv"), kb.sem("ac")
    ca, cs, cb, gp = kb.sem("ca"), kb.sem("cs"), kb.sem("cb"), kb.sem("gp")
    bdscr = kb.ds["bdscr"]
    rbuf = kb.ds["rbuf"]
    ident = C.bf16(kb.ident_off, 128)

    u2h = lambda bb: A.bf16(scr["u2h"], S, parts=64, p0=bb)
    v2h = lambda bb: A.bf16(scr["v2h"], S, parts=64, p0=bb)
    rh = lambda bb: A.bf16(scr["rh"], S, parts=64, p0=bb)
    sc_ap = lambda sl, o, w: A.f32(scr["sc"] + sl * S * 4 + o * 4, w)
    bdst_ap = lambda par, o, w: A.bf16(scr["bdst"] + par * S * 2 + o * 2, w)
    p_ap = lambda par, o, w: A.bf16(scr["p"] + par * S * 2 + o * 2, w)
    pT_ap = lambda sl, w: A.bf16(scr["pT"] + sl * 1024 * 2, w)
    cst = lambda hp: A.bf16(scr["cst"] + hp * S * 2, S, parts=64)
    nm_ap = lambda i, w=1: A.f32(scr["nm"] + i * 4, w)

    NU = NH * QT
    first = {"t": True, "v": True, "s": True, "g": True, "sq": True}

    def wt(eng, toks, key=None):
        if key is not None and first[key]:
            first[key] = False
            toks = list(toks) + list(ready)
        for s, n in toks:
            eng.wait_ge(s.h, n)

    bd_bank = [[], []]
    ac_bank = [[], []]
    bdst_st = [[], []]
    slot_st = [[] for _ in range(BD_SLOTS)]
    p_st = [[], [], []]
    sc_st = [[], [], []]
    PTb_st = [[], []]
    pT_st = [[], [], [], []]
    ctx_col = [[], []]
    head_done_pe = {}
    cst_tok = {}
    prep_tok = {}
    U = [dict() for _ in range(NU)]
    gidx = 0

    def jblocks(kt):
        out, o = [], S - kt
        while o < S:
            w = min(512, S - o)
            out.append((o, w))
            o += w
        return out

    def front(u):
        h, t = divmod(u, QT)
        bb = (h % 2) * 64
        kt = 128 * (t + 1)
        st = U[u]
        st.update(h=h, t=t, bb=bb, kt=kt, par=u % 3, scs=u % 3, par2=u % 2,
                  slot=u % BD_SLOTS, hc=h // 2)
        if t == 0:
            gate = list(head_done_pe.get(h - 2, []))

            @block.vector
            def _(v, h=h, bb=bb, gate=gate):
                wt(v, gate, "v")
                hc = h // 2
                qsl = A.bf16(q_off + 2 * (hc * S), S, parts=64, p0=bb)
                ub = C.f32(kb.cp_off + 4 * (CP["u_b"] + hc), 1, parts=64, p0=bb)
                vb = C.f32(kb.cp_off + 4 * (CP["v_b"] + hc), 1, parts=64, p0=bb)
                v.tensor_scalar(u2h(bb), qsl, ub, None, ALU.add)
                dv.inc(v.tensor_scalar(v2h(bb), qsl, vb, None, ALU.add))
            uv_tok = (dv, dv.n)

            @block.scalar
            def _(s, h=h, bb=bb, gate=gate):
                wt(s, gate, "s")
                s.dma_start(out=rh(bb), in_=rbuf.ap()[h * DH:(h + 1) * DH, :]).then_inc(ca.h, 16)
                ca.n += 16
            prep_tok[h] = [uv_tok, (ca, ca.n)]

        jb = jblocks(kt)
        evac_toks = []
        for bi, (jo, wb_) in enumerate(jb):
            bank = bi % 2

            @block.tensor
            def _(tq, u=u, t=t, bb=bb, jo=jo, wb_=wb_, bank=bank, bi=bi, h=h):
                toks = list(bd_bank[bank])
                if bi == 0:
                    toks += prep_tok[h]
                wt(tq, toks, "t")
                pe.inc(tq.matmul(PS[bank][:, 0:wb_],
                                 v2h(bb)[:, t * 128:(t + 1) * 128],
                                 rh(bb)[:, jo:jo + wb_], start=True, stop=True))
            pe_t = pe.n

            @block.vector
            def _(v, jo=jo, wb_=wb_, bank=bank, pe_t=pe_t, kt=kt, par2=u % 2, bi=bi):
                toks = [(pe, pe_t)]
                if bi == 0:
                    toks += bdst_st[par2]
                wt(v, toks, "v")
                dv.inc(v.tensor_copy(out=bdst_ap(par2, jo - (S - kt), wb_),
                                     in_=PS[bank][:, 0:wb_]))
            bd_bank[bank] = [(dv, dv.n)]
        st["bdev"] = (dv, dv.n)

        @block.sync
        def _(q, u=u, kt=kt, par2=u % 2, slot=u % BD_SLOTS):
            wt(q, [U[u]["bdev"]] + slot_st[slot], "sq")
            q.dma_start(out=bdscr.ap()[slot, :, S - kt:S],
                        in_=bdst_ap(par2, 0, kt)).then_inc(cs.h, 16)
            cs.n += 16
        bdst_st[u % 2] = [(cs, cs.n)]
        st["bdwr"] = (cs, cs.n)

    def mid(u):
        st = U[u]
        kt, par, slot = st["kt"], st["par"], st["slot"]

        @block.gpsimd
        def _(g, u=u, kt=kt, par=par, slot=slot):
            toks = [U[u]["bdwr"]] + p_st[par]
            if kb.xl_first and u < BD_SLOTS:
                toks = toks + [kb.bdtail_tok]
            wt(g, toks, "g")
            src = bass.AP(bdscr, slot * 128 * BD_W + (S - 1 - 128 * st["t"]),
                          [[BD_W - 1, 128], [1, kt]])
            g.dma_start(out=p_ap(par, 0, kt), in_=src).then_inc(cb.h, 16)
            cb.n += 16
        st["bdrd"] = (cb, cb.n)

    def back1(u):
        st = U[u]
        h, t, bb, kt, par, scs = st["h"], st["t"], st["bb"], st["kt"], st["par"], st["scs"]
        hc = st["hc"]
        nblk = (kt + 511) // 512
        ttr_done = []
        for bi in range(nblk):
            ko = bi * 512
            wb_ = min(512, kt - ko)
            bank = bi % 2

            @block.tensor
            def _(tq, t=t, ko=ko, wb_=wb_, bank=bank, bb=bb, hc=hc):
                wt(tq, ac_bank[bank], "t")
                pe.inc(tq.matmul(PS[2 + bank][:, 0:wb_],
                                 u2h(bb)[:, t * 128:(t + 1) * 128],
                                 A.bf16(k_off, KC * S, parts=64, p0=bb)[:, hc * S + ko:hc * S + ko + wb_],
                                 start=True, stop=True))
            pe_t = pe.n

            @block.vector
            def _(v, u=u, bi=bi, ko=ko, wb_=wb_, bank=bank, pe_t=pe_t, par=par, scs=scs):
                toks = [(pe, pe_t)]
                if bi == 0:
                    toks = toks + [U[u]["bdrd"]] + sc_st[scs]
                wt(v, toks, "v")
                dv.inc(v.tensor_tensor(out=sc_ap(scs, ko, wb_), in0=PS[2 + bank][:, 0:wb_],
                                       in1=p_ap(par, ko, wb_), op=ALU.add))
            ac_bank[bank] = [(dv, dv.n)]
            ttr_done.append((dv, dv.n))
        st["ttr"] = (dv, dv.n)
        if t == QT - 1:
            head_done_pe[h] = [(pe, pe.n)]

        @block.vector
        def _(v, par=par, scs=scs, kt=kt):
            dv.inc(v.reduce_max(nm_ap(18 + par), sc_ap(scs, 0, kt), axis=AX.X))
            v.wait_ge(dv.h, dv.n)
            dv.inc(v.tensor_scalar_mul(nm_ap(21 + par), nm_ap(18 + par), -0.125))
        st["negm"] = (dv, dv.n)

        @block.scalar
        def _(sq, u=u, par=par, scs=scs, kt=kt):
            wt(sq, [U[u]["negm"]], "s")
            ac.inc(sq.activation(out=p_ap(par, 0, kt), in_=sc_ap(scs, 0, kt), func=AF.Exp,
                                 bias=nm_ap(21 + par), scale=0.125,
                                 accum_out=nm_ap(24 + par)))
        sc_st[scs] = [(ac, ac.n)]
        st["exp"] = (ac, ac.n)

    def back2(u):
        nonlocal gidx
        st = U[u]
        h, t, bb, kt, par = st["h"], st["t"], st["bb"], st["kt"], st["par"]
        hc, hp = st["hc"], h % 2

        @block.vector
        def _(v, u=u, par=par, kt=kt):
            v.wait_ge(ac.h, U[u]["exp"][1])
            dv.inc(v.reciprocal(nm_ap(27 + par), nm_ap(24 + par)))
            v.wait_ge(dv.h, dv.n)
            dv.inc(v.tensor_scalar(p_ap(par, 0, kt), p_ap(par, 0, kt), nm_ap(27 + par),
                                   None, ALU.mult))
        psc_tok = (dv, dv.n)

        ngrp = (t + 1 + 7) // 8
        grp_sl = []
        for g_ in range(ngrp):
            n_in_g = min(8, t + 1 - g_ * 8)
            bankb = gidx % 2
            sl = gidx % 4
            gidx += 1

            @block.tensor
            def _(tq, g_=g_, n_in_g=n_in_g, bankb=bankb, par=par, psc_tok=psc_tok):
                wt(tq, [psc_tok] + PTb_st[bankb], "t")
                for i in range(n_in_g):
                    kbk = g_ * 8 + i
                    ins = tq.transpose(PT[bankb][:, i * 128:(i + 1) * 128],
                                       p_ap(par, kbk * 128, 128), ident)
                pe.inc(ins)
            pe_tr = pe.n

            @block.vector
            def _(v, sl=sl, n_in_g=n_in_g, bankb=bankb, pe_tr=pe_tr):
                wt(v, [(pe, pe_tr)] + pT_st[sl])
                dv.inc(v.tensor_copy(out=pT_ap(sl, n_in_g * 128),
                                     in_=PT[bankb][:, 0:n_in_g * 128]))
            PTb_st[bankb] = [(dv, dv.n)]
            grp_sl.append((sl, (dv, dv.n)))
        p_st[par] = [(pe, pe.n)]

        chalf = u % 2
        c0 = chalf * 128

        @block.tensor
        def _(tq, t=t, h=h, grp_sl=grp_sl, c0=c0, chalf=chalf):
            for sl, tok in grp_sl:
                tq.wait_ge(tok[0].h, tok[1])
            wt(tq, ctx_col[chalf])
            for kbk in range(t + 1):
                sl = grp_sl[kbk // 8][0]
                ins = tq.matmul(PS[4][0:64, c0:c0 + 128],
                                A.bf16(vt_off + 2 * (kbk * 1024 + h * DH), DH),
                                pT_ap(sl, 1024)[:, (kbk % 8) * 128:(kbk % 8 + 1) * 128],
                                start=(kbk == 0), stop=(kbk == t))
            pe.inc(ins)
        pe_ctx = pe.n
        for kbk in range(t + 1):
            sl = grp_sl[kbk // 8][0]
            pT_st[sl] = [(pe, pe.n)]
        p_st[par] = [(pe, pe.n)]

        @block.vector
        def _(v, t=t, hp=hp, pe_ctx=pe_ctx, c0=c0, h=h):
            toks = [(pe, pe_ctx)]
            if t == 0:
                toks = toks + list(cst_tok.get(h - 2, []))
            wt(v, toks)
            dv.inc(v.tensor_copy(out=cst(hp)[:, t * 128:(t + 1) * 128],
                                 in_=PS[4][0:64, c0:c0 + 128]))
        ctx_col[chalf] = [(dv, dv.n)]

        if t == QT - 1:
            dv_last = dv.n

            @block.scalar
            def _(s, h=h, bb=bb, hc=hc, hp=hp, dv_last=dv_last):
                s.wait_ge(dv.h, dv_last)
                dst = A.bf16(ctx_off + 2 * (hc * S), S, parts=64, p0=bb)
                s.dma_start(out=dst, in_=cst(hp)[:, :]).then_inc(cs.h, 16)
                cs.n += 16
            cst_tok[h] = [(cs, cs.n)]

    for it in range(NU + 3):
        if 3 <= it < NU + 3:
            back2(it - 3)
        if 2 <= it < NU + 2:
            back1(it - 2)
        if 1 <= it < NU + 1:
            mid(it - 1)
        if it < NU:
            front(it)
        if 1 <= it < NU + 1:
            uu = it - 1
            slot_st[uu % BD_SLOTS] = [U[uu]["bdrd"]]

    kb.xl_done = [(cs, cs.n), (dv, dv.n)]


def _p_woh(kb, elem, ctx_off, ready):
    A, C, block, nc, PS = kb.A, kb.C, kb.block, kb.nc, kb.PS
    cw, ca, cs, pe, dv, ac = (kb.sem(n) for n in ["cw", "ca", "cs", "pe", "dv", "ac"])
    A.mark()
    wo_off = A.alloc(KC * D * 2)
    y32_off = A.alloc(KC * 512 * 4)
    s2_off = A.alloc(KC * 512 * 4)
    h1c_off = A.alloc(2 * 512 * 2)
    scr = ln_scr_alloc(A)
    wo = A.bf16(wo_off, KC * D)
    y32 = A.f32(y32_off, KC * 512)
    s2 = A.f32(s2_off, KC * 512)

    @block.gpsimd
    def _(q):
        for s, n in ready:
            q.wait_ge(s.h, n)
        src = kb.w["ffn_Wo"].rearrange("(k p) c -> p k c", p=128)
        q.dma_start(out=wo.rearrange("p (k c) -> p k c", k=KC), in_=src).then_inc(cw.h, 16)
        cw.n += 16
    wo_tok = (cw, cw.n)

    state = {"ps_free": list(ready)}
    bank_free = [[] for _ in range(4)]
    y32_free = []
    s2_free = []
    h1c_busy = [[], []]
    for tb in range(S // 512):
        @block.sync
        def _(q, tb=tb, elem=elem):
            for s, n in y32_free:
                q.wait_ge(s.h, n)
            src = kb.ds["ybuf"].ap()[elem][:, tb * 512:(tb + 1) * 512].rearrange(
                "(c p) w -> p c w", p=128)
            q.dma_start(out=y32.rearrange("p (c w) -> p c w", c=KC), in_=src).then_inc(ca.h, 16)
            ca.n += 16
        y_tok = (ca, ca.n)

        dc_toks = []
        for dc in range(KC):
            bank = dc % 4

            @block.tensor
            def _(t, tb=tb, dc=dc, bank=bank):
                if tb == 0 and dc == 0:
                    t.wait_ge(cw.h, wo_tok[1])
                    for s, n in ready:
                        t.wait_ge(s.h, n)
                for s, n in bank_free[bank]:
                    t.wait_ge(s.h, n)
                for k in range(KC):
                    ins = t.matmul(PS[bank][:, 0:512],
                                   wo[:, k * D + dc * 128:k * D + (dc + 1) * 128],
                                   A.bf16(ctx_off + 2 * (k * S + tb * 512), 512),
                                   start=(k == 0), stop=(k == KC - 1))
                pe.inc(ins)
            pe_t = pe.n

            @block.vector
            def _(v, dc=dc, bank=bank, pe_t=pe_t, y_tok=y_tok):
                v.wait_ge(pe.h, pe_t)
                v.wait_ge(ca.h, y_tok[1])
                for s, n in s2_free:
                    v.wait_ge(s.h, n)
                dv.inc(v.tensor_add(s2[:, dc * 512:(dc + 1) * 512],
                                    PS[bank][:, 0:512], y32[:, dc * 512:(dc + 1) * 512]))
            bank_free[bank] = [(dv, dv.n)]
            dc_toks.append((dv, dv.n))
        y32_free = [(dv, dv.n)]
        s2_free = []

        def y_put(c, tmp, tok, tb=tb):
            h1c = A.bf16(h1c_off + (c % 2) * 512 * 2, 512)
            toks = []

            @block.vector
            def _(v, c=c):
                for s, n in h1c_busy[c % 2]:
                    v.wait_ge(s.h, n)
                dv.inc(v.tensor_copy(out=h1c, in_=tmp))
            bf_tok = (dv, dv.n)

            @block.sync
            def _(q, c=c, bf_tok=bf_tok, tok=tok):
                q.wait_ge(tok[0].h, tok[1])
                q.dma_start(out=kb.ds["h1f"].ap()[elem, c * 128:(c + 1) * 128,
                                                  tb * 512:(tb + 1) * 512],
                            in_=tmp).then_inc(cs.h, 16)
                cs.n += 16
                toks.append((cs, cs.n))
                q.wait_ge(bf_tok[0].h, bf_tok[1])
                q.dma_start(out=kb.ds["h1b"].ap()[elem, c * 128:(c + 1) * 128,
                                                  tb * 512:(tb + 1) * 512],
                            in_=h1c).then_inc(cs.h, 16)
                cs.n += 16
            h1c_busy[c % 2] = [(cs, cs.n)]
            return toks

        state = _ln_block(kb, s2, scr, y_put, gamma=CP["ln1_g"], beta=CP["ln1_b"],
                          ready=dc_toks, state=state)
        s2_free = [(dv, kb.sem("dv").n), (ac, kb.sem("ac").n), (pe, kb.sem("pe").n)]
    A.release()
    kb.ln_ps_free = state["ps_free"]
    kb.woh_done = [(cs, cs.n)]


def _p_ffg(kb, ready):
    A, C, block, nc, PS = kb.A, kb.C, kb.block, kb.nc, kb.PS
    cw, ca, cs, pe, dv, ac = (kb.sem(n) for n in ["cw", "ca", "cs", "pe", "dv", "ac"])
    A.mark()
    w1_off = A.alloc(KC * FFD * 2)
    w2_off = A.alloc(FFC * D * 2)
    hid_off = A.alloc(FFC * 512 * 2)
    h1b_off = A.alloc(KC * 512 * 2)
    h132_off = A.alloc(2 * 512 * 4)
    s2_off = A.alloc(KC * 512 * 4)
    h2c_off = A.alloc(2 * 512 * 2)
    scr = {"xbf": hid_off, "rows": A.alloc(3 * LNW * 4), "tmp": A.alloc(2 * LNW * 4)}
    w1 = A.bf16(w1_off, KC * FFD)
    w2 = A.bf16(w2_off, FFC * D)
    hid = A.bf16(hid_off, FFC * 512)
    h1btb = A.bf16(h1b_off, KC * 512)
    s2 = A.f32(s2_off, KC * 512)

    @block.gpsimd
    def _(q):
        for s, n in ready:
            q.wait_ge(s.h, n)
        q.dma_start(out=w1.rearrange("p (k c) -> p k c", k=KC),
                    in_=kb.w["ffn_Wff1"].rearrange("(k p) c -> p k c", p=128)).then_inc(cw.h, 16)
        cw.n += 16
        q.dma_start(out=w2.rearrange("p (f c) -> p f c", f=FFC),
                    in_=kb.w["ffn_Wff2"].rearrange("(f p) c -> p f c", p=128)).then_inc(cw.h, 16)
        cw.n += 16
    w_tok = (cw, cw.n)

    state = {"ps_free": list(ready)}
    bank_free = [[] for _ in range(4)]
    h1b_free = list(ready)
    h132_free = [[], []]
    s2_free = []
    h2c_busy = [[], []]
    hid_writable = []

    @block.tensor
    def _(t):
        for s, n in ready:
            t.wait_ge(s.h, n)
        t.wait_ge(cw.h, w_tok[1])

    for tb in range(NTB):
        elem, tbe = tb // 4, tb % 4

        @block.sync
        def _(q, elem=elem, tbe=tbe):
            for s, n in h1b_free:
                q.wait_ge(s.h, n)
            src = kb.ds["h1b"].ap()[elem, :, tbe * 512:(tbe + 1) * 512].rearrange(
                "(c p) w -> p c w", p=128)
            q.dma_start(out=h1btb.rearrange("p (c w) -> p c w", c=KC), in_=src).then_inc(ca.h, 16)
            ca.n += 16
        h1b_tok = (ca, ca.n)

        hid_toks = []
        for ffc in range(FFC):
            bank = ffc % 4

            @block.tensor
            def _(t, ffc=ffc, bank=bank, h1b_tok=h1b_tok):
                if ffc == 0:
                    t.wait_ge(ca.h, h1b_tok[1])
                for s, n in bank_free[bank]:
                    t.wait_ge(s.h, n)
                for k in range(KC):
                    ins = t.matmul(PS[bank][:, 0:512],
                                   w1[:, k * FFD + ffc * 128:k * FFD + (ffc + 1) * 128],
                                   h1btb[:, k * 512:(k + 1) * 512],
                                   start=(k == 0), stop=(k == KC - 1))
                pe.inc(ins)
            pe_t = pe.n

            @block.vector
            def _(v, ffc=ffc, bank=bank, pe_t=pe_t, hw=list(hid_writable)):
                v.wait_ge(pe.h, pe_t)
                for s, n in hw:
                    v.wait_ge(s.h, n)
                b_ap = C.f32(kb.cp_off + 4 * (CP["bff1"] + ffc), 1)
                dv.inc(v.tensor_scalar(hid[:, ffc * 512:(ffc + 1) * 512],
                                       PS[bank][:, 0:512], b_ap, 0.0,
                                       ALU.add, ALU.max))
            bank_free[bank] = [(dv, dv.n)]
            hid_toks.append((dv, dv.n))
        h1b_free = [(pe, pe.n)]

        dc_toks = []
        for dc in range(KC):
            bank = dc % 4
            par = dc % 2

            @block.sync
            def _(q, elem=elem, tbe=tbe, dc=dc, par=par):
                for s, n in h132_free[par]:
                    q.wait_ge(s.h, n)
                q.dma_start(out=A.f32(h132_off + par * 512 * 4, 512),
                            in_=kb.ds["h1f"].ap()[elem, dc * 128:(dc + 1) * 128,
                                                  tbe * 512:(tbe + 1) * 512]).then_inc(ca.h, 16)
                ca.n += 16
            h132_tok = (ca, ca.n)

            @block.tensor
            def _(t, dc=dc, bank=bank, hid_toks=list(hid_toks)):
                if dc == 0:
                    for s, n in hid_toks:
                        t.wait_ge(s.h, n)
                for s, n in bank_free[bank]:
                    t.wait_ge(s.h, n)
                for ffc in range(FFC):
                    ins = t.matmul(PS[bank][:, 0:512],
                                   w2[:, ffc * D + dc * 128:ffc * D + (dc + 1) * 128],
                                   hid[:, ffc * 512:(ffc + 1) * 512],
                                   start=(ffc == 0), stop=(ffc == FFC - 1))
                pe.inc(ins)
            pe_t = pe.n

            @block.vector
            def _(v, dc=dc, bank=bank, par=par, pe_t=pe_t, h132_tok=h132_tok):
                v.wait_ge(pe.h, pe_t)
                v.wait_ge(ca.h, h132_tok[1])
                for s, n in s2_free:
                    v.wait_ge(s.h, n)
                b2 = C.f32(kb.cp_off + 4 * (CP["bff2"] + dc), 1)
                dv.inc(v.scalar_tensor_tensor(s2[:, dc * 512:(dc + 1) * 512],
                                              PS[bank][:, 0:512], b2,
                                              A.f32(h132_off + par * 512 * 4, 512),
                                              ALU.add, ALU.add))
            bank_free[bank] = [(dv, dv.n)]
            h132_free[par] = [(dv, dv.n)]
            dc_toks.append((dv, dv.n))
        s2_free = []

        def y_put(c, tmp, tok, tb=tb, elem=elem, tbe=tbe):
            h2c = A.bf16(h2c_off + (c % 2) * 512 * 2, 512)
            toks = []

            @block.vector
            def _(v, c=c):
                for s, n in h2c_busy[c % 2]:
                    v.wait_ge(s.h, n)
                dv.inc(v.tensor_copy(out=h2c, in_=tmp))
            bf_tok = (dv, dv.n)

            @block.scalar
            def _(q, c=c, bf_tok=bf_tok, tok=tok):
                q.wait_ge(tok[0].h, tok[1])
                q.dma_start(out=kb.ds["h2f"].ap()[elem, c * 128:(c + 1) * 128,
                                                  tbe * 512:(tbe + 1) * 512],
                            in_=tmp).then_inc(cs.h, 16)
                cs.n += 16
                toks.append((cs, cs.n))
                q.wait_ge(bf_tok[0].h, bf_tok[1])
                q.dma_start(out=kb.ds["h2b"].ap()[elem, c * 128:(c + 1) * 128,
                                                  tbe * 512:(tbe + 1) * 512],
                            in_=h2c).then_inc(cs.h, 16)
                cs.n += 16
            h2c_busy[c % 2] = [(cs, cs.n)]
            return toks

        state = _ln_block(kb, s2, scr, y_put, gamma=CP["ln2_g"], beta=CP["ln2_b"],
                          ready=dc_toks, state=state)
        s2_free = [(dv, kb.sem("dv").n), (ac, kb.sem("ac").n)]
        hid_writable = state["xbf_free"]
    A.release()
    kb.ln_ps_free = state["ps_free"]
    kb.ffg_done = [(cs, cs.n)]


def _p_gate(kb, ready):
    A, C, block, nc, PS = kb.A, kb.C, kb.block, kb.nc, kb.PS
    cw, ca, cs, pe, dv, ac = (kb.sem(n) for n in ["cw", "ca", "cs", "pe", "dv", "ac"])
    A.mark()
    wg_off = A.alloc(KC * NE * 2)
    h2b_off = A.alloc(KC * 512 * 2)
    lg_off = A.alloc(512 * 4)
    gt_off = A.alloc(32 * 4)
    gg_off = A.alloc(32 * 4)
    sm_off = A.alloc(16 * 4)
    gr_off = A.alloc(512 * 2)
    wg = A.bf16(wg_off, KC * NE)
    h2btb = A.bf16(h2b_off, KC * 512)
    lg = A.f32(lg_off, 512, parts=NE)
    gt = A.f32(gt_off, 32)
    gg = A.f32(gg_off, 32)
    gr = A.bf16(gr_off, 512, parts=NE)
    identf = C.f32(kb.identf_off, 128)
    gbf = [[] for _ in range(4)]

    @block.gpsimd
    def _(q):
        for s, n in ready:
            q.wait_ge(s.h, n)
        q.dma_start(out=wg.rearrange("p (k c) -> p k c", k=KC),
                    in_=kb.w["ffn_Wg"].rearrange("(k p) c -> p k c", p=128)).then_inc(cw.h, 16)
        cw.n += 16
    wg_tok = (cw, cw.n)

    h2b_free = list(ready)
    lg_free = []
    gr_free = []
    for tb in range(NTB):
        elem, tbe = tb // 4, tb % 4
        bankA = tb % 2
        bankB = 2 + tb % 2

        @block.sync
        def _(q, elem=elem, tbe=tbe):
            for s, n in h2b_free:
                q.wait_ge(s.h, n)
            src = kb.ds["h2b"].ap()[elem, :, tbe * 512:(tbe + 1) * 512].rearrange(
                "(c p) w -> p c w", p=128)
            q.dma_start(out=h2btb.rearrange("p (c w) -> p c w", c=KC), in_=src).then_inc(ca.h, 16)
            ca.n += 16
        h_tok = (ca, ca.n)

        @block.tensor
        def _(t, bankA=bankA, h_tok=h_tok, tb=tb):
            t.wait_ge(ca.h, h_tok[1])
            if tb == 0:
                t.wait_ge(cw.h, wg_tok[1])
                for s, n in ready:
                    t.wait_ge(s.h, n)
            for s, n in gbf[bankA]:
                t.wait_ge(s.h, n)
            for k in range(KC):
                ins = t.matmul(PS[bankA][0:NE, 0:512], wg[:, k * NE:(k + 1) * NE],
                               h2btb[:, k * 512:(k + 1) * 512],
                               start=(k == 0), stop=(k == KC - 1))
            pe.inc(ins)
        pe_lg = pe.n
        h2b_free = [(pe, pe.n)]

        @block.vector
        def _(v, pe_lg=pe_lg, lgf=list(lg_free), bankA=bankA):
            v.wait_ge(pe.h, pe_lg)
            for s, n in lgf:
                v.wait_ge(s.h, n)
            dv.inc(v.tensor_copy(out=lg, in_=PS[bankA][0:NE, 0:512]))
        ac_lg = dv.n
        gbf[bankA] = [(dv, dv.n)]

        @block.tensor
        def _(t, bankB=bankB, ac_lg=ac_lg):
            t.wait_ge(dv.h, ac_lg)
            for s, n in gbf[bankB]:
                t.wait_ge(s.h, n)
            for g in range(4):
                ins = t.transpose(PS[bankB][:, g * 8:(g + 1) * 8],
                                  lg[:, g * 128:(g + 1) * 128], identf[0:NE, 0:NE])
            pe.inc(ins)
        pe_tr = pe.n
        lg_free = [(pe, pe.n)]

        @block.vector
        def _(v, bankB=bankB, pe_tr=pe_tr):
            v.wait_ge(pe.h, pe_tr)
            dv.inc(v.tensor_copy(out=gt, in_=PS[bankB][:, 0:32]))
        ac_gt = dv.n
        gbf[bankB] = [(dv, dv.n)]

        for g in range(4):
            @block.vector
            def _(v, g=g, ac_gt=ac_gt):
                if g == 0:
                    v.wait_ge(dv.h, ac_gt)
                gsl = gt[:, g * 8:(g + 1) * 8]
                dv.inc(v.reduce_max(A.f32(sm_off, 1), gsl, axis=AX.X))
                v.wait_ge(dv.h, dv.n)
                dv.inc(v.tensor_scalar_mul(A.f32(sm_off + 4, 1), A.f32(sm_off, 1), -1.0))
            dv_mx = dv.n

            @block.scalar
            def _(sq, g=g, dv_mx=dv_mx):
                sq.wait_ge(dv.h, dv_mx)
                ac.inc(sq.activation(out=gg[:, g * 8:(g + 1) * 8],
                                     in_=gt[:, g * 8:(g + 1) * 8], func=AF.Exp,
                                     bias=A.f32(sm_off + 4, 1), scale=1.0,
                                     accum_out=A.f32(sm_off + 8, 1)))
            ac_ex = ac.n

            @block.vector
            def _(v, g=g, ac_ex=ac_ex):
                v.wait_ge(ac.h, ac_ex)
                dv.inc(v.reciprocal(A.f32(sm_off + 12, 1), A.f32(sm_off + 8, 1)))
                v.wait_ge(dv.h, dv.n)
                gsl = gt[:, g * 8:(g + 1) * 8]
                v.tensor_scalar(gg[:, g * 8:(g + 1) * 8], gsl, A.f32(sm_off, 1), None,
                                ALU.is_equal)
                dv.inc(v.tensor_scalar(gg[:, g * 8:(g + 1) * 8], gg[:, g * 8:(g + 1) * 8],
                                       A.f32(sm_off + 12, 1), None, ALU.mult))
            dv_gg = dv.n

        @block.tensor
        def _(t, bankB=bankB, dv_gg=dv_gg):
            t.wait_ge(dv.h, dv_gg)
            for s, n in gbf[bankB]:
                t.wait_ge(s.h, n)
            for g in range(4):
                ins = t.transpose(PS[bankB][0:NE, g * 128:(g + 1) * 128],
                                  gg[:, g * 8:(g + 1) * 8], identf)
            pe.inc(ins)
        pe_gb = pe.n

        @block.vector
        def _(v, bankB=bankB, pe_gb=pe_gb, grf=list(gr_free)):
            v.wait_ge(pe.h, pe_gb)
            for s, n in grf:
                v.wait_ge(s.h, n)
            dv.inc(v.tensor_copy(out=gr, in_=PS[bankB][0:NE, 0:512]))
        ac_gr = dv.n
        gbf[bankB] = [(dv, dv.n)]

        @block.sync
        def _(q, tb=tb, ac_gr=ac_gr):
            q.wait_ge(dv.h, ac_gr)
            q.dma_start(out=kb.ds["gall"].ap()[:, tb * 512:(tb + 1) * 512],
                        in_=gr).then_inc(cs.h, 16)
            cs.n += 16
        gr_free = [(cs, cs.n)]
    A.release()
    kb.gate_done = [(cs, cs.n)]


def _p_moe(kb, ready):
    A, C, block, nc, PS = kb.A, kb.C, kb.block, kb.nc, kb.PS
    cw, ca, cs, cb, pe, dv, ac = (kb.sem(n) for n in ["cw", "ca", "cs", "cb", "pe", "dv", "ac"])
    A.mark()
    w1_off = A.alloc(KC * FFD * 2)
    w2_off = A.alloc(FFC * D * 2)
    hid_off = A.alloc(FFC * 512 * 2)
    h2tb_off = A.alloc(KC * 512 * 2)
    gb_off = A.alloc(2 * 512 * 2)
    stage_off = A.alloc(KC * 512 * 4)
    be1_off = A.alloc(FFD * 2)
    ge_off = A.alloc(2 * 512 * 2)
    ga_off = A.alloc(2 * 512 * 2)
    be2_off = A.alloc(D * 2)
    w1 = A.bf16(w1_off, KC * FFD)
    w2 = A.bf16(w2_off, FFC * D)
    hid = A.bf16(hid_off, FFC * 512)
    h2tb = A.bf16(h2tb_off, KC * 512)
    gb_ap = lambda p_: A.bf16(gb_off + p_ * 512 * 2, 512)
    stage = A.f32(stage_off, KC * 512)
    be1r = A.bf16(be1_off, FFD, parts=1)
    ge_ap = lambda p_: A.bf16(ge_off + p_ * 512 * 2, 512, parts=1)
    ga_ap = lambda p_: A.bf16(ga_off + p_ * 512 * 2, 512, parts=NE)
    be2 = A.bf16(be2_off, D, parts=NE)
    ones_row = C.bf16(kb.ones_row_bf_off, 128, parts=1)

    @block.gpsimd
    def _(q):
        for s, n in ready:
            q.wait_ge(s.h, n)
        q.dma_start(out=be2, in_=kb.w["ffn_be2"][:, :]).then_inc(cw.h, 16)
        cw.n += 16
    be2_tok = (cw, cw.n)

    bank_free = [[] for _ in range(4)]
    h2_free = list(ready)
    gb_free = [[], []]
    ge_free = [[], []]
    ga_free = [[], []]
    stage_free = []
    acc_state = [[] for _ in range(NTB)]
    hid_writable = []
    w1_tok = {}
    w2_tok = {}

    def emit_w1(e, wf):
        @block.gpsimd
        def _(q, e=e, wf=list(wf)):
            for s, n in wf:
                q.wait_ge(s.h, n)
            q.dma_start(out=w1.rearrange("p (k c) -> p k c", k=KC),
                        in_=kb.w["ffn_We1"][e].rearrange("(k p) c -> p k c", p=128)).then_inc(cw.h, 16)
            cw.n += 16
            q.dma_start(out=be1r, in_=kb.w["ffn_be1"][e][None, :]).then_inc(cw.h, 16)
            cw.n += 16
        w1_tok[e] = (cw, cw.n)

    def emit_w2(e, wf):
        @block.gpsimd
        def _(q, e=e, wf=list(wf)):
            for s, n in wf:
                q.wait_ge(s.h, n)
            q.dma_start(out=w2.rearrange("p (f c) -> p f c", f=FFC),
                        in_=kb.w["ffn_We2"][e].rearrange("(f p) c -> p f c", p=128)).then_inc(cw.h, 16)
            cw.n += 16
        w2_tok[e] = (cw, cw.n)

    emit_w1(0, ready)
    emit_w2(0, ready)

    for e in range(NE):
        for tb in range(NTB):
            elem, tbe = tb // 4, tb % 4
            par = tb % 2

            @block.sync
            def _(q, e=e, tb=tb, elem=elem, tbe=tbe, par=par, h2f=list(h2_free)):
                for s, n in h2f + ge_free[par] + ga_free[par]:
                    q.wait_ge(s.h, n)
                src = kb.ds["h2b"].ap()[elem, :, tbe * 512:(tbe + 1) * 512].rearrange(
                    "(c p) w -> p c w", p=128)
                q.dma_start(out=h2tb.rearrange("p (c w) -> p c w", c=KC),
                            in_=src).then_inc(ca.h, 16)
                ca.n += 16
                q.dma_start(out=ge_ap(par), in_=kb.ds["gall"].ap()[e:e + 1, tb * 512:(tb + 1) * 512]
                            ).then_inc(ca.h, 16)
                ca.n += 16
                if e == NE - 1:
                    q.dma_start(out=ga_ap(par), in_=kb.ds["gall"].ap()[:, tb * 512:(tb + 1) * 512]
                                ).then_inc(ca.h, 16)
                    ca.n += 16
            ld_tok = (ca, ca.n)

            bankg = 3

            @block.tensor
            def _(t, e=e, tb=tb, par=par, bankg=bankg, ld_tok=ld_tok,
                  gbf=list(gb_free[par])):
                t.wait_ge(ca.h, ld_tok[1])
                for s, n in bank_free[bankg]:
                    t.wait_ge(s.h, n)
                for s, n in gbf:
                    t.wait_ge(s.h, n)
                pe.inc(t.matmul(PS[bankg][:, 0:512], ones_row, ge_ap(par),
                                start=True, stop=True))
            pe_gb = pe.n

            @block.vector
            def _(v, par=par, bankg=bankg, pe_gb=pe_gb):
                v.wait_ge(pe.h, pe_gb)
                dv.inc(v.tensor_copy(out=gb_ap(par), in_=PS[bankg][:, 0:512]))
            bank_free[bankg] = [(dv, dv.n)]
            gb_tok = (dv, dv.n)

            @block.vector
            def _(v, par=par, gb_tok=gb_tok):
                v.wait_ge(dv.h, gb_tok[1])
                for k in range(KC):
                    ins = v.tensor_mul(h2tb[:, k * 512:(k + 1) * 512],
                                       h2tb[:, k * 512:(k + 1) * 512], gb_ap(par))
                dv.inc(ins)
            gh_tok = (dv, dv.n)
            gb_free[par] = [(dv, dv.n)]

            hid_toks = []
            for ffc in range(FFC):
                bank = ffc % 3

                @block.tensor
                def _(t, e=e, tb=tb, ffc=ffc, par=par, bank=bank, gh_tok=gh_tok,
                      w1t=w1_tok[e]):
                    if ffc == 0:
                        t.wait_ge(dv.h, gh_tok[1])
                        if tb == 0:
                            t.wait_ge(cw.h, w1t[1])
                    for s, n in bank_free[bank]:
                        t.wait_ge(s.h, n)
                    for k in range(KC):
                        t.matmul(PS[bank][:, 0:512],
                                 w1[:, k * FFD + ffc * 128:k * FFD + (ffc + 1) * 128],
                                 h2tb[:, k * 512:(k + 1) * 512],
                                 start=(k == 0), stop=False)
                    pe.inc(t.matmul(PS[bank][:, 0:512],
                                    be1r[0:1, ffc * 128:(ffc + 1) * 128], ge_ap(par),
                                    start=False, stop=True))
                pe_t = pe.n

                @block.vector
                def _(v, ffc=ffc, bank=bank, pe_t=pe_t, hw=list(hid_writable)):
                    v.wait_ge(pe.h, pe_t)
                    for s, n in hw:
                        v.wait_ge(s.h, n)
                    dv.inc(v.tensor_scalar(hid[:, ffc * 512:(ffc + 1) * 512],
                                           PS[bank][:, 0:512], 0.0, None, ALU.max))
                bank_free[bank] = [(dv, dv.n)]
                hid_toks.append((dv, dv.n))
            hid_writable = []
            h2_free = [(pe, pe.n)]
            ge_free[par] = [(pe, pe.n)]
            if tb == NTB - 1 and e + 1 < NE:
                emit_w1(e + 1, [(pe, pe.n)])

            dc_toks = []
            for dc in range(KC):
                bank = dc % 3

                @block.tensor
                def _(t, e=e, tb=tb, dc=dc, par=par, bank=bank, hid_toks=list(hid_toks),
                      w2t=w2_tok[e]):
                    if dc == 0:
                        for s, n in hid_toks:
                            t.wait_ge(s.h, n)
                        if tb == 0:
                            t.wait_ge(cw.h, w2t[1])
                            if e == NE - 1:
                                t.wait_ge(cw.h, be2_tok[1])
                    for s, n in bank_free[bank]:
                        t.wait_ge(s.h, n)
                    last = (e != NE - 1)
                    for ffc in range(FFC):
                        ins = t.matmul(PS[bank][:, 0:512],
                                       w2[:, ffc * D + dc * 128:ffc * D + (dc + 1) * 128],
                                       hid[:, ffc * 512:(ffc + 1) * 512],
                                       start=(ffc == 0),
                                       stop=(last and ffc == FFC - 1))
                    if e == NE - 1:
                        ins = t.matmul(PS[bank][:, 0:512],
                                       be2[:, dc * 128:(dc + 1) * 128], ga_ap(par),
                                       start=False, stop=True)
                    pe.inc(ins)
                pe_t = pe.n

                @block.vector
                def _(v, dc=dc, bank=bank, pe_t=pe_t, sf=list(stage_free)):
                    v.wait_ge(pe.h, pe_t)
                    for s, n in sf:
                        v.wait_ge(s.h, n)
                    dv.inc(v.tensor_copy(out=stage[:, dc * 512:(dc + 1) * 512],
                                         in_=PS[bank][:, 0:512]))
                bank_free[bank] = [(dv, dv.n)]
                dc_toks.append((dv, dv.n))
            if e == NE - 1:
                ga_free[par] = [(pe, pe.n)]
            if tb == NTB - 1 and e + 1 < NE:
                emit_w2(e + 1, [(pe, pe.n)])
            stage_free = []

            accv = kb.ds["moeacc"].ap()[:, tb * 512:(tb + 1) * 512].rearrange(
                "(c p) w -> p c w", p=128)

            @block.gpsimd
            def _(q, e=e, tb=tb, dc_toks=list(dc_toks), accst=list(acc_state[tb])):
                if e > 0:
                    for s, n in dc_toks + accst:
                        q.wait_ge(s.h, n)
                    q.dma_start(out=stage.rearrange("p (c w) -> p c w", c=KC), in_=accv,
                                accum_op=ALU.add).then_inc(cb.h, 16)
                    cb.n += 16
            acc_tok = (cb, cb.n)

            @block.scalar
            def _(q, e=e, tb=tb, dc_toks=list(dc_toks), acc_tok=acc_tok,
                  accst=list(acc_state[tb])):
                if e > 0:
                    q.wait_ge(cb.h, acc_tok[1])
                else:
                    for s, n in dc_toks + accst:
                        q.wait_ge(s.h, n)
                q.dma_start(out=accv, in_=stage.rearrange("p (c w) -> p c w", c=KC)
                            ).then_inc(cs.h, 16)
                cs.n += 16
            acc_state[tb] = [(cs, cs.n)]
            stage_free = [(cs, cs.n)]
    A.release()
    kb.moe_done = [(cs, cs.n)]


def _p_out(kb, x_src, out_dst, scale, ready):
    A, C, block, nc, PS = kb.A, kb.C, kb.block, kb.nc, kb.PS
    cw, ca, cs, pe, dv, ac = (kb.sem(n) for n in ["cw", "ca", "cs", "pe", "dv", "ac"])
    A.mark()
    wo_off = A.alloc(KC * D * 2)
    acc_off = A.alloc(KC * 512 * 4)
    h2f_off = A.alloc(KC * 512 * 4)
    s3_off = A.alloc(KC * 512 * 4)
    h3_off = A.alloc(KC * 512 * 2)
    xr_off = A.alloc(KC * 512 * 4)
    oc_off = A.alloc(2 * 512 * 4)
    scr = ln_scr_alloc(A)
    wo = A.bf16(wo_off, KC * D)
    acc = A.f32(acc_off, KC * 512)
    h2f = A.f32(h2f_off, KC * 512)
    s3 = A.f32(s3_off, KC * 512)
    h3 = A.bf16(h3_off, KC * 512)
    xr = A.f32(xr_off, KC * 512)

    @block.gpsimd
    def _(q):
        for s, n in ready:
            q.wait_ge(s.h, n)
        q.dma_start(out=wo.rearrange("p (k c) -> p k c", k=KC),
                    in_=kb.w["ffn_Wout"].rearrange("(k p) c -> p k c", p=128)).then_inc(cw.h, 16)
        cw.n += 16
    wo_tok = (cw, cw.n)

    state = {"ps_free": list(ready)}
    bank_free = [[] for _ in range(4)]
    in_free = list(ready)
    s3_free = []
    h3_free = []
    xr_free = list(ready)
    oc_busy = [[], []]

    @block.tensor
    def _(t):
        for s, n in ready:
            t.wait_ge(s.h, n)
        t.wait_ge(cw.h, wo_tok[1])

    for tb in range(NTB):
        elem, tbe = tb // 4, tb % 4

        @block.sync
        def _(q, elem=elem, tbe=tbe, tb=tb, inf=list(in_free)):
            for s, n in inf:
                q.wait_ge(s.h, n)
            q.dma_start(out=acc.rearrange("p (c w) -> p c w", c=KC),
                        in_=kb.ds["moeacc"].ap()[:, tb * 512:(tb + 1) * 512].rearrange(
                            "(c p) w -> p c w", p=128)).then_inc(ca.h, 16)
            ca.n += 16
            q.dma_start(out=h2f.rearrange("p (c w) -> p c w", c=KC),
                        in_=kb.ds["h2f"].ap()[elem, :, tbe * 512:(tbe + 1) * 512].rearrange(
                            "(c p) w -> p c w", p=128)).then_inc(ca.h, 16)
            ca.n += 16
        ld_tok = (ca, ca.n)

        @block.vector
        def _(v, ld_tok=ld_tok, s3f=list(s3_free)):
            v.wait_ge(ca.h, ld_tok[1])
            for s, n in s3f:
                v.wait_ge(s.h, n)
            for c in range(KC):
                ins = v.tensor_add(s3[:, c * 512:(c + 1) * 512],
                                   h2f[:, c * 512:(c + 1) * 512],
                                   acc[:, c * 512:(c + 1) * 512])
            dv.inc(ins)
        s3_tok = (dv, dv.n)
        in_free = [(dv, dv.n)]

        def y_put(c, tmp, tok):
            @block.vector
            def _(v, c=c):
                for s, n in h3_free:
                    v.wait_ge(s.h, n)
                dv.inc(v.tensor_copy(out=h3[:, c * 512:(c + 1) * 512], in_=tmp))
            return [(dv, dv.n)]

        state = _ln_block(kb, s3, scr, y_put, gamma=CP["ln3_g"], beta=CP["ln3_b"],
                          ready=[s3_tok], state=state)
        h3_tok = (dv, kb.sem("dv").n)
        s3_free = [(dv, kb.sem("dv").n), (ac, kb.sem("ac").n)]

        @block.sync
        def _(q, elem=elem, tbe=tbe, xrf=list(xr_free)):
            for s, n in xrf:
                q.wait_ge(s.h, n)
            q.dma_start(out=xr.rearrange("p (c w) -> p c w", c=KC),
                        in_=x_src[elem][:, tbe * 512:(tbe + 1) * 512].rearrange(
                            "(c p) w -> p c w", p=128)).then_inc(ca.h, 16)
            ca.n += 16
        xr_tok = (ca, ca.n)

        h3_free = []
        for dc in range(KC):
            bank = dc % 4
            par = dc % 2

            @block.tensor
            def _(t, dc=dc, bank=bank, h3_tok=h3_tok):
                if dc == 0:
                    t.wait_ge(dv.h, h3_tok[1])
                for s, n in bank_free[bank]:
                    t.wait_ge(s.h, n)
                for k in range(KC):
                    ins = t.matmul(PS[bank][:, 0:512],
                                   wo[:, k * D + dc * 128:k * D + (dc + 1) * 128],
                                   h3[:, k * 512:(k + 1) * 512],
                                   start=(k == 0), stop=(k == KC - 1))
                pe.inc(ins)
            pe_t = pe.n
            h3_free.append((pe, pe.n))

            @block.vector
            def _(v, dc=dc, bank=bank, par=par, pe_t=pe_t, xr_tok=xr_tok,
                  ocb=list(oc_busy[par])):
                v.wait_ge(pe.h, pe_t)
                v.wait_ge(ca.h, xr_tok[1])
                for s, n in ocb:
                    v.wait_ge(s.h, n)
                dv.inc(v.scalar_tensor_tensor(A.f32(oc_off + par * 512 * 4, 512),
                                              PS[bank][:, 0:512], scale,
                                              xr[:, dc * 512:(dc + 1) * 512],
                                              ALU.mult, ALU.add))
            bank_free[bank] = [(dv, dv.n)]
            oc_tok = (dv, dv.n)

            @block.sync
            def _(q, elem=elem, tbe=tbe, dc=dc, par=par, oc_tok=oc_tok):
                q.wait_ge(dv.h, oc_tok[1])
                q.dma_start(out=out_dst[elem][dc * 128:(dc + 1) * 128,
                                              tbe * 512:(tbe + 1) * 512],
                            in_=A.f32(oc_off + par * 512 * 4, 512)).then_inc(cs.h, 16)
                cs.n += 16
            oc_busy[par] = [(cs, cs.n)]
        xr_free = [(dv, dv.n)]
    A.release()
    kb.ln_ps_free = state["ps_free"]
    kb.out_done = [(cs, cs.n)]


def _p_sqkv(kb, z_off, q_off, k_off, vt_off, wbuf_off, bvr_off, ready):
    A, C, block, nc, PS = kb.A, kb.C, kb.block, kb.nc, kb.PS
    cw, pe, ac, dv = kb.sem("cw"), kb.sem("pe"), kb.sem("ac"), kb.sem("dv")
    names = ["attn_wq", "attn_wk", "attn_wv"]
    wb = lambda par: A.bf16(wbuf_off + par * KC * 512 * 2, KC * 512)
    z_ap = lambda k: A.bf16(z_off + 2 * k * S, S)
    bvr = A.bf16(bvr_off, D, parts=1)
    ones_row = C.bf16(kb.ones_row_bf_off, 128, parts=1)

    @block.tensor
    def _(t):
        for s, n in ready:
            t.wait_ge(s.h, n)

    @block.gpsimd
    def _(q):
        for s, n in ready:
            q.wait_ge(s.h, n)
        q.dma_start(out=bvr, in_=kb.w["attn_bv"][None, :]).then_inc(cw.h, 16)
        cw.n += 16
    bv_tok = (cw, cw.n)

    wload = {}
    wfree = [[], []]
    bank_free = [[] for _ in range(4)]

    def emit_load(cb):
        par = cb % 2
        mat, half = names[cb // 2], cb % 2

        @block.gpsimd
        def _(q):
            for s, n in wfree[par]:
                q.wait_ge(s.h, n)
            src = kb.w[mat][:, half * 512:(half + 1) * 512].rearrange("(k p) c -> p k c", p=128)
            q.dma_start(out=wb(par).rearrange("p (k c) -> p k c", k=KC), in_=src).then_inc(cw.h, 16)
            cw.n += 16
        wload[cb] = (cw, cw.n)

    emit_load(0)
    emit_load(1)
    nbank = 0
    for cb in range(6):
        par = cb % 2
        if cb < 4:
            dst_off = q_off if cb < 2 else k_off
            bias = "bq" if cb < 2 else "bk"
            fbase = (cb % 2) * 4
            for m in range(4):
                f = fbase + m
                for n in range(4):
                    bank = nbank % 4
                    nbank += 1

                    @block.tensor
                    def _(t, cb=cb, par=par, m=m, n=n, bank=bank):
                        t.wait_ge(cw.h, wload[cb][1])
                        for s, nn in bank_free[bank]:
                            t.wait_ge(s.h, nn)
                        for k in range(KC):
                            ins = t.matmul(PS[bank][:, 0:512],
                                           wb(par)[:, k * 512 + m * 128:k * 512 + (m + 1) * 128],
                                           z_ap(k)[:, n * 512:(n + 1) * 512],
                                           start=(k == 0), stop=(k == KC - 1))
                        pe.inc(ins)
                    pe_t = pe.n

                    @block.vector
                    def _(v, f=f, n=n, bank=bank, pe_t=pe_t, dst_off=dst_off, bias=bias):
                        v.wait_ge(pe.h, pe_t)
                        dst = kb.A.bf16(dst_off + 2 * (f * S + n * 512), 512)
                        b_ap = C.f32(kb.cp_off + 4 * (CP[bias] + f), 1)
                        dv.inc(v.tensor_scalar(dst, PS[bank][:, 0:512], b_ap, None, ALU.add))
                    bank_free[bank] = [(dv, dv.n)]
            wfree[par] = [(pe, pe.n)]
            if cb + 2 < 6:
                emit_load(cb + 2)
        else:
            vb = cb - 4
            for tt in range(16):
                bank = nbank % 4
                nbank += 1

                @block.tensor
                def _(t, cb=cb, par=par, tt=tt, vb=vb, bank=bank):
                    t.wait_ge(cw.h, wload[cb][1])
                    if tt == 0:
                        t.wait_ge(cw.h, bv_tok[1])
                    for s, nn in bank_free[bank]:
                        t.wait_ge(s.h, nn)
                    for k in range(KC):
                        t.matmul(PS[bank][:, 0:512],
                                 z_ap(k)[:, tt * 128:(tt + 1) * 128],
                                 wb(par)[:, k * 512:(k + 1) * 512],
                                 start=(k == 0), stop=False)
                    pe.inc(t.matmul(PS[bank][:, 0:512], ones_row,
                                    bvr[0:1, vb * 512:(vb + 1) * 512],
                                    start=False, stop=True))
                pe_t = pe.n

                @block.vector
                def _(v, tt=tt, vb=vb, bank=bank, pe_t=pe_t):
                    v.wait_ge(pe.h, pe_t)
                    dst = kb.A.bf16(vt_off + 2 * (tt * 1024 + vb * 512), 512)
                    dv.inc(v.tensor_copy(out=dst, in_=PS[bank][:, 0:512]))
                bank_free[bank] = [(dv, dv.n)]
            wfree[par] = [(pe, pe.n)]
            if cb + 2 < 6:
                emit_load(cb + 2)

    kb.sqkv_done = [(ac, ac.n), (dv, dv.n)]


def _sp_scr(kb):
    A = kb.A
    return {
        "sc": A.alloc(3 * WIN * 4), "p": A.alloc(2 * WIN * 2), "pT": A.alloc(2 * WIN * 2),
        "cst": A.alloc(2 * S * 2), "nm": A.alloc(16 * 4),
    }


def _p_satt(kb, q_off, k_off, vt_off, ctx_off, mask_off, scr, ready):
    A, C, block, nc, PS, PT = kb.A, kb.C, kb.block, kb.nc, kb.PS, kb.PT
    pe, dv, ac, cs = kb.sem("pe"), kb.sem("dv"), kb.sem("ac"), kb.sem("cs")
    ident = C.bf16(kb.ident_off, 128)
    sc_ap = lambda sl, w: A.f32(scr["sc"] + sl * WIN * 4, w)
    p_ap = lambda par, w: A.bf16(scr["p"] + par * WIN * 2, w)
    pT_ap = lambda par, w: A.bf16(scr["pT"] + par * WIN * 2, w)
    cst = lambda hp: A.bf16(scr["cst"] + hp * S * 2, S, parts=64)
    nm_ap = lambda i: A.f32(scr["nm"] + i * 4, 1)
    MW = 512 + WIN + 512

    units = []
    for h in range(NH):
        for c in range(NCH):
            for qt in range(2):
                units.append((h, c, qt))
    NU = len(units)
    first = {"t": True, "v": True, "s": True}

    def wt(eng, toks, key=None):
        if key is not None and first[key]:
            first[key] = False
            toks = list(toks) + list(ready)
        for s, n in toks:
            eng.wait_ge(s.h, n)

    sc_st = [[], [], []]
    p_st = [[], []]
    pT_st = [[], []]
    PTb_st = [[], []]
    ab_st = [[] for _ in range(4)]
    ctx_col = [[], []]
    cst_tok = {}
    U = [dict() for _ in range(NU)]

    def front(u):
        h, c, qt = units[u]
        bb = (h % 2) * 64
        hc = h // 2
        lo = max(0, 2 * c - 2)
        hi = min(16, 2 * c + 4)
        w128 = hi - lo
        wk = w128 * 128
        mvar = 0 if c == 0 else (2 if c == NCH - 1 else 1)
        mbase_col = [0, 512, 512 + WIN][mvar]
        qcol = c * 256 + qt * 128
        st = U[u]
        st.update(h=h, c=c, qt=qt, bb=bb, hc=hc, lo=lo, w128=w128, wk=wk,
                  scs=u % 3, par=u % 2)
        blocks = []
        o = 0
        while o < wk:
            wb_ = min(512, wk - o)
            blocks.append((o, wb_))
            o += wb_
        for bi, (ko, wb_) in enumerate(blocks):
            bank = (u % 2) * 2 + bi % 2

            @block.tensor
            def _(t, bb=bb, hc=hc, qcol=qcol, lo=lo, ko=ko, wb_=wb_, bank=bank):
                wt(t, ab_st[bank], "t")
                ksl = A.bf16(k_off, KC * S, parts=64, p0=bb)
                qsl = A.bf16(q_off, KC * S, parts=64, p0=bb)
                pe.inc(t.matmul(PS[bank][:, 0:wb_],
                                qsl[:, hc * S + qcol:hc * S + qcol + 128],
                                ksl[:, hc * S + lo * 128 + ko:hc * S + lo * 128 + ko + wb_],
                                start=True, stop=True))
            pe_t = pe.n

            @block.vector
            def _(v, u=u, scs=u % 3, ko=ko, wb_=wb_, bank=bank, pe_t=pe_t, qt=qt,
                  mbase_col=mbase_col, bi=bi):
                toks = [(pe, pe_t)]
                if bi == 0:
                    toks = toks + sc_st[scs]
                wt(v, toks, "v")
                mask = A.bf16(mask_off + 2 * (qt * MW + mbase_col + ko), wb_)
                dv.inc(v.tensor_add(sc_ap(scs, WIN)[:, ko:ko + wb_],
                                    PS[bank][:, 0:wb_], mask))
            ab_st[bank] = [(dv, dv.n)]
        st["sc_tok"] = (dv, dv.n)

    def back1(u):
        st = U[u]
        scs, par, wk = st["scs"], st["par"], st["wk"]

        @block.vector
        def _(v, u=u, scs=scs, wk=wk, par=par):
            v.wait_ge(dv.h, U[u]["sc_tok"][1])
            dv.inc(v.reduce_max(nm_ap(par), sc_ap(scs, wk), axis=AX.X))
            v.wait_ge(dv.h, dv.n)
            dv.inc(v.tensor_scalar_mul(nm_ap(2 + par), nm_ap(par), -0.125))
        dv_nm = dv.n

        @block.scalar
        def _(sq, scs=scs, wk=wk, par=par, dv_nm=dv_nm, pst=list(p_st[par])):
            wt(sq, [(dv, dv_nm)] + pst, "s")
            ac.inc(sq.activation(out=p_ap(par, wk), in_=sc_ap(scs, wk), func=AF.Exp,
                                 bias=nm_ap(2 + par), scale=0.125,
                                 accum_out=nm_ap(4 + par)))
        sc_st[scs] = [(ac, ac.n)]
        st["exp"] = (ac, ac.n)

    def back2(u):
        st = U[u]
        h, bb, hc, lo, w128, wk = st["h"], st["bb"], st["hc"], st["lo"], st["w128"], st["wk"]
        par, qt, c = st["par"], st["qt"], st["c"]
        hp = h % 2
        qcol = c * 256 + qt * 128

        @block.vector
        def _(v, u=u, par=par, wk=wk):
            v.wait_ge(ac.h, U[u]["exp"][1])
            dv.inc(v.reciprocal(nm_ap(6 + par), nm_ap(4 + par)))
            v.wait_ge(dv.h, dv.n)
            dv.inc(v.tensor_scalar(p_ap(par, wk), p_ap(par, wk), nm_ap(6 + par),
                                   None, ALU.mult))
        dv_p = dv.n

        bankb = u % 2

        @block.tensor
        def _(t, w128=w128, par=par, bankb=bankb, dv_p=dv_p):
            wt(t, [(dv, dv_p)] + PTb_st[bankb], "t")
            for i in range(w128):
                ins = t.transpose(PT[bankb][:, i * 128:(i + 1) * 128],
                                  p_ap(par, WIN)[:, i * 128:(i + 1) * 128], ident)
            pe.inc(ins)
        pe_tr = pe.n
        p_st[par] = [(pe, pe.n)]

        @block.vector
        def _(v, w128=w128, par=par, bankb=bankb, pe_tr=pe_tr, ptst=list(pT_st[par])):
            wt(v, [(pe, pe_tr)] + ptst)
            dv.inc(v.tensor_copy(out=pT_ap(par, w128 * 128),
                                 in_=PT[bankb][:, 0:w128 * 128]))
        PTb_st[bankb] = [(dv, dv.n)]
        pT_tok = (dv, dv.n)

        chalf = u % 2
        c0 = chalf * 128

        @block.tensor
        def _(t, h=h, w128=w128, lo=lo, par=par, pT_tok=pT_tok, c0=c0, chalf=chalf):
            t.wait_ge(dv.h, pT_tok[1])
            wt(t, ctx_col[chalf])
            for i in range(w128):
                kbk = lo + i
                ins = t.matmul(PS[4][0:64, c0:c0 + 128],
                               A.bf16(vt_off + 2 * (kbk * 1024 + h * DH), DH),
                               pT_ap(par, WIN)[:, i * 128:(i + 1) * 128],
                               start=(i == 0), stop=(i == w128 - 1))
            pe.inc(ins)
        pe_ctx = pe.n
        pT_st[par] = [(pe, pe.n)]

        @block.vector
        def _(v, qcol=qcol, pe_ctx=pe_ctx, c0=c0, h=h, hp=hp,
              first_u=(c == 0 and qt == 0)):
            toks = [(pe, pe_ctx)]
            if first_u:
                toks = toks + list(cst_tok.get(h - 2, []))
            wt(v, toks)
            dv.inc(v.tensor_copy(out=cst(hp)[:, qcol:qcol + 128],
                                 in_=PS[4][0:64, c0:c0 + 128]))
        ctx_col[chalf] = [(dv, dv.n)]

        if c == NCH - 1 and qt == 1:
            dv_last = dv.n

            @block.sync
            def _(s, h=h, bb=bb, hc=hc, hp=hp, dv_last=dv_last):
                s.wait_ge(dv.h, dv_last)
                dst = A.bf16(ctx_off + 2 * (hc * S), S, parts=64, p0=bb)
                s.dma_start(out=dst, in_=cst(hp)[:, :]).then_inc(cs.h, 16)
                cs.n += 16
            cst_tok[h] = [(cs, cs.n)]

    SATT_SKEW = 0
    if SATT_SKEW == 0:
        for it in range(NU):
            front(it)
            back1(it)
            back2(it)
    else:
        for it in range(NU + 2):
            if 2 <= it < NU + 2:
                back2(it - 2)
            if 1 <= it < NU + 1:
                back1(it - 1)
            if it < NU:
                front(it)

    kb.satt_done = [(cs, cs.n), (dv, dv.n)]


def _p_swo(kb, elem, ctx_off, x_src, x_dst, ready):
    A, C, block, nc, PS = kb.A, kb.C, kb.block, kb.nc, kb.PS
    cw, ca, cs, pe, dv = (kb.sem(n) for n in ["cw", "ca", "cs", "pe", "dv"])
    A.mark()
    wo_off = A.alloc(KC * D * 2)
    x1_off = A.alloc(KC * 512 * 4)
    oc_off = A.alloc(2 * 512 * 4)
    wo = A.bf16(wo_off, KC * D)
    x1 = A.f32(x1_off, KC * 512)

    @block.gpsimd
    def _(q):
        for s, n in ready:
            q.wait_ge(s.h, n)
        q.dma_start(out=wo.rearrange("p (k c) -> p k c", k=KC),
                    in_=kb.w["attn_wo"].rearrange("(k p) c -> p k c", p=128)).then_inc(cw.h, 16)
        cw.n += 16
    wo_tok = (cw, cw.n)

    bank_free = [[] for _ in range(4)]
    x1_free = []
    oc_busy = [[], []]
    for tb in range(S // 512):
        @block.sync
        def _(q, tb=tb, x1f=list(x1_free)):
            for s, n in x1f:
                q.wait_ge(s.h, n)
            q.dma_start(out=x1.rearrange("p (c w) -> p c w", c=KC),
                        in_=x_src[elem][:, tb * 512:(tb + 1) * 512].rearrange(
                            "(c p) w -> p c w", p=128)).then_inc(ca.h, 16)
            ca.n += 16
        x1_tok = (ca, ca.n)

        for dc in range(KC):
            bank = dc % 4
            par = dc % 2

            @block.tensor
            def _(t, tb=tb, dc=dc, bank=bank):
                if tb == 0 and dc == 0:
                    t.wait_ge(cw.h, wo_tok[1])
                    for s, n in ready:
                        t.wait_ge(s.h, n)
                for s, n in bank_free[bank]:
                    t.wait_ge(s.h, n)
                for k in range(KC):
                    ins = t.matmul(PS[bank][:, 0:512],
                                   wo[:, k * D + dc * 128:k * D + (dc + 1) * 128],
                                   A.bf16(ctx_off + 2 * (k * S + tb * 512), 512),
                                   start=(k == 0), stop=(k == KC - 1))
                pe.inc(ins)
            pe_t = pe.n

            @block.vector
            def _(v, dc=dc, bank=bank, par=par, pe_t=pe_t, x1_tok=x1_tok,
                  ocb=list(oc_busy[par])):
                v.wait_ge(pe.h, pe_t)
                v.wait_ge(ca.h, x1_tok[1])
                for s, n in ocb:
                    v.wait_ge(s.h, n)
                bo = C.f32(kb.cp_off + 4 * (CP["bo"] + dc), 1)
                dv.inc(v.scalar_tensor_tensor(A.f32(oc_off + par * 512 * 4, 512),
                                              PS[bank][:, 0:512], bo,
                                              x1[:, dc * 512:(dc + 1) * 512],
                                              ALU.add, ALU.add))
            bank_free[bank] = [(dv, dv.n)]
            oc_tok = (dv, dv.n)

            @block.sync
            def _(q, tb=tb, dc=dc, par=par, oc_tok=oc_tok):
                q.wait_ge(dv.h, oc_tok[1])
                q.dma_start(out=x_dst[elem][dc * 128:(dc + 1) * 128, tb * 512:(tb + 1) * 512],
                            in_=A.f32(oc_off + par * 512 * 4, 512)).then_inc(cs.h, 16)
                cs.n += 16
            oc_busy[par] = [(cs, cs.n)]
        x1_free = [(dv, dv.n)]
    A.release()
    kb.swo_done = [(cs, cs.n)]


def _p_mln(kb, src, y_sb_off, f32_dram=None, scr=None, ready0=()):
    A, block, nc = kb.A, kb.block, kb.nc
    ca, cs, dv, ac = kb.sem("ca"), kb.sem("cs"), kb.sem("dv"), kb.sem("ac")
    nblk = S // LNW
    state = {"ps_free": list(kb.ln_ps_free), "tmp_busy": None}
    xloaded = []
    xfree = scr.get("xfree", [[], []])

    for b in range(nblk):
        par = b % 2
        x_blk = A.f32(scr["xblk"], KC * LNW)

        @block.sync
        def _(q, b=b, par=par, x_blk=x_blk):
            if b == 0:
                for s, n in ready0:
                    q.wait_ge(s.h, n)
            for s, n in xfree[0] + xfree[1]:
                q.wait_ge(s.h, n)
            src_blk = src[:, b * LNW:(b + 1) * LNW].rearrange("(c p) w -> p c w", p=128)
            out3 = x_blk.rearrange("p (c w) -> p c w", c=KC)
            q.dma_start(out=out3, in_=src_blk).then_inc(ca.h, 16); ca.n += 16
        xloaded.append((ca, ca.n))

        def y_put(c, tmp, tok, b=b, par=par):
            toks = []
            @block.vector
            def _(v):
                y_sb = A.bf16(y_sb_off + 2 * (c * S + b * LNW), LNW)
                v.tensor_copy(out=y_sb, in_=tmp)
            if f32_dram is not None:
                @block.sync
                def _(q):
                    q.wait_ge(tok[0].h, tok[1])
                    q.dma_start(out=f32_dram[c * 128:(c + 1) * 128, b * LNW:(b + 1) * LNW],
                                in_=tmp).then_inc(cs.h, 16)
                    cs.n += 16
                toks.append((cs, cs.n))
            else:
                toks.append(tok)
            return toks

        state["tmp_busy"] = scr.setdefault("tmp_busy", {0: [], 1: []})
        st = _ln_block(kb, x_blk, scr, y_put, ready=[xloaded[-1]] + list(kb.const_tok) + list(ready0), state=state)
        state = st
        scr["tmp_busy"] = st["tmp_busy"]
        xfree[par] = [(dv, kb.sem("dv").n), (ac, kb.sem("ac").n)]
    scr["xfree"] = xfree
    kb.ln_ps_free = state["ps_free"]


def _mln_scr(kb):
    A = kb.A
    scr = ln_scr_alloc(A)
    scr["xblk"] = A.alloc(KC * LNW * 4)
    return scr


def _p_finish(kb, dbg):
    nc, block = kb.nc, kb.block
    cs = kb.sem("cs")

    @block.sync
    def _(q):
        for s in kb._sems.values():
            if s.n > 0 and s.h is not cs.h:
                q.wait_ge(s.h, s.n)
        for name, out_ap in kb.dbg_outs.items():
            t = kb.ds[name]
            q.dma_start(out=out_ap, in_=t.ap()).then_inc(cs.h, 16); cs.n += 16
        q.wait_ge(cs.h, cs.n)


def kernel(**inputs):
    xs = np.asarray(inputs["L0_attn_hidden_states"])
    shards = []
    for c in range(NCORES):
        xc = xs[c * EPC:(c + 1) * EPC]
        shards.append(np.ascontiguousarray(xc.transpose(0, 2, 1)))

    cpack_np = _cpack_host({k: np.asarray(v) for k, v in inputs.items()})
    pe_np = _host_pe_T()
    mask_np = _host_sparse_masks()

    nc = build_program()
    base = {"pe_T": pe_np, "cpack": cpack_np, "spmask": mask_np,
            "ident": np.eye(128, dtype=np.float32)}
    for k in list(inputs.keys()):
        if k == "L0_attn_hidden_states":
            continue
        base[k.replace("L0_", "")] = np.ascontiguousarray(np.asarray(inputs[k], dtype=np.float32))
    in_maps = [dict(base, x=shards[c]) for c in range(NCORES)]
    res = run_bass_kernel_spmd(nc, in_maps, core_ids=list(range(NCORES)))
    outs = []
    for c in range(NCORES):
        yc = res.results[c]["y"]
        outs.append(yc.transpose(0, 2, 1))
    return np.ascontiguousarray(np.concatenate(outs, axis=0))
